# version 01
# baseline (speedup 1.0000x reference)
import numpy as np
import ml_dtypes
import concourse.bass as bass
import concourse.mybir as mybir
import concourse.library_config as libcfg
from concourse.library_overlay import lower_extended_insts
from concourse.bass_utils import run_bass_kernel_spmd

F32 = mybir.dt.float32
BF16 = mybir.dt.bfloat16
I16 = mybir.dt.int16
U32 = mybir.dt.uint32
AF = mybir.ActivationFunctionType
ALU = mybir.AluOpType
AX = mybir.AxisListType.X

D, FF, E, DFF, CAP = 1024, 3072, 8, 4096, 5120
T, TL = 16384, 2048
DC, FC, FFC = 8, 24, 32
A = 2 * T
RG = [list(range(8))]
VAL_ITERS = 30
SNAP_ITERS = 8


def build(debug=(), stage=6):
    nc = bass.Bass("TRN2", target_bir_lowering=False)
    dram = nc.dram_tensor

    x_fm = dram("x_fm", [D, TL], F32, kind="ExternalInput").ap()
    x_tm = dram("x_tm", [TL, D], F32, kind="ExternalInput").ap()
    wg_t = dram("wg_t", [D, D], F32, kind="ExternalInput").ap()
    wv_t = dram("wv_t", [D, D], F32, kind="ExternalInput").ap()
    wd_t = dram("wd_t", [D, D], F32, kind="ExternalInput").ap()
    biases = dram("biases", [128, 3 * DC], F32, kind="ExternalInput").ap()
    rmsw = dram("rmsw", [128, 2 * DC], F32, kind="ExternalInput").ap()
    wgate_t = dram("wgate_t", [D, FF], F32, kind="ExternalInput").ap()
    wup_t = dram("wup_t", [D, FF], F32, kind="ExternalInput").ap()
    wout_t = dram("wout_t", [FF, D], F32, kind="ExternalInput").ap()
    router_t = dram("router_t", [D, E], F32, kind="ExternalInput").ap()
    pre_t = dram("pre_t", [D, DFF], F32, kind="ExternalInput").ap()
    gatew_t = dram("gatew_t", [D, DFF], F32, kind="ExternalInput").ap()
    post_t = dram("post_t", [DFF, D], F32, kind="ExternalInput").ap()
    c_ident = dram("c_ident", [128, 128], BF16, kind="ExternalInput").ap()
    c_identf = dram("c_identf", [128, 128], F32, kind="ExternalInput").ap()
    c_ones = dram("c_ones", [128, 1], F32, kind="ExternalInput").ap()
    c_ones1 = dram("c_ones1", [1, 128], F32, kind="ExternalInput").ap()
    c_iota8 = dram("c_iota8", [128, 8], F32, kind="ExternalInput").ap()
    c_tokp1 = dram("c_tokp1", [128, 256], F32, kind="ExternalInput").ap()
    c_iotasl = dram("c_iotasl", [16, 2048], F32, kind="ExternalInput").ap()
    c_myexp = dram("c_myexp", [128, 1], F32, kind="ExternalInput").ap()
    c_carrysel = dram("c_carrysel", [128, 8], F32, kind="ExternalInput").ap()
    c_triu = dram("c_triu", [128, 128], F32, kind="ExternalInput").ap()

    out_tm = dram("out_tm", [TL, D], F32, kind="ExternalOutput").ap()
    dbg_specs = {
        "d_xn": ([128, DC * TL], BF16), "d_x1": ([128, DC * TL], BF16),
        "d_thr": ([32, 16], F32), "d_lists": ([16, 2 * 512], F32),
        "d_wtsl": ([128, 40], F32), "d_kany": ([128, 256], F32),
        "d_moe": ([TL, D], F32), "d_l1a": ([17, 1280], F32),
    }
    dbg = {}
    for name in debug:
        shp, dtp = dbg_specs[name]
        dbg[name] = dram(name, shp, dtp, kind="ExternalOutput").ap()

    xnloc = dram("xnloc", [TL, D], BF16, kind="Internal").ap()
    xnall = dram("xnall", [T, D], BF16, kind="Internal").ap()
    lgloc = dram("lgloc", [TL, E], F32, kind="Internal").ap()
    lgall = dram("lgall", [T, E], F32, kind="Internal").ap()
    carrloc = dram("carrloc", [1, D], F32, kind="Internal").ap()
    carrall = dram("carrall", [8, D], F32, kind="Internal").ap()
    moeacc = dram("moeacc", [T + 1, D], F32, kind="Internal").ap()
    moers = dram("moers", [TL, D], F32, kind="Internal").ap()
    attnfm = dram("attnfm", [D, TL], BF16, kind="Internal").ap()
    bnc1 = dram("bnc1", [1, A], F32, kind="Internal").ap()
    bnc2 = dram("bnc2", [1, A], F32, kind="Internal").ap()
    bncw = dram("bncw", [1, CAP], F32, kind="Internal").ap()

    guards = []

    def sbuf_r(name, shape, dtp):
        return sbuf(name, shape, dtp, side="right")

    def sbuf(name, shape, dtp, side="left"):
        g = nc.sbuf_tensor(name, shape, dtp, side=side)
        t = g.__enter__()
        guards.append((name, g))
        return t

    def psum(name, shape, dtp=F32):
        g = nc.psum_tensor(name, shape, dtp)
        t = g.__enter__()
        guards.append((name, g))
        return t

    def free(*names):
        for nm in names:
            for i, (n, g) in enumerate(guards):
                if n == nm:
                    g.__exit__(None, None, None)
                    guards.pop(i)
                    break
            else:
                raise KeyError(nm)

    identt = sbuf_r("identt", [128, 128], BF16)
    identf = sbuf_r("identf", [128, 128], F32)
    onest = sbuf_r("onest", [128, 1], F32)
    ones1t = sbuf_r("ones1t", [1, 128], F32)
    iota8t = sbuf_r("iota8t", [128, 8], F32)
    tokp1t = sbuf_r("tokp1t", [128, 256], F32)
    myexpt = sbuf_r("myexpt", [128, 1], F32)
    carryselt = sbuf_r("carryselt", [128, 8], F32)
    triut = sbuf_r("triut", [128, 128], F32)
    biast = sbuf_r("biast", [128, 3 * DC], F32)
    rmswt = sbuf_r("rmswt", [128, 2 * DC], F32)
    zerocol = sbuf_r("zerocol", [128, 1], F32)
    EPSC = sbuf_r("EPSC", [128, 2], F32)
    rtb = sbuf_r("rtb", [128, DC * E], BF16)
    HFIN = sbuf_r("HFIN", [128, DC], F32)
    HALL = sbuf_r("HALL", [128, 8, DC], F32)
    H0 = sbuf_r("H0", [128, DC], F32)
    IG = sbuf_r("IG", [128, 320], I16)
    ISC = sbuf_r("ISC", [128, 320], I16)
    WTSL = sbuf_r("WTSL", [128, 40], F32)
    XN = sbuf_r("XN", [128, DC, TL], BF16)

    class Ctr:
        def __init__(self, h):
            self.h = h
            self.n = 0

        def inc(self, instr, k=1):
            instr.then_inc(self.h, k)
            self.n += k
            return instr

    def sem(name):
        g = nc.semaphore(name)
        h = g.__enter__()
        guards.append((name, g))
        return Ctr(h)

    def dma(ctr, eng, fn, *a, **k):
        eng.wait_ge(ctr.h, ctr.n)
        return ctr.inc(fn(*a, **k), 16)

    class SerEng:

        def __init__(self, eng, ctr):
            self._e = eng
            self._c = ctr

        def wait_ge(self, h, v_):
            self._e.wait_ge(h, v_)

        def __getattr__(self, name):
            f = getattr(self._e, name)

            def g(*a, **k):
                self._e.wait_ge(self._c.h, self._c.n)
                return self._c.inc(f(*a, **k))
            return g

    s_w = sem("s_w")
    s_gw = sem("s_gw")
    s_pe = sem("s_pe")
    s_ev = sem("s_ev")
    s_sc = sem("s_sc")
    s_gp = sem("s_gp")
    s_gd = sem("s_gd")
    s_cc = sem("s_cc")
    s_st = sem("s_st")

    blk_g = nc.Block()
    blk = blk_g.__enter__()

    def finish_partial():
        @blk.sync
        def _(q):
            q.wait_ge(s_pe.h, s_pe.n)
            q.wait_ge(s_ev.h, s_ev.n)
            q.wait_ge(s_sc.h, s_sc.n)
            q.wait_ge(s_gp.h, s_gp.n)
            q.wait_ge(s_gd.h, s_gd.n)
            q.wait_ge(s_gw.h, s_gw.n)
            q.wait_ge(s_cc.h, s_cc.n)
            q.wait_ge(s_st.h, s_st.n)
            for tb in range(16):
                s_st.inc(q.dma_start(out=out_tm[128 * tb:128 * (tb + 1), :],
                                     in_=x_tm[128 * tb:128 * (tb + 1), :]), 16)
            q.wait_ge(s_st.h, s_st.n)
        blk_g.__exit__(None, None, None)
        for _n, _g in reversed(guards):
            _g.__exit__(None, None, None)
        return nc


    XT = sbuf("XT", [128, DC, TL], F32)
    XNT = sbuf("XNT", [128, D], BF16)
    LGV = sbuf("LGV", [128, 16 * E], F32)
    zeroz = sbuf("zeroz", [128, 4096], F32)
    SR = sbuf("SR", [1, 2 * TL], F32)
    SRX = sbuf("SRX", [1, TL], F32)
    X2T = sbuf("X2T", [128, DC, TL], F32)
    rA = SR[:, 0:TL]
    rB = SR[:, TL:2 * TL]
    rX = SRX[:, :]

    psA = psum("psA", [1, 512])
    psB = psum("psB", [1, 512])
    psBC = psum("psBC", [128, 512])
    psT = psum("psT", [128, 128], BF16)
    psL = psum("psL", [128, E])

    @blk.sync
    def _(q):
        for d, s in [(identt, c_ident), (identf, c_identf), (onest, c_ones), (ones1t, c_ones1),
                     (iota8t, c_iota8), (tokp1t, c_tokp1), (myexpt, c_myexp),
                     (carryselt, c_carrysel), (triut, c_triu),
                     (biast, biases), (rmswt, rmsw)]:
            s_w.inc(q.dma_start(out=d[:], in_=s[:]), 16)
        s_w.inc(q.dma_start(out=XT[:],
                            in_=x_fm.rearrange("(k p) t -> p k t", p=128)), 16)
        s_w.inc(q.dma_start(
            out=X2T[:, 0, 0:DC * E].rearrange("p (k e) -> p k e", e=E),
            in_=router_t.rearrange("(k p) e -> p k e", p=128)), 16)
    w_all = s_w.n

    @blk.vector
    def _(v):
        v = SerEng(v, s_ev)
        v.memset(zerocol[:], 0.0)
        v.memset(EPSC[:, 0:1], 1e-5)
        v.memset(EPSC[:, 1:2], 1e-6)
        v.memset(zeroz[:], 0.0)
    ev_zero = s_ev.n

    @blk.scalar
    def _(q):
        q.wait_ge(s_ev.h, ev_zero)
        for r0 in range(0, T, 512):
            dma(s_st, q, q.dma_start, 
                out=moeacc[r0:r0 + 512, :].rearrange("(a b) d -> a (b d)", b=4),
                in_=zeroz[:, :])
        dma(s_st, q, q.dma_start, out=moeacc[T:T + 1, :], in_=zeroz[0:1, 0:D])
        q.wait_ge(s_w.h, w_all)
        s_sc.inc(q.copy(out=rtb[:], in_=X2T[:, 0, 0:DC * E]))
    st_zero = s_st.n
    sc_rtb = s_sc.n

    @blk.scalar
    def _(sc):
        sc = SerEng(sc, s_sc)
        sc.wait_ge(s_sc.h, sc_rtb)
        for k in range(DC):
            sc.activation(out=X2T[:, k, :], in_=XT[:, k, :],
                                   func=AF.Square)
    sc_sq = s_sc.n

    pe_stats0 = s_pe.n

    @blk.tensor
    def _(t):
        t.wait_ge(s_sc.h, sc_sq)
        for tt in range(4):
            sl = slice(512 * tt, 512 * (tt + 1))
            if tt > 0:
                t.wait_ge(s_ev.h, ev_zero + 2 * tt)
            for k in range(DC):
                t.matmul(psA[:, :], onest[:], XT[:, k, sl],
                         start=(k == 0), stop=(k == DC - 1))
            for k in range(DC):
                mm = t.matmul(psB[:, :], onest[:], X2T[:, k, sl],
                              start=(k == 0), stop=(k == DC - 1))
            s_pe.inc(mm)

    @blk.vector
    def _(v):
        v = SerEng(v, s_ev)
        for tt in range(4):
            v.wait_ge(s_pe.h, pe_stats0 + tt + 1)
            sl = slice(512 * tt, 512 * (tt + 1))
            v.tensor_scalar(out=rA[:, sl], in0=psA[:, :],
                                     scalar1=1.0 / D, scalar2=None, op0=ALU.mult)
            v.tensor_scalar(out=rB[:, sl], in0=psB[:, :],
                                     scalar1=1.0 / D, scalar2=None, op0=ALU.mult)
        v.tensor_tensor(out=rX, in0=rA, in1=rA, op=ALU.mult)
        v.tensor_sub(rB, rB, rX)
    ev_var = s_ev.n

    @blk.scalar
    def _(sc):
        sc = SerEng(sc, s_sc)
        sc.wait_ge(s_ev.h, ev_var)
        sc.activation(out=rX, in_=rB, func=AF.Sqrt, bias=EPSC[0:1, 0:1])
    sc_sd = s_sc.n

    @blk.vector
    def _(v):
        v = SerEng(v, s_ev)
        v.wait_ge(s_sc.h, sc_sd)
        v.reciprocal(out=rX, in_=rX)
        v.tensor_tensor(out=rB, in0=rB, in1=rX, op=ALU.mult)
        v.tensor_tensor(out=rB, in0=rB, in1=rX, op=ALU.mult)
    ev_mx2 = s_ev.n

    @blk.scalar
    def _(sc):
        sc = SerEng(sc, s_sc)
        sc.wait_ge(s_ev.h, ev_mx2)
        sc.activation(out=rB, in_=rB, func=AF.Sqrt, bias=EPSC[0:1, 1:2])
    sc_s2 = s_sc.n

    @blk.vector
    def _(v):
        v = SerEng(v, s_ev)
        v.wait_ge(s_sc.h, sc_s2)
        v.reciprocal(out=rB, in_=rB)
    ev_s2 = s_ev.n

    free("X2T")
    RBC = sbuf_r("RBC", [128, TL], F32)
    SBC = sbuf_r("SBC", [128, TL], F32)
    MBC = sbuf_r("MBC", [128, TL], F32)

    pe_bc0 = s_pe.n

    @blk.tensor
    def _(t):
        t.wait_ge(s_ev.h, ev_s2)
        for i in range(12):
            row = [rA, rX, rB][i // 4]
            tt = i % 4
            if i >= 1:
                t.wait_ge(s_ev.h, ev_s2 + i)
            s_pe.inc(t.matmul(psBC[:, :], ones1t[:],
                              row[:, 512 * tt:512 * (tt + 1)],
                              start=True, stop=True))

    @blk.vector
    def _(v):
        v = SerEng(v, s_ev)
        for i in range(12):
            dst = [MBC, RBC, SBC][i // 4]
            tt = i % 4
            v.wait_ge(s_pe.h, pe_bc0 + i + 1)
            if i < 4:
                v.tensor_scalar(out=dst[:, 512 * tt:512 * (tt + 1)],
                                         in0=psBC[:, :], scalar1=-1.0,
                                         scalar2=None, op0=ALU.mult)
            else:
                v.tensor_copy(out=dst[:, 512 * tt:512 * (tt + 1)],
                                       in_=psBC[:, :])
    ev_bc = s_ev.n

    @blk.vector
    def _(v):
        v = SerEng(v, s_ev)
        for k in range(DC):
            v.tensor_add(XN[:, k, :], XT[:, k, :], MBC[:, :])
            v.tensor_tensor(out=XN[:, k, :], in0=XN[:, k, :],
                                     in1=RBC[:, :], op=ALU.mult)
    ev_xn = s_ev.n

    pe_tb0 = s_pe.n
    ev_tb0 = s_ev.n

    @blk.tensor
    def _(t):
        t.wait_ge(s_ev.h, ev_xn)
        for tb in range(16):
            sl = slice(128 * tb, 128 * (tb + 1))
            if tb > 0:
                t.wait_ge(s_st.h, st_zero + 16 * tb)
            for k in range(DC):
                if tb > 0 or k >= 1:
                    t.wait_ge(s_ev.h, ev_tb0 + tb * (DC + 1) + k)
                s_pe.inc(t.transpose(psT[:, :], XN[:, k, sl], identt[:]))
            for k in range(DC):
                mm = t.matmul(psL[:, :], XN[:, k, sl], rtb[:, E * k:E * (k + 1)],
                              start=(k == 0), stop=(k == DC - 1))
            s_pe.inc(mm)

    @blk.vector
    def _(v):
        v = SerEng(v, s_ev)
        for tb in range(16):
            base = pe_tb0 + tb * (DC + 1)
            for k in range(DC):
                v.wait_ge(s_pe.h, base + k + 1)
                v.tensor_copy(out=XNT[:, 128 * k:128 * (k + 1)],
                                       in_=psT[:, :])
            v.wait_ge(s_pe.h, base + DC + 1)
            v.tensor_copy(out=LGV[:, E * tb:E * (tb + 1)], in_=psL[:, :])
    ev_tb_end = s_ev.n

    @blk.sync
    def _(q):
        for tb in range(16):
            q.wait_ge(s_ev.h, ev_tb0 + tb * (DC + 1) + DC)
            dma(s_st, q, q.dma_start, out=xnloc[128 * tb:128 * (tb + 1), :],
                                 in_=XNT[:, :])
        q.wait_ge(s_ev.h, ev_tb_end)
        dma(s_st, q, q.dma_start, out=lgloc.rearrange("(c p) e -> p c e", p=128),
                             in_=LGV[:, :].rearrange("p (c e) -> p c e", e=E))
        if "d_xn" in dbg:
            dma(s_st, q, q.dma_start, out=dbg["d_xn"],
                                 in_=XN.rearrange("p k t -> p (k t)"))
    st_xn = s_st.n

    @blk.gpsimd
    def _(g):
        g.wait_ge(s_st.h, st_xn)
        s_cc.inc(g.collective_compute("AllGather", ALU.bypass, replica_groups=RG,
                                      ins=[lgloc[:]], outs=[lgall[:]]))
        s_cc.inc(g.collective_compute("AllGather", ALU.bypass, replica_groups=RG,
                                      ins=[xnloc[:]], outs=[xnall[:]]))
    cc_lg, cc_xn = 1, 2

    if stage <= 1:
        return finish_partial()

    free("SRX", "SR", "zeroz", "LGV", "XNT", "XT")
    free("psL", "psT")
    free("MBC")
    XSA = sbuf("XSA", [128, DC, TL], BF16)
    TMPV = sbuf("TMPV", [128, 2, 512], F32)
    ATT = sbuf("ATT", [128, DC, TL], BF16)
    XN2 = sbuf("XN2", [128, DC, TL], BF16)
    WMX0 = sbuf("WMX0", [128, DC, D], BF16)
    WMX1 = sbuf("WMX1", [128, DC, D], BF16)
    psW0 = psum("psW0", [128, 512])
    psW1 = psum("psW1", [128, 512])

    @blk.vector
    def _(v):
        v = SerEng(v, s_ev)
        v.wait_ge(s_st.h, st_xn)
        v.wait_ge(s_pe.h, pe_tb0 + 16 * (DC + 1))
        v.wait_ge(s_sc.h, sc_sq)
        for k in range(DC):
            v.tensor_scalar(out=ATT[:, k, :], in0=XN[:, k, :],
                            scalar1=rmswt[:, k:k + 1], scalar2=None, op0=ALU.mult)
            v.tensor_tensor(out=XN2[:, k, :], in0=ATT[:, k, :],
                                     in1=SBC[:, :], op=ALU.mult)
    ev_xn2 = s_ev.n

    gw_mix0 = s_gw.n

    @blk.gpsimd
    def _(g):
        g.wait_ge(s_st.h, st_xn)
        g.wait_ge(s_pe.h, pe_tb0 + 16 * (DC + 1))
        g.wait_ge(s_sc.h, sc_sq)
        g.wait_ge(s_ev.h, ev_xn)
        dma(s_gw, g, g.dma_start, out=WMX0[:],
                             in_=wg_t.rearrange("(k p) o -> p k o", p=128))
        dma(s_gw, g, g.dma_start, out=WMX1[:],
                             in_=wv_t.rearrange("(k p) o -> p k o", p=128))

    pe_mix0 = s_pe.n
    sc_mix0 = s_sc.n
    ev_mix0 = s_ev.n

    @blk.tensor
    def _(t):
        t.wait_ge(s_ev.h, ev_xn2)
        for gi in range(96):
            wi, oc, tt = gi // 32, (gi // 4) % 8, gi % 4
            buf = [WMX0, WMX1, WMX0][wi]
            if gi % 32 == 0:
                t.wait_ge(s_gw.h, gw_mix0 + 16 * (wi + 1))
            if gi >= 2:
                t.wait_ge(s_sc.h, sc_mix0 + gi - 1)
            pw = psW0 if gi % 2 == 0 else psW1
            sl = slice(512 * tt, 512 * (tt + 1))
            for k in range(DC):
                mm = t.matmul(pw[:, :], buf[:, k, 128 * oc:128 * (oc + 1)],
                              XN2[:, k, sl], start=(k == 0), stop=(k == DC - 1))
            s_pe.inc(mm)
    pe_mix_end = s_pe.n

    @blk.gpsimd
    def _(g):
        g.wait_ge(s_pe.h, pe_mix0 + 32)
        dma(s_gw, g, g.dma_start, out=WMX0[:],
                             in_=wd_t.rearrange("(k p) o -> p k o", p=128))
    gw_mix_end = s_gw.n

    @blk.scalar
    def _(sc):
        sc = SerEng(sc, s_sc)
        for gi in range(96):
            wi, oc, tt = gi // 32, (gi // 4) % 8, gi % 4
            sc.wait_ge(s_pe.h, pe_mix0 + gi + 1)
            pw = psW0 if gi % 2 == 0 else psW1
            sl = slice(512 * tt, 512 * (tt + 1))
            bias = biast[:, wi * DC + oc:wi * DC + oc + 1]
            if wi == 0:
                sc.activation(out=XSA[:, oc, sl], in_=pw[:, :],
                                       func=AF.Sigmoid, bias=bias)
            else:
                if gi >= 34:
                    sc.wait_ge(s_ev.h, ev_mix0 + gi - 33)
                fn = AF.Tanh if wi == 1 else AF.Sigmoid
                sc.activation(out=TMPV[:, gi % 2, :], in_=pw[:, :],
                                       func=fn, bias=bias)
    sc_mix_end = s_sc.n

    @blk.vector
    def _(v):
        v = SerEng(v, s_ev)
        for gi in range(32, 96):
            wi, oc, tt = gi // 32, (gi // 4) % 8, gi % 4
            sl = slice(512 * tt, 512 * (tt + 1))
            v.wait_ge(s_sc.h, sc_mix0 + gi + 1)
            if wi == 1:
                v.tensor_tensor(out=XSA[:, oc, sl], in0=XSA[:, oc, sl],
                                         in1=TMPV[:, gi % 2, :], op=ALU.mult)
            else:
                v.tensor_scalar(out=ATT[:, oc, sl],
                                         in0=TMPV[:, gi % 2, :], scalar1=0.998,
                                         scalar2=0.001, op0=ALU.mult, op1=ALU.add)

    @blk.vector
    def _(v):
        v = SerEng(v, s_ev)
        for k in range(DC):
            v.tensor_tensor_scan(out=XSA[:, k, :], data0=ATT[:, k, :],
                                 data1=XSA[:, k, :], initial=0.0,
                                 op0=ALU.mult, op1=ALU.add)
            v.tensor_tensor_scan(out=ATT[:, k, :], data0=ATT[:, k, :],
                                 data1=zerocol[:].broadcast_to([128, TL]),
                                 initial=1.0, op0=ALU.mult, op1=ALU.add)
            v.tensor_copy(out=HFIN[:, k:k + 1], in_=XSA[:, k, TL - 1:TL])
        v.engine_nop()
    ev_scan = s_ev.n

    @blk.sync
    def _(q):
        q.wait_ge(s_ev.h, ev_scan)
        dma(s_st, q, q.dma_start, out=carrloc.rearrange("o (k p) -> p o k", p=128),
                             in_=HFIN[:, :].unsqueeze(1),
                             allow_slow_non_contiguous=True)
    st_carr = s_st.n

    @blk.gpsimd
    def _(g):
        g.wait_ge(s_st.h, st_carr)
        s_cc.inc(g.collective_compute("AllGather", ALU.bypass, replica_groups=RG,
                                      ins=[carrloc[:]], outs=[carrall[:]]))
    cc_carr = s_cc.n

    @blk.sync
    def _(q):
        q.wait_ge(s_cc.h, cc_carr)
        dma(s_st, q, q.dma_start, out=HALL[:],
                             in_=carrall.rearrange("r (k p) -> p r k", p=128),
                             allow_slow_non_contiguous=True)
    st_hall = s_st.n

    @blk.vector
    def _(v):
        v = SerEng(v, s_ev)
        v.wait_ge(s_st.h, st_hall)
        v.tensor_scalar(out=H0[:, :], in0=HALL[:, 0, :],
                        scalar1=carryselt[:, 0:1], scalar2=None, op0=ALU.mult)
        for r in range(1, 8):
            v.scalar_tensor_tensor(out=H0[:, :], in0=HALL[:, r, :],
                                   scalar=carryselt[:, r:r + 1], in1=H0[:, :],
                                   op0=ALU.mult, op1=ALU.add)
        for k in range(DC):
            v.scalar_tensor_tensor(out=XSA[:, k, :], in0=ATT[:, k, :],
                                   scalar=H0[:, k:k + 1], in1=XSA[:, k, :],
                                   op0=ALU.mult, op1=ALU.add)
            v.tensor_add(XN[:, k, :], XN[:, k, :], XSA[:, k, :])
    ev_x1 = s_ev.n

    def ev_x1k(k):
        return ev_x1 - 2 * DC + 2 * (k + 1)

    @blk.sync
    def _(q):
        if "d_x1" in dbg:
            q.wait_ge(s_ev.h, ev_x1)
            dma(s_st, q, q.dma_start, out=dbg["d_x1"],
                                 in_=XN.rearrange("p k t -> p (k t)"))

    if stage <= 2:
        return finish_partial()

    free("WMX1", "WMX0", "XN2", "ATT")
    free("SBC")
    SQ2 = sbuf("SQ2", [128, TL], F32)
    SR3 = sbuf("SR3", [1, TL], F32)
    r3 = SR3[:, :]
    MT = sbuf("MT", [128, FC, 1024], BF16)
    psX = psum("psX", [128, 512])
    psY5 = psum("psY5", [128, 512])
    WGF = sbuf("WGF", [128, DC, 6 * 128], BF16)
    WUF = sbuf("WUF", [128, DC, 6 * 128], BF16)
    WOF0 = sbuf("WOF0", [128, FC, 128], BF16)
    WOF1 = sbuf("WOF1", [128, FC, 128], BF16)
    ASTG = sbuf("ASTG", [128, 2, 512], BF16)

    pe_st2_0 = s_pe.n
    sc_st2_0 = s_sc.n

    @blk.scalar
    def _(sc):
        sc = SerEng(sc, s_sc)
        sc.wait_ge(s_ev.h, ev_x1)
        for k in range(DC):
            if k >= 1:
                sc.wait_ge(s_pe.h, pe_st2_0 + k)
            sc.activation(out=SQ2[:, :], in_=XN[:, k, :], func=AF.Square)

    @blk.tensor
    def _(t):
        for k in range(DC):
            t.wait_ge(s_sc.h, sc_st2_0 + k + 1)
            for tt in range(4):
                pw = [psA, psB, psW0, psW1][tt]
                mm = t.matmul(pw[0:1, 0:512], onest[:],
                              SQ2[:, 512 * tt:512 * (tt + 1)],
                              start=(k == 0), stop=(k == DC - 1))
            s_pe.inc(mm)
    pe_st2_end = s_pe.n

    @blk.vector
    def _(v):
        v = SerEng(v, s_ev)
        v.wait_ge(s_pe.h, pe_st2_end)
        for tt in range(4):
            pw = [psA, psB, psW0, psW1][tt]
            v.tensor_scalar(out=r3[:, 512 * tt:512 * (tt + 1)], in0=pw[0:1, 0:512],
                            scalar1=1.0 / D, scalar2=None, op0=ALU.mult)
        v.engine_nop()
    ev_st2 = s_ev.n

    @blk.scalar
    def _(sc):
        sc = SerEng(sc, s_sc)
        sc.wait_ge(s_ev.h, ev_st2)
        sc.activation(out=r3, in_=r3, func=AF.Sqrt, bias=EPSC[0:1, 1:2])
    sc_r3 = s_sc.n

    @blk.vector
    def _(v):
        v = SerEng(v, s_ev)
        v.wait_ge(s_sc.h, sc_r3)
        v.reciprocal(out=r3, in_=r3)
    ev_r3 = s_ev.n

    pe_r3b0 = s_pe.n

    @blk.tensor
    def _(t):
        t.wait_ge(s_ev.h, ev_r3)
        for tt in range(4):
            if tt >= 1:
                t.wait_ge(s_ev.h, ev_r3 + tt)
            s_pe.inc(t.matmul(psBC[:, :], ones1t[:],
                              r3[:, 512 * tt:512 * (tt + 1)],
                              start=True, stop=True))

    @blk.vector
    def _(v):
        v = SerEng(v, s_ev)
        for tt in range(4):
            v.wait_ge(s_pe.h, pe_r3b0 + tt + 1)
            v.tensor_copy(out=RBC[:, 512 * tt:512 * (tt + 1)],
                                   in_=psBC[:, :])
        for k in range(DC):
            v.tensor_scalar(out=SQ2[:, :], in0=XN[:, k, :],
                            scalar1=rmswt[:, DC + k:DC + k + 1], scalar2=None,
                            op0=ALU.mult)
            v.tensor_tensor(out=XSA[:, k, :], in0=SQ2[:, :],
                                     in1=RBC[:, :], op=ALU.mult)
    ev_xn3 = s_ev.n

    gwF = s_gw.n
    peF = s_pe.n
    scF = s_sc.n
    evF = s_ev.n
    stF = s_st.n

    def gw_ab(h, grp):
        return gwF + (16 * h + 2 * (grp + 1)) * 16

    def gw_wo(h, dsub):
        return gwF + (16 * h + 8 + dsub + 1) * 16

    def pe_a(h, gl):
        return peF + 64 * h + gl + 1

    def pe_b(h, bl):
        return peF + 64 * h + 48 + bl + 1

    def sc_a(h, gl):
        return scF + 48 * h + gl + 1

    def ev_a(h, gl):
        return evF + 64 * h + gl + 1

    def ev_b(h, bl):
        return evF + 64 * h + 48 + bl + 1

    @blk.gpsimd
    def _(g):
        g.wait_ge(s_pe.h, pe_mix_end)
        g.wait_ge(s_ev.h, ev_x1)
        for h in range(2):
            for grp in range(4):
                if h > 0 or grp >= 1:
                    g.wait_ge(s_pe.h, peF + 64 * h + 12 * grp if grp >= 1
                              else peF + 64 * (h - 1) + 48)
                csl = slice(6 * 128 * grp, 6 * 128 * (grp + 1))
                dma(s_gw, g, g.dma_start, 
                    out=WGF[:],
                    in_=wgate_t.rearrange("(k p) o -> p k o", p=128)[:, :, csl])
                dma(s_gw, g, g.dma_start, 
                    out=WUF[:],
                    in_=wup_t.rearrange("(k p) o -> p k o", p=128)[:, :, csl])
            for dsub in range(DC):
                wo = WOF0 if dsub % 2 == 0 else WOF1
                if h > 0 or dsub >= 2:
                    g.wait_ge(s_pe.h, pe_b(h, 2 * (dsub - 2) + 2) if dsub >= 2
                              else peF + 64 * h)
                dsl = slice(128 * dsub, 128 * (dsub + 1))
                dma(s_gw, g, g.dma_start, 
                    out=wo[:],
                    in_=wout_t.rearrange("(c p) d -> p c d", p=128)[:, :, dsl])
    gw_ffn_end = s_gw.n

    @blk.tensor
    def _(t):
        t.wait_ge(s_ev.h, ev_xn3)
        for h in range(2):
            gl = 0
            for grp in range(4):
                t.wait_ge(s_gw.h, gw_ab(h, grp))
                for ttl in range(2):
                    tt = 2 * h + ttl
                    sl = slice(512 * tt, 512 * (tt + 1))
                    for j in range(6):
                        if h > 0 or gl >= 2:
                            t.wait_ge(s_sc.h, sc_a(h, gl - 2) if gl >= 2
                                      else sc_a(h - 1, 46 + gl))
                            t.wait_ge(s_ev.h, ev_a(h, gl - 2) if gl >= 2
                                      else ev_b(h - 1, 15))
                        gbank = psW0 if gl % 2 == 0 else psX
                        ubank = psW1 if gl % 2 == 0 else psY5
                        for k in range(DC):
                            t.matmul(gbank[:, :], WGF[:, k, 128 * j:128 * (j + 1)],
                                     XSA[:, k, sl], start=(k == 0),
                                     stop=(k == DC - 1))
                        for k in range(DC):
                            mm = t.matmul(ubank[:, :],
                                          WUF[:, k, 128 * j:128 * (j + 1)],
                                          XSA[:, k, sl], start=(k == 0),
                                          stop=(k == DC - 1))
                        s_pe.inc(mm)
                        gl += 1
            bl = 0
            for dsub in range(DC):
                wo = WOF0 if dsub % 2 == 0 else WOF1
                t.wait_ge(s_gw.h, gw_wo(h, dsub))
                for ttl in range(2):
                    if bl == 0:
                        t.wait_ge(s_sc.h, sc_a(h, 47))
                        t.wait_ge(s_ev.h, ev_a(h, 47))
                    elif bl >= 2:
                        t.wait_ge(s_ev.h, ev_b(h, bl - 2))
                    pw = psW0 if bl % 2 == 0 else psW1
                    for oc in range(FC):
                        mm = t.matmul(pw[:, :], wo[:, oc, :],
                                      MT[:, oc, 512 * ttl:512 * (ttl + 1)],
                                      start=(oc == 0), stop=(oc == FC - 1))
                    s_pe.inc(mm)
                    bl += 1
    pe_ffn_end = s_pe.n

    @blk.scalar
    def _(sc):
        sc = SerEng(sc, s_sc)
        for h in range(2):
            for gl in range(48):
                sc.wait_ge(s_pe.h, pe_a(h, gl))
                if h > 0 or gl >= 2:
                    sc.wait_ge(s_ev.h, ev_a(h, gl - 2) if gl >= 2
                               else ev_a(h - 1, 46 + gl))
                gbank = psW0 if gl % 2 == 0 else psX
                sc.activation(out=TMPV[:, gl % 2, :], in_=gbank[:, :],
                              func=AF.Silu)
    sc_ffn_end = s_sc.n

    @blk.vector
    def _(v):
        v = SerEng(v, s_ev)
        for h in range(2):
            for gl in range(48):
                grp, ttl, j = gl // 12, (gl // 6) % 2, gl % 6
                oc = grp * 6 + j
                v.wait_ge(s_sc.h, sc_a(h, gl))
                ubank = psW1 if gl % 2 == 0 else psY5
                v.tensor_tensor(out=MT[:, oc, 512 * ttl:512 * (ttl + 1)],
                                in0=TMPV[:, gl % 2, :], in1=ubank[:, :],
                                op=ALU.mult)
            for bl in range(16):
                dsub, ttl = bl // 2, bl % 2
                tt = 2 * h + ttl
                v.wait_ge(s_pe.h, pe_b(h, bl))
                nbi = 16 * h + bl
                if nbi >= 2:
                    v.wait_ge(s_st.h, stF + 16 * (nbi - 1))
                pw = psW0 if bl % 2 == 0 else psW1
                v.tensor_add(ASTG[:, bl % 2, :], pw[:, :],
                                      XN[:, dsub, 512 * tt:512 * (tt + 1)])
    ev_ffn_end = s_ev.n

    @blk.sync
    def _(q):
        for h in range(2):
            for bl in range(16):
                dsub, ttl = bl // 2, bl % 2
                tt = 2 * h + ttl
                q.wait_ge(s_ev.h, ev_b(h, bl))
                dma(s_st, q, q.dma_start, 
                    out=attnfm[128 * dsub:128 * (dsub + 1),
                               512 * tt:512 * (tt + 1)],
                    in_=ASTG[:, bl % 2, :])
    st_attn = s_st.n

    if stage <= 3:
        return finish_partial()

    free("ASTG", "WOF1", "WOF0", "WUF", "WGF", "MT", "SR3", "SQ2", "TMPV",
         "XSA")
    free("RBC")
    LGA = sbuf("LGA", [128, 128 * E], F32)
    TK = sbuf("TK", [128, 6 * 128], F32)
    SC1 = sbuf("SC1", [128, 32 * E], F32)
    SC2 = sbuf("SC2", [128, 32 * E], F32)
    WPLN = sbuf("WPLN", [128, 256], F32)
    EPL = sbuf("EPL", [128, 256], F32)
    WPL = sbuf("WPL", [128, E, 256], F32)
    EQF = sbuf("EQF", [128, E, 256], F32)
    KEEPA = sbuf("KEEPA", [128, E, 256], F32)
    JK = sbuf("JK", [128, 256], F32)
    JK2 = sbuf("JK2", [128, 256], F32)
    CNTS = sbuf("CNTS", [128, 32], F32)
    CT32 = sbuf("CT32", [32, 128], F32)
    M32 = sbuf("M32", [32, 128], F32)
    BCT = sbuf("BCT", [128, 32], F32)
    RST = sbuf("RST", [32, 16], F32)
    TOTS = sbuf("TOTS", [128, E], F32)
    CARR = sbuf("CARR", [128, E], F32)
    KANY = sbuf("KANY", [128, 256], F32)
    PWp = sbuf("PWp", [128, 256], F32)
    PTp = sbuf("PTp", [128, 256], F32)
    WSUM = sbuf("WSUM", [128, 128], F32)
    SRCA = sbuf("SRCA", [16, 2048], F32)
    CMP1 = sbuf("CMP1", [16, 320], F32)
    CMP2 = sbuf("CMP2", [16, 320], F32)
    L1A = sbuf("L1A", [16, 1280], F32)
    L2A = sbuf("L2A", [16, 640], F32)
    L1B = sbuf("L1B", [16, 1280], F32)
    L2B = sbuf("L2B", [16, 640], F32)
    NF1 = sbuf("NF1", [1, 16], U32)
    NFF = sbuf("NFF", [1, 16], F32)
    NFB16 = sbuf("NFB16", [16, 16], F32)
    MK = sbuf("MK", [16, 320], F32)
    iotaslt = sbuf("iotaslt", [16, 320], F32)

    lo, hi, mid, wst = (RST[:, i:i + 1] for i in range(4))
    done, tot, cgt, ge8 = (RST[:, i:i + 1] for i in range(4, 8))
    m8, c28, ngt, quo = (RST[:, i:i + 1] for i in range(8, 12))
    sc8a, sc8b, sc8c, sc8d = (RST[:, i:i + 1] for i in range(12, 16))

    def transp(v, dst, src):
        for b in range(4):
            v.transpose(dst[32 * b:32 * (b + 1), 0:32],
                        src[0:32, 32 * b:32 * (b + 1)])

    def transp_c(v, dst, src):
        for b in range(4):
            v.transpose(dst[0:32, 32 * b:32 * (b + 1)],
                        src[32 * b:32 * (b + 1), 0:32])

    def colsum(v, dstcol, op=ALU.add):
        transp_c(v, CT32, CNTS)
        v.tensor_reduce(out=dstcol, in_=CT32[:, :], axis=AX, op=op)

    def bcast8(v, dstT, srccol):
        v.tensor_copy(out=M32[:, :], in_=srccol.broadcast_to([32, 128]))
        transp(v, dstT, M32)

    @blk.sync
    def _(q):
        q.wait_ge(s_cc.h, cc_lg)
        q.wait_ge(s_pe.h, pe_ffn_end)
        q.wait_ge(s_ev.h, ev_ffn_end)
        dma(s_st, q, q.dma_start, out=LGA[:],
                             in_=lgall.rearrange("(p j) e -> p (j e)", p=128))
        dma(s_st, q, q.dma_start, out=iotaslt[:], in_=c_iotasl[:, 0:320])
    st_lga = s_st.n

    m1 = TK[:, 0:128]
    m2 = TK[:, 128:256]
    e1 = TK[:, 256:384]
    e2 = TK[:, 384:512]
    w1 = TK[:, 512:640]
    w2 = TK[:, 640:768]

    @blk.vector
    def _(v):
        v = SerEng(v, s_ev)
        v.wait_ge(s_st.h, st_lga)
        for qd in range(4):
            jsl = slice(32 * qd, 32 * (qd + 1))
            LV = LGA.rearrange("p (j e) -> p j e", e=E)[:, jsl, :]
            A3 = SC1.rearrange("p (j e) -> p j e", e=E)
            B3 = SC2.rearrange("p (j e) -> p j e", e=E)
            io8 = iota8t[:].unsqueeze(1).broadcast_to([128, 32, E])
            v.tensor_reduce(out=m1[:, jsl], in_=LV, axis=AX, op=ALU.max)
            v.tensor_tensor(out=A3, in0=LV,
                            in1=m1[:, jsl].unsqueeze(2).broadcast_to([128, 32, E]),
                            op=ALU.is_equal)
            v.tensor_tensor(out=B3, in0=A3, in1=io8, op=ALU.mult)
            v.scalar_tensor_tensor(out=B3, in0=A3, scalar=-9.0, in1=B3,
                                   op0=ALU.mult, op1=ALU.add)
            v.tensor_scalar(out=B3, in0=B3, scalar1=9.0, scalar2=None, op0=ALU.add)
            v.tensor_reduce(out=e1[:, jsl], in_=B3, axis=AX, op=ALU.min)
            v.tensor_tensor(out=A3, in0=io8,
                            in1=e1[:, jsl].unsqueeze(2).broadcast_to([128, 32, E]),
                            op=ALU.is_equal)
            v.scalar_tensor_tensor(out=A3, in0=A3, scalar=-1e30, in1=LV,
                                   op0=ALU.mult, op1=ALU.add)
            v.tensor_reduce(out=m2[:, jsl], in_=A3, axis=AX, op=ALU.max)
            v.tensor_tensor(out=B3, in0=A3,
                            in1=m2[:, jsl].unsqueeze(2).broadcast_to([128, 32, E]),
                            op=ALU.is_equal)
            v.tensor_tensor(out=A3, in0=B3, in1=io8, op=ALU.mult)
            v.scalar_tensor_tensor(out=A3, in0=B3, scalar=-9.0, in1=A3,
                                   op0=ALU.mult, op1=ALU.add)
            v.tensor_scalar(out=A3, in0=A3, scalar1=9.0, scalar2=None, op0=ALU.add)
            v.tensor_reduce(out=e2[:, jsl], in_=A3, axis=AX, op=ALU.min)
        v.tensor_sub(w1[:, :], m1[:, :], m2[:, :])
    ev_top2a = s_ev.n

    @blk.scalar
    def _(sc):
        sc = SerEng(sc, s_sc)
        sc.wait_ge(s_ev.h, ev_top2a)
        sc.activation(out=w1[:, :], in_=w1[:, :], func=AF.Sigmoid)
    sc_w1 = s_sc.n

    if stage <= 3.1:
        return finish_partial()

    @blk.vector
    def _(v):
        v = SerEng(v, s_ev)
        v.wait_ge(s_sc.h, sc_w1)
        v.memset(CNTS[:, :], 0.0)
        v.memset(RST[:, :], 0.0)
        v.tensor_scalar(out=w2[:, :], in0=w1[:, :], scalar1=-1.0, scalar2=1.0,
                        op0=ALU.mult, op1=ALU.add)
        v.tensor_copy(out=WPLN[:, 0:256:2], in_=w1[:, :])
        v.tensor_copy(out=WPLN[:, 1:256:2], in_=w2[:, :])
        v.tensor_copy(out=EPL[:, 0:256:2], in_=e1[:, :])
        v.tensor_copy(out=EPL[:, 1:256:2], in_=e2[:, :])
        v.tensor_scalar(out=JK[:, :], in0=WPLN[:, :], scalar1=2.0, scalar2=None,
                        op0=ALU.add)
        for e in range(E):
            v.tensor_scalar(out=JK2[:, :], in0=EPL[:, :], scalar1=float(e),
                            scalar2=None, op0=ALU.is_equal)
            v.tensor_tensor(out=WPL[:, e, :], in0=JK[:, :], in1=JK2[:, :],
                            op=ALU.mult)
            v.tensor_scalar(out=WPL[:, e, :], in0=WPL[:, e, :], scalar1=-2.0,
                            scalar2=None, op0=ALU.add)
            v.tensor_scalar(out=JK2[:, :], in0=WPL[:, e, :], scalar1=-0.5,
                            scalar2=None, op0=ALU.is_gt, op1=ALU.add,
                            accum_out=CNTS[:, e:e + 1])
        colsum(v, tot)
        v.memset(lo, -1.0)
        v.memset(hi, 1.0)
        v.memset(wst, -1.0)
        v.tensor_scalar(out=done, in0=tot, scalar1=float(CAP) + 0.5,
                        scalar2=None, op0=ALU.is_lt)

        for it in range(VAL_ITERS):
            v.tensor_add(sc8a, lo, hi)
            v.tensor_scalar(out=mid, in0=sc8a, scalar1=0.5, scalar2=None,
                            op0=ALU.mult)
            bcast8(v, BCT, mid)
            for e in range(E):
                v.tensor_scalar(out=JK[:, :], in0=WPL[:, e, :],
                                scalar1=BCT[:, e:e + 1], scalar2=None,
                                op0=ALU.is_gt, op1=ALU.add,
                                accum_out=CNTS[:, e:e + 1])
            colsum(v, cgt)
            v.tensor_scalar(out=ge8, in0=cgt, scalar1=float(CAP) - 0.5,
                            scalar2=None, op0=ALU.is_gt)
            v.tensor_sub(sc8a, mid, lo)
            v.tensor_tensor(out=sc8a, in0=sc8a, in1=ge8, op=ALU.mult)
            v.tensor_add(lo, lo, sc8a)
            v.tensor_scalar(out=sc8b, in0=ge8, scalar1=-1.0, scalar2=1.0,
                            op0=ALU.mult, op1=ALU.add)
            v.tensor_sub(sc8a, mid, hi)
            v.tensor_tensor(out=sc8a, in0=sc8a, in1=sc8b, op=ALU.mult)
            v.tensor_add(hi, hi, sc8a)

        for it in range(SNAP_ITERS):
            bcast8(v, BCT, lo)
            for e in range(E):
                v.tensor_scalar(out=JK[:, :], in0=WPL[:, e, :],
                                scalar1=BCT[:, e:e + 1], scalar2=None,
                                op0=ALU.is_gt)
                v.tensor_scalar(out=JK2[:, :], in0=WPL[:, e, :], scalar1=-2.0,
                                scalar2=None, op0=ALU.add)
                v.tensor_tensor(out=JK2[:, :], in0=JK2[:, :], in1=JK[:, :],
                                op=ALU.mult)
                v.tensor_scalar(out=JK2[:, :], in0=JK2[:, :], scalar1=2.0,
                                scalar2=None, op0=ALU.add, op1=ALU.min,
                                accum_out=CNTS[:, e:e + 1])
            colsum(v, m8, op=ALU.min)
            bcast8(v, BCT, m8)
            for e in range(E):
                v.tensor_scalar(out=JK[:, :], in0=WPL[:, e, :],
                                scalar1=BCT[:, e:e + 1], scalar2=None,
                                op0=ALU.is_gt, op1=ALU.add,
                                accum_out=CNTS[:, e:e + 1])
            colsum(v, c28)
            v.tensor_scalar(out=ge8, in0=c28, scalar1=float(CAP) - 0.5,
                            scalar2=None, op0=ALU.is_gt)
            v.tensor_scalar(out=sc8b, in0=done, scalar1=-1.0, scalar2=1.0,
                            op0=ALU.mult, op1=ALU.add)
            v.tensor_scalar(out=sc8c, in0=ge8, scalar1=-1.0, scalar2=1.0,
                            op0=ALU.mult, op1=ALU.add)
            v.tensor_tensor(out=sc8c, in0=sc8c, in1=sc8b, op=ALU.mult)
            v.tensor_sub(sc8a, m8, wst)
            v.tensor_tensor(out=sc8a, in0=sc8a, in1=sc8c, op=ALU.mult)
            v.tensor_add(wst, wst, sc8a)
            v.tensor_add(done, done, sc8c)
            v.tensor_scalar(out=sc8b, in0=done, scalar1=-1.0, scalar2=1.0,
                            op0=ALU.mult, op1=ALU.add)
            v.tensor_tensor(out=sc8c, in0=ge8, in1=sc8b, op=ALU.mult)
            v.tensor_sub(sc8a, m8, lo)
            v.tensor_tensor(out=sc8a, in0=sc8a, in1=sc8c, op=ALU.mult)
            v.tensor_add(lo, lo, sc8a)

        v.tensor_scalar(out=sc8b, in0=done, scalar1=-1.0, scalar2=1.0,
                        op0=ALU.mult, op1=ALU.add)
        v.tensor_sub(sc8a, lo, wst)
        v.tensor_tensor(out=sc8a, in0=sc8a, in1=sc8b, op=ALU.mult)
        v.tensor_add(wst, wst, sc8a)

        bcast8(v, BCT, wst)
        for e in range(E):
            v.tensor_scalar(out=JK[:, :], in0=WPL[:, e, :],
                            scalar1=BCT[:, e:e + 1], scalar2=None,
                            op0=ALU.is_gt, op1=ALU.add,
                            accum_out=CNTS[:, e:e + 1])
            v.tensor_scalar(out=EQF[:, e, :], in0=WPL[:, e, :],
                            scalar1=BCT[:, e:e + 1], scalar2=None,
                            op0=ALU.is_equal)
        colsum(v, ngt)
        v.tensor_scalar(out=quo, in0=ngt, scalar1=-1.0, scalar2=float(CAP),
                        op0=ALU.mult, op1=ALU.add)
        for e in range(E):
            v.tensor_tensor_scan(out=KEEPA[:, e, :], data0=EQF[:, e, :],
                                 data1=zerocol[:].broadcast_to([128, 256]),
                                 initial=0.0, op0=ALU.add, op1=ALU.add)
            v.tensor_copy(out=TOTS[:, e:e + 1], in_=KEEPA[:, e, 255:256])
        v.engine_nop()
    ev_rt_pre = s_ev.n

    if stage <= 3.2:
        return finish_partial()

    @blk.tensor
    def _(t):
        t.wait_ge(s_ev.h, ev_rt_pre)
        s_pe.inc(t.matmul(psBC[:, 0:E], triut[:], TOTS[:, :],
                          start=True, stop=True))
    pe_carr = s_pe.n

    @blk.vector
    def _(v):
        v = SerEng(v, s_ev)
        v.wait_ge(s_pe.h, pe_carr)
        v.tensor_copy(out=CARR[:, :], in_=psBC[:, 0:E])
        bcast8(v, BCT, quo)
        for e in range(E):
            v.tensor_sub(JK[:, :], KEEPA[:, e, :], EQF[:, e, :])
            v.tensor_scalar(out=JK[:, :], in0=JK[:, :],
                            scalar1=CARR[:, e:e + 1], scalar2=None, op0=ALU.add)
            v.tensor_scalar(out=JK[:, :], in0=JK[:, :],
                            scalar1=BCT[:, e:e + 1], scalar2=None, op0=ALU.is_lt)
            v.tensor_tensor(out=KEEPA[:, e, :], in0=JK[:, :], in1=EQF[:, e, :],
                            op=ALU.mult)
        bcast8(v, BCT, wst)
        for e in range(E):
            v.tensor_scalar(out=JK[:, :], in0=WPL[:, e, :],
                            scalar1=BCT[:, e:e + 1], scalar2=None, op0=ALU.is_gt)
            v.tensor_add(KEEPA[:, e, :], KEEPA[:, e, :], JK[:, :])
        v.tensor_reduce(out=KANY[:, :],
                        in_=KEEPA.rearrange("p e j -> p j e"), axis=AX, op=ALU.add)
        v.tensor_tensor(out=PWp[:, :], in0=WPLN[:, :], in1=KANY[:, :], op=ALU.mult)
        v.tensor_add(WSUM[:, :], PWp[:, 0:256:2], PWp[:, 1:256:2])
        v.tensor_scalar(out=WSUM[:, :], in0=WSUM[:, :], scalar1=1.1920929e-07,
                        scalar2=None, op0=ALU.max)
        v.reciprocal(out=WSUM[:, :], in_=WSUM[:, :])
        v.tensor_tensor(out=PWp[:, 0:256:2], in0=PWp[:, 0:256:2], in1=WSUM[:, :],
                        op=ALU.mult)
        v.tensor_tensor(out=PWp[:, 1:256:2], in0=PWp[:, 1:256:2], in1=WSUM[:, :],
                        op=ALU.mult)
        v.tensor_scalar(out=JK[:, :], in0=EPL[:, :], scalar1=myexpt[:, 0:1],
                        scalar2=None, op0=ALU.is_equal)
        v.tensor_tensor(out=JK[:, :], in0=JK[:, :], in1=KANY[:, :], op=ALU.mult)
        v.tensor_tensor_scan(out=JK2[:, :], data0=JK[:, :],
                             data1=zerocol[:].broadcast_to([128, 256]),
                             initial=0.0, op0=ALU.add, op1=ALU.add)
        v.tensor_copy(out=TOTS[:, 0:1], in_=JK2[:, 255:256])
        v.engine_nop()
    ev_rt2 = s_ev.n

    @blk.tensor
    def _(t):
        t.wait_ge(s_ev.h, ev_rt2)
        s_pe.inc(t.matmul(psBC[:, 0:1], triut[:], TOTS[:, 0:1],
                          start=True, stop=True))
    pe_carr2 = s_pe.n

    @blk.vector
    def _(v):
        v = SerEng(v, s_ev)
        v.wait_ge(s_pe.h, pe_carr2)
        v.tensor_copy(out=CARR[:, 0:1], in_=psBC[:, 0:1])
        v.tensor_sub(JK2[:, :], JK2[:, :], JK[:, :])
        v.tensor_scalar(out=JK2[:, :], in0=JK2[:, :], scalar1=CARR[:, 0:1],
                        scalar2=None, op0=ALU.add)
        v.tensor_scalar(out=JK2[:, :], in0=JK2[:, :], scalar1=float(CAP) - 0.5,
                        scalar2=None, op0=ALU.is_lt)
        v.tensor_tensor(out=JK[:, :], in0=JK[:, :], in1=JK2[:, :], op=ALU.mult)
        v.tensor_tensor(out=PTp[:, :], in0=tokp1t[:, :], in1=JK[:, :], op=ALU.mult)
        v.tensor_scalar(out=PTp[:, :], in0=PTp[:, :], scalar1=-1.0, scalar2=None,
                        op0=ALU.add)
        v.tensor_scalar(out=PWp[:, :], in0=PWp[:, :], scalar1=1.0, scalar2=None,
                        op0=ALU.add)
        v.tensor_tensor(out=PWp[:, :], in0=PWp[:, :], in1=JK[:, :], op=ALU.mult)
        v.tensor_scalar(out=PWp[:, :], in0=PWp[:, :], scalar1=-1.0, scalar2=None,
                        op0=ALU.add)
        v.engine_nop()
    ev_planes = s_ev.n

    @blk.sync
    def _(q):
        q.wait_ge(s_ev.h, ev_planes)
        if "d_kany" in dbg:
            dma(s_st, q, q.dma_start, out=dbg["d_kany"], in_=KANY[:, :])
        if "d_thr" in dbg:
            dma(s_st, q, q.dma_start, out=dbg["d_thr"], in_=RST[:, :])
    st_pln = s_st.n

    if stage <= 3.3:
        return finish_partial()

    @blk.vector
    def _(v):
        v = SerEng(v, s_ev)
        v.memset(L1A[:, :], -1.0)
        v.memset(L2A[:, :], -1.0)
        v.memset(CMP1[:, :], -1.0)
        v.memset(L1B[:, :], -1.0)
        v.memset(L2B[:, :], -1.0)
        v.memset(CMP2[:, :], -1.0)
    ev_cmpset = s_ev.n

    gp0 = s_gp.n
    pe_tr0 = s_pe.n
    ev_tr0 = s_ev.n

    @blk.tensor
    def _(t):
        t.wait_ge(s_ev.h, ev_planes)
        for hi in range(16):
            if hi >= 1:
                t.wait_ge(s_ev.h, ev_tr0 + hi)
            ps_sl = psBC[0:16, 128 * (hi % 2):128 * (hi % 2) + 128]
            s_pe.inc(t.matmul(ps_sl, PTp[:, 16 * hi:16 * hi + 16],
                              identf[:], start=True, stop=True))

    @blk.vector
    def _(v):
        v = SerEng(v, s_ev)
        for hi in range(16):
            v.wait_ge(s_pe.h, pe_tr0 + hi + 1)
            ps_sl = psBC[0:16, 128 * (hi % 2):128 * (hi % 2) + 128]
            v.tensor_copy(
                out=SRCA[0:16, :].rearrange("r (p hi2) -> r p hi2", hi2=16)[:, :, hi],
                in_=ps_sl)
    assert s_ev.n == ev_tr0 + 16

    if stage <= 3.31:
        return finish_partial()

    @blk.gpsimd
    def _(g):
        g.wait_ge(s_ev.h, ev_tr0 + 16)
        g.wait_ge(s_ev.h, ev_cmpset)
        for c in range(4):
            g.load_library(libcfg.sparse_gather)
            s_gp.inc(g.sparse_gather(out=L1A[:, 320 * c:320 * (c + 1)],
                                     in_=SRCA[:, 512 * c:512 * (c + 1)],
                                     num_found=NF1[:, c:c + 1]))

    if stage <= 3.32:
        return finish_partial()

    def nf_group(gpw, k0, k1, tiles):
        @blk.vector
        def _(v):
            v = SerEng(v, s_ev)
            v.wait_ge(s_gp.h, gpw)
            v.tensor_copy(out=NFF[:, k0:k1], in_=NF1[:, k0:k1])
        ev_c = s_ev.n

        @blk.tensor
        def _(t):
            t.wait_ge(s_ev.h, ev_c)
            s_pe.inc(t.matmul(psBC[0:16, 496:496 + (k1 - k0)],
                              ones1t[0:1, 0:16], NFF[:, k0:k1],
                              start=True, stop=True))
        pe_c = s_pe.n

        @blk.vector
        def _(v):
            v = SerEng(v, s_ev)
            v.wait_ge(s_pe.h, pe_c)
            v.tensor_copy(out=NFB16[:, k0:k1], in_=psBC[0:16, 496:496 + (k1 - k0)])
            for j, tile in enumerate(tiles):
                v.tensor_scalar(out=MK[:, :], in0=iotaslt[:, :],
                                scalar1=NFB16[:, k0 + j:k0 + j + 1],
                                scalar2=None, op0=ALU.is_lt)
                v.scalar_tensor_tensor(out=tile, in0=tile, scalar=1.0,
                                       in1=MK[:, :], op0=ALU.add, op1=ALU.mult)
                v.tensor_scalar(out=tile, in0=tile, scalar1=-1.0, scalar2=None,
                                op0=ALU.add)
        return s_ev.n

    ev_sanA1 = nf_group(gp0 + 4, 0, 4,
                        [L1A[:, 320 * c:320 * (c + 1)] for c in range(4)])

    if "d_l1a" in dbg:
        @blk.sync
        def _(q):
            q.wait_ge(s_ev.h, ev_sanA1)
            dma(s_st, q, q.dma_start, out=dbg["d_l1a"][0:16, :], in_=L1A[:, :])
            dma(s_st, q, q.dma_start, out=dbg["d_l1a"][16:17, 0:16], in_=NFF[:, :])

    if stage <= 3.35:
        return finish_partial()

    pe_tr1 = s_pe.n
    ev_tr1 = s_ev.n

    @blk.tensor
    def _(t):
        t.wait_ge(s_gp.h, gp0 + 4)
        t.wait_ge(s_ev.h, ev_sanA1)
        for hi in range(16):
            if hi >= 1:
                t.wait_ge(s_ev.h, ev_tr1 + hi)
            ps_sl = psBC[0:16, 128 * (hi % 2):128 * (hi % 2) + 128]
            s_pe.inc(t.matmul(ps_sl, PWp[:, 16 * hi:16 * hi + 16],
                              identf[:], start=True, stop=True))

    @blk.vector
    def _(v):
        v = SerEng(v, s_ev)
        v.wait_ge(s_gp.h, gp0 + 4)
        for hi in range(16):
            v.wait_ge(s_pe.h, pe_tr1 + hi + 1)
            ps_sl = psBC[0:16, 128 * (hi % 2):128 * (hi % 2) + 128]
            v.tensor_copy(
                out=SRCA[0:16, :].rearrange("r (p hi2) -> r p hi2", hi2=16)[:, :, hi],
                in_=ps_sl)
    ev_tr1_end = s_ev.n

    @blk.gpsimd
    def _(g):
        g.wait_ge(s_ev.h, ev_sanA1)
        g.load_library(libcfg.sparse_gather)
        s_gp.inc(g.sparse_gather(out=L2A[:, 0:320], in_=L1A[:, 0:640],
                                 num_found=NF1[:, 4:5]))
        g.load_library(libcfg.sparse_gather)
        s_gp.inc(g.sparse_gather(out=L2A[:, 320:640], in_=L1A[:, 640:1280],
                                 num_found=NF1[:, 5:6]))

    ev_sanA2 = nf_group(gp0 + 6, 4, 6, [L2A[:, 0:320], L2A[:, 320:640]])

    @blk.gpsimd
    def _(g):
        g.wait_ge(s_ev.h, ev_sanA2)
        g.load_library(libcfg.sparse_gather)
        s_gp.inc(g.sparse_gather(out=CMP1[:, :], in_=L2A[:, :],
                                 num_found=NF1[:, 6:7]))

    ev_sanA3 = nf_group(gp0 + 7, 6, 7, [CMP1[:, :]])

    if stage <= 3.4:
        return finish_partial()

    @blk.gpsimd
    def _(g):
        g.wait_ge(s_ev.h, ev_tr1_end)
        for c in range(4):
            g.load_library(libcfg.sparse_gather)
            s_gp.inc(g.sparse_gather(out=L1B[:, 320 * c:320 * (c + 1)],
                                     in_=SRCA[:, 512 * c:512 * (c + 1)],
                                     num_found=NF1[:, 8 + c:9 + c]))

    ev_sanB1 = nf_group(gp0 + 11, 8, 12,
                        [L1B[:, 320 * c:320 * (c + 1)] for c in range(4)])

    @blk.gpsimd
    def _(g):
        g.wait_ge(s_ev.h, ev_sanB1)
        g.load_library(libcfg.sparse_gather)
        s_gp.inc(g.sparse_gather(out=L2B[:, 0:320], in_=L1B[:, 0:640],
                                 num_found=NF1[:, 12:13]))
        g.load_library(libcfg.sparse_gather)
        s_gp.inc(g.sparse_gather(out=L2B[:, 320:640], in_=L1B[:, 640:1280],
                                 num_found=NF1[:, 13:14]))

    ev_sanB2 = nf_group(gp0 + 13, 12, 14, [L2B[:, 0:320], L2B[:, 320:640]])

    @blk.gpsimd
    def _(g):
        g.wait_ge(s_ev.h, ev_sanB2)
        g.load_library(libcfg.sparse_gather)
        s_gp.inc(g.sparse_gather(out=CMP2[:, :], in_=L2B[:, :],
                                 num_found=NF1[:, 14:15]))
        g.load_library(libcfg.mlp)

    ev_sanB3 = nf_group(gp0 + 14, 14, 15, [CMP2[:, :]])

    @blk.vector
    def _(v):
        v = SerEng(v, s_ev)
        v.wait_ge(s_gp.h, gp0 + 14)
        v.memset(IG[:, :], 0)
        v.memset(ISC[:, :], 0)
        v.tensor_scalar(out=CMP1[:, :], in0=CMP1[:, :], scalar1=1.0,
                        scalar2=None, op0=ALU.add)
        v.tensor_scalar(out=CMP2[:, :], in0=CMP2[:, :], scalar1=0.0,
                        scalar2=None, op0=ALU.max)
        v.tensor_scalar(out=SRCA[:, 0:320], in0=CMP1[:, :], scalar1=-1.0,
                        scalar2=0.0, op0=ALU.add, op1=ALU.max)
        v.tensor_copy(out=IG[0:16, :], in_=SRCA[:, 0:320])
        v.tensor_copy(out=ISC[0:16, :], in_=CMP1[:, 0:320])
    ev_lists = s_ev.n

    if stage <= 3.5:
        return finish_partial()

    @blk.sync
    def _(q):
        q.wait_ge(s_ev.h, ev_lists)
        dma(s_st, q, q.dma_start, out=IG[16:32, :], in_=IG[0:16, :])
        dma(s_st, q, q.dma_start, out=ISC[16:32, :], in_=ISC[0:16, :])
        dma(s_st, q, q.dma_start, out=bncw.rearrange("o (r jj) -> r o jj", r=16),
            in_=CMP2[:, 0:320].unsqueeze(1))
        if "d_lists" in dbg:
            dma(s_st, q, q.dma_start, out=dbg["d_lists"][:, 0:320], in_=CMP1[:, :])
            dma(s_st, q, q.dma_start, out=dbg["d_lists"][:, 512:832], in_=CMP2[:, :])
    st_idx0 = s_st.n

    if stage <= 3.6:
        return finish_partial()

    @blk.sync
    def _(q):
        q.wait_ge(s_st.h, st_idx0)
        for hi in range(8):
            dma(s_st, q, q.dma_start, out=WTSL[16 * hi:16 * (hi + 1), :],
                in_=bncw.rearrange("o (r c2 hi2) -> r (o c2 hi2)", r=16, hi2=8)
                [:, hi::8][:, 0:40],
                allow_slow_non_contiguous=True)
        if "d_wtsl" in dbg:
            dma(s_st, q, q.dma_start, out=dbg["d_wtsl"], in_=WTSL[:, :])
    st_wtsl = s_st.n

    if stage <= 4:
        return finish_partial()

    free("iotaslt", "MK", "NFB16", "NFF", "NF1", "L2B", "L1B", "L2A", "L1A",
         "CMP2", "CMP1", "SRCA", "WSUM",
         "PTp", "PWp", "KANY", "CARR", "TOTS", "RST", "BCT", "M32", "CT32",
         "CNTS", "JK2", "JK", "KEEPA", "EQF", "WPL", "EPL", "WPLN", "SC2",
         "SC1", "TK", "LGA")
    free("XN")
    M2 = sbuf("M2", [128, FFC, 1024], BF16)
    GIN = sbuf("GIN", [128, 2, DC, 512], BF16)
    WPR = sbuf("WPR", [128, DC, 512], BF16)
    WGA = sbuf("WGA", [128, DC, 512], BF16)
    WPO = sbuf("WPO", [128, FFC, D], BF16)
    EOU = sbuf("EOU", [128, 4, D], F32)
    TMPX = sbuf("TMPX", [128, 2, 512], F32)

    CH5 = 5
    if stage <= 4.3:
        CH5 = 1
    elif stage <= 4.6:
        CH5 = 3

    gp5 = s_gp.n
    gd5 = s_gd.n
    gw5 = s_gw.n
    pe5 = s_pe.n
    sc5 = s_sc.n
    ev5 = s_ev.n

    def pe_g(c, gl):
        return pe5 + 80 * c + gl + 1

    def pe_p(c, bl):
        return pe5 + 80 * c + 64 + bl + 1

    def sc_g(c, gl):
        return sc5 + 64 * c + gl + 1

    def ev_g(c, gl):
        return ev5 + 80 * c + gl + 1

    def ev_p(c, bl):
        return ev5 + 80 * c + 64 + bl + 1

    def gw_grp(c, grp):
        return gw5 + (17 * c + 2 * (grp + 1)) * 16

    def gw_wpo(c):
        return gw5 + (17 * c + 17) * 16

    def gp_gath(c):
        return gd5 + 64 * c + 32

    def gp_scat(c, half):
        if c < 0:
            return gd5
        return gd5 + 64 * c + 32 + 16 * (half + 1)

    if stage <= 4.25:
        @blk.gpsimd
        def _(g):
            g.wait_ge(s_cc.h, cc_xn)
            g.wait_ge(s_st.h, st_wtsl)
            g.load_library(libcfg.mlp)
            if stage > 4.15:
                dma(s_gd, g, g.dma_gather,
                    out_ap=GIN[:, 0, :, :], in_ap=xnall[:],
                    idxs_ap=IG[:, 0:32], num_idxs=512,
                    num_idxs_reg=512, elem_size=D, transpose=True)
        return finish_partial()

    @blk.gpsimd
    def _(g):
        g.wait_ge(s_cc.h, cc_xn)
        g.wait_ge(s_st.h, st_wtsl)
        for c in range(CH5):
            if c >= 1:
                g.wait_ge(s_pe.h, pe5 + 80 * (c - 1) + 64)
            for gh in range(2):
                g.load_library(libcfg.mlp)
                dma(s_gd, g, g.dma_gather,
                    out_ap=GIN[:, gh, :, :], in_ap=xnall[:],
                    idxs_ap=IG[:, 64 * c + 32 * gh:64 * c + 32 * (gh + 1)],
                    num_idxs=512, num_idxs_reg=512, elem_size=D, transpose=True)
            for grp in range(8):
                if c > 0 or grp >= 1:
                    g.wait_ge(s_pe.h, pe5 + 80 * c + 8 * grp
                              if grp >= 1 else pe5 + 80 * (c - 1) + 64)
                fsl = slice(512 * grp, 512 * (grp + 1))
                dma(s_gw, g, g.dma_start, 
                    out=WPR[:],
                    in_=pre_t.rearrange("(k p) f -> p k f", p=128)[:, :, fsl])
                dma(s_gw, g, g.dma_start, 
                    out=WGA[:],
                    in_=gatew_t.rearrange("(k p) f -> p k f", p=128)[:, :, fsl])
            if c >= 1:
                g.wait_ge(s_pe.h, pe5 + 80 * (c - 1) + 80)
            dma(s_gw, g, g.dma_start, 
                out=WPO[:],
                in_=post_t.rearrange("(f p) d -> p f d", p=128))
            for half in range(2):
                g.wait_ge(s_ev.h, ev_p(c, 8 * (half + 1) - 1))
                g.load_library(libcfg.mlp)
                dma(s_gd, g, g.dma_scatter_add, 
                    out_ap=moeacc[:], in_ap=EOU[:],
                    idxs_ap=ISC[:, 64 * c + 32 * half:64 * c + 32 * (half + 1)],
                    num_idxs=512, num_idxs_reg=512, elem_size=D)

    @blk.tensor
    def _(t):
        for c in range(CH5):
            t.wait_ge(s_gd.h, gp_gath(c))
            for gl in range(64):
                ffc, tt = gl // 2, gl % 2
                grp, j = ffc // 4, ffc % 4
                t.wait_ge(s_gw.h, gw_grp(c, grp))
                gg = 64 * c + gl
                if gg >= 2:
                    t.wait_ge(s_sc.h, sc_g(c, gl - 2) if gl >= 2
                              else sc_g(c - 1, 62 + gl))
                    t.wait_ge(s_ev.h, ev_g(c, gl - 2) if gl >= 2
                              else ev_p(c - 1, 15))
                pre_b = psW0 if gl % 2 == 0 else psX
                gat_b = psW1 if gl % 2 == 0 else psY5
                sl = slice(512 * tt, 512 * (tt + 1))
                for k in range(DC):
                    t.matmul(pre_b[:, :], WPR[:, k, 128 * j:128 * (j + 1)],
                             GIN[:, tt, k, :], start=(k == 0),
                             stop=(k == DC - 1))
                for k in range(DC):
                    mm = t.matmul(gat_b[:, :], WGA[:, k, 128 * j:128 * (j + 1)],
                                  GIN[:, tt, k, :], start=(k == 0),
                                  stop=(k == DC - 1))
                s_pe.inc(mm)
            t.wait_ge(s_gw.h, gw_wpo(c))
            for bl in range(16):
                half, scp, dtile = bl // 8, (bl // 2) % 4, bl % 2
                scg = half * 4 + scp
                bb = 16 * c + bl
                if bl >= 2:
                    t.wait_ge(s_ev.h, ev_p(c, bl - 2))
                else:
                    t.wait_ge(s_ev.h, ev_g(c, 63))
                pw = psW0 if bl % 2 == 0 else psW1
                for ffc in range(FFC):
                    mm = t.matmul(pw[:, :], M2[:, ffc, 128 * scg:128 * (scg + 1)],
                                  WPO[:, ffc, 512 * dtile:512 * (dtile + 1)],
                                  start=(ffc == 0), stop=(ffc == FFC - 1))
                s_pe.inc(mm)
    pe5_end = s_pe.n

    @blk.scalar
    def _(sc):
        sc = SerEng(sc, s_sc)
        for c in range(CH5):
            for gl in range(64):
                sc.wait_ge(s_pe.h, pe_g(c, gl))
                gg = 64 * c + gl
                if gg >= 2:
                    sc.wait_ge(s_ev.h, ev_g(c, gl - 2) if gl >= 2
                               else ev_g(c - 1, 62 + gl))
                gat_b = psW1 if gl % 2 == 0 else psY5
                sc.activation(out=TMPX[:, gl % 2, :], in_=gat_b[:, :],
                                       func=AF.Silu)

    @blk.vector
    def _(v):
        v = SerEng(v, s_ev)
        for c in range(CH5):
            for gl in range(64):
                ffc, tt = gl // 2, gl % 2
                v.wait_ge(s_sc.h, sc_g(c, gl))
                pre_b = psW0 if gl % 2 == 0 else psX
                sl = slice(512 * tt, 512 * (tt + 1))
                v.tensor_tensor(out=M2[:, ffc, sl], in0=pre_b[:, :],
                                         in1=TMPX[:, gl % 2, :], op=ALU.mult)
            for bl in range(16):
                half, scp, dtile = bl // 8, (bl // 2) % 4, bl % 2
                v.wait_ge(s_pe.h, pe_p(c, bl))
                if bl < 8:
                    v.wait_ge(s_gd.h, gp_scat(c - 1, 1))
                else:
                    v.wait_ge(s_gd.h, gp_scat(c, 0))
                pw = psW0 if bl % 2 == 0 else psW1
                col = 8 * c + 4 * half + scp
                v.tensor_scalar(
                    out=EOU[:, scp, 512 * dtile:512 * (dtile + 1)],
                    in0=pw[:, :], scalar1=WTSL[:, col:col + 1], scalar2=None,
                    op0=ALU.mult)
    ev5_end = s_ev.n

    if stage <= 4.9:
        return finish_partial()

    @blk.gpsimd
    def _(g):
        g.wait_ge(s_gd.h, gd5 + 64 * CH5)
        s_cc.inc(g.collective_compute(
            "ReduceScatter", ALU.add, replica_groups=RG,
            ins=[moeacc[1:T + 1, :]], outs=[moers[:]]))
    cc_rs = s_cc.n

    if stage <= 5:
        return finish_partial()

    free("TMPX", "EOU", "WPO", "WGA", "WPR", "GIN", "M2")
    XF = sbuf("XF", [128, 2, D], F32)
    MF = sbuf("MF", [128, 2, D], F32)
    AFB = sbuf("AFB", [128, 2, D], BF16)
    OF = sbuf("OF", [128, 16, D], F32)

    st6 = s_st.n
    ev6 = s_ev.n
    n_per_tb = 3 + (1 if "d_moe" in dbg else 0)

    @blk.sync
    def _(q):
        q.wait_ge(s_cc.h, cc_rs)
        for tb in range(16):
            r = tb % 2
            if tb >= 2:
                q.wait_ge(s_ev.h, ev6 + 2 * (tb - 1))
            sl = slice(128 * tb, 128 * (tb + 1))
            dma(s_st, q, q.dma_start, out=XF[:, r, :], in_=x_tm[sl, :])
            dma(s_st, q, q.dma_start, out=MF[:, r, :], in_=moers[sl, :])
            dma(s_st, q, q.dma_start, out=AFB[:, r, :], in_=attnfm[:, sl],
                                 transpose=True)
            if "d_moe" in dbg:
                dma(s_st, q, q.dma_start, out=dbg["d_moe"][sl, :], in_=moers[sl, :])

    @blk.vector
    def _(v):
        v = SerEng(v, s_ev)
        for tb in range(16):
            r = tb % 2
            v.wait_ge(s_st.h, st6 + 16 * n_per_tb * (tb + 1))
            v.tensor_add(OF[:, tb, :], XF[:, r, :], AFB[:, r, :])
            v.tensor_add(OF[:, tb, :], OF[:, tb, :], MF[:, r, :])
    ev6_end = s_ev.n

    @blk.sync
    def _(q):
        for tb in range(16):
            r = tb % 2
            q.wait_ge(s_ev.h, ev6 + 2 * (tb + 1))
            dma(s_st, q, q.dma_start, out=out_tm[128 * tb:128 * (tb + 1), :],
                                 in_=OF[:, tb, :])
        q.wait_ge(s_st.h, s_st.n)

    blk_g.__exit__(None, None, None)
    for n, g in reversed(guards):
        g.__exit__(None, None, None)
    return nc


def make_consts():
    ident = np.eye(128, dtype=ml_dtypes.bfloat16)
    ones = np.ones((128, 1), np.float32)
    ones1 = np.ones((1, 128), np.float32)
    iota8 = np.broadcast_to(np.arange(8, dtype=np.float32), (128, 8)).copy()
    f = np.arange(128)[:, None] * 256 + np.arange(256)[None, :]
    tokp1 = (f // 2 + 1).astype(np.float32)
    iotasl = (np.arange(2048)[None, :] * 16
              + np.arange(16)[:, None]).astype(np.float32)
    triu = (np.arange(128)[:, None] < np.arange(128)[None, :]).astype(np.float32)
    return ident, ones, ones1, iota8, tokp1, iotasl, triu


_BUILD_CACHE = {}


def kernel(L0_attn_x, L0_attn_rms_mix_w, L0_attn_rms_ffn_w, L0_attn_Wg,
           L0_attn_bg, L0_attn_Wv, L0_attn_bv, L0_attn_Wd, L0_attn_bd,
           L0_attn_W_gate, L0_attn_W_up, L0_attn_W_out, L0_ffn_router_weight,
           L0_ffn_ff_pre_act_weight, L0_ffn_gate_weight,
           L0_ffn_ff_post_act_weight, _debug=(), _return_debug=False):
    key = tuple(sorted(_debug))
    if key not in _BUILD_CACHE:
        nc = build(debug=_debug)
        lower_extended_insts(nc)
        _BUILD_CACHE[key] = nc
    nc = _BUILD_CACHE[key]

    x = np.ascontiguousarray(np.asarray(L0_attn_x, dtype=np.float32)
                             .reshape(T, D))
    ident, ones, ones1, iota8, tokp1, iotasl, triu = make_consts()

    def tr(w):
        return np.ascontiguousarray(np.asarray(w, np.float32).T)

    wgT, wvT, wdT = tr(L0_attn_Wg), tr(L0_attn_Wv), tr(L0_attn_Wd)
    wgateT, wupT = tr(L0_attn_W_gate), tr(L0_attn_W_up)
    woutT = tr(L0_attn_W_out)
    routerT = tr(L0_ffn_router_weight)
    b3 = np.stack([np.asarray(L0_attn_bg, np.float32),
                   np.asarray(L0_attn_bv, np.float32),
                   np.asarray(L0_attn_bd, np.float32)], 0)
    biases = np.ascontiguousarray(
        b3.reshape(3, DC, 128).transpose(2, 0, 1).reshape(128, 3 * DC))
    rw = np.stack([np.asarray(L0_attn_rms_mix_w, np.float32),
                   np.asarray(L0_attn_rms_ffn_w, np.float32)], 0)
    rmsw = np.ascontiguousarray(
        rw.reshape(2, DC, 128).transpose(2, 0, 1).reshape(128, 2 * DC))

    in_maps = []
    for c in range(8):
        xs = x[TL * c:TL * (c + 1)]
        carrysel = np.zeros((128, 8), np.float32)
        if c % 2 == 1:
            carrysel[:, c - 1] = 1.0
        in_maps.append({
            "x_fm": np.ascontiguousarray(xs.T),
            "x_tm": np.ascontiguousarray(xs),
            "wg_t": wgT, "wv_t": wvT, "wd_t": wdT,
            "biases": biases, "rmsw": rmsw,
            "wgate_t": wgateT, "wup_t": wupT, "wout_t": woutT,
            "router_t": routerT,
            "pre_t": tr(np.asarray(L0_ffn_ff_pre_act_weight)[c]),
            "gatew_t": tr(np.asarray(L0_ffn_gate_weight)[c]),
            "post_t": tr(np.asarray(L0_ffn_ff_post_act_weight)[c]),
            "c_ident": ident, "c_identf": np.eye(128, dtype=np.float32), "c_ones": ones, "c_ones1": ones1,
            "c_iota8": iota8, "c_tokp1": tokp1, "c_iotasl": iotasl,
            "c_myexp": np.full((128, 1), float(c), np.float32),
            "c_carrysel": carrysel, "c_triu": triu,
        })

    res = run_bass_kernel_spmd(nc, in_maps, core_ids=list(range(8)))
    out = np.concatenate([res.results[c]["out_tm"] for c in range(8)], axis=0)
    out = out.reshape(4, 4096, 1024).astype(np.float32)
    if _return_debug:
        return out, res
    return out


# version 02
# speedup vs baseline: 1.1110x; 1.1110x over previous
import numpy as np
import ml_dtypes
import concourse.bass as bass
import concourse.mybir as mybir
import concourse.library_config as libcfg
from concourse.library_overlay import lower_extended_insts
from concourse.bass_utils import run_bass_kernel_spmd

F32 = mybir.dt.float32
BF16 = mybir.dt.bfloat16
I16 = mybir.dt.int16
U32 = mybir.dt.uint32
AF = mybir.ActivationFunctionType
ALU = mybir.AluOpType
AX = mybir.AxisListType.X

D, FF, E, DFF, CAP = 1024, 3072, 8, 4096, 5120
T, TL = 16384, 2048
DC, FC, FFC = 8, 24, 32
A = 2 * T
RG = [list(range(8))]
VAL_ITERS = 30
SNAP_ITERS = 8


def build(debug=(), stage=6):
    nc = bass.Bass("TRN2", target_bir_lowering=False)
    dram = nc.dram_tensor

    x_fm = dram("x_fm", [D, TL], F32, kind="ExternalInput").ap()
    x_tm = dram("x_tm", [TL, D], F32, kind="ExternalInput").ap()
    wg_t = dram("wg_t", [D, D], F32, kind="ExternalInput").ap()
    wv_t = dram("wv_t", [D, D], F32, kind="ExternalInput").ap()
    wd_t = dram("wd_t", [D, D], F32, kind="ExternalInput").ap()
    biases = dram("biases", [128, 3 * DC], F32, kind="ExternalInput").ap()
    rmsw = dram("rmsw", [128, 2 * DC], F32, kind="ExternalInput").ap()
    wgate_t = dram("wgate_t", [D, FF], F32, kind="ExternalInput").ap()
    wup_t = dram("wup_t", [D, FF], F32, kind="ExternalInput").ap()
    wout_t = dram("wout_t", [FF, D], F32, kind="ExternalInput").ap()
    router_t = dram("router_t", [D, E], F32, kind="ExternalInput").ap()
    pre_t = dram("pre_t", [D, DFF], F32, kind="ExternalInput").ap()
    gatew_t = dram("gatew_t", [D, DFF], F32, kind="ExternalInput").ap()
    post_t = dram("post_t", [DFF, D], F32, kind="ExternalInput").ap()
    c_ident = dram("c_ident", [128, 128], BF16, kind="ExternalInput").ap()
    c_identf = dram("c_identf", [128, 128], F32, kind="ExternalInput").ap()
    c_ones = dram("c_ones", [128, 1], F32, kind="ExternalInput").ap()
    c_ones1 = dram("c_ones1", [1, 128], F32, kind="ExternalInput").ap()
    c_iota8 = dram("c_iota8", [128, 8], F32, kind="ExternalInput").ap()
    c_tokp1 = dram("c_tokp1", [128, 256], F32, kind="ExternalInput").ap()
    c_iotasl = dram("c_iotasl", [16, 2048], F32, kind="ExternalInput").ap()
    c_myexp = dram("c_myexp", [128, 1], F32, kind="ExternalInput").ap()
    c_carrysel = dram("c_carrysel", [128, 8], F32, kind="ExternalInput").ap()
    c_triu = dram("c_triu", [128, 128], F32, kind="ExternalInput").ap()

    out_tm = dram("out_tm", [TL, D], F32, kind="ExternalOutput").ap()
    dbg_specs = {
        "d_xn": ([128, DC * TL], BF16), "d_x1": ([128, DC * TL], BF16),
        "d_thr": ([32, 16], F32), "d_lists": ([16, 2 * 512], F32),
        "d_wtsl": ([128, 40], F32), "d_kany": ([128, 256], F32),
        "d_moe": ([TL, D], F32), "d_l1a": ([17, 1280], F32),
    }
    dbg = {}
    for name in debug:
        shp, dtp = dbg_specs[name]
        dbg[name] = dram(name, shp, dtp, kind="ExternalOutput").ap()

    xnloc = dram("xnloc", [TL, D], BF16, kind="Internal").ap()
    xnall = dram("xnall", [T, D], BF16, kind="Internal").ap()
    lgloc = dram("lgloc", [TL, E], F32, kind="Internal").ap()
    lgall = dram("lgall", [T, E], F32, kind="Internal").ap()
    carrloc = dram("carrloc", [1, D], F32, kind="Internal").ap()
    carrall = dram("carrall", [8, D], F32, kind="Internal").ap()
    moeacc = dram("moeacc", [T + 1, D], F32, kind="Internal").ap()
    moers = dram("moers", [TL, D], F32, kind="Internal").ap()
    attnfm = dram("attnfm", [D, TL], BF16, kind="Internal").ap()
    bnc1 = dram("bnc1", [1, A], F32, kind="Internal").ap()
    bnc2 = dram("bnc2", [1, A], F32, kind="Internal").ap()
    bncw = dram("bncw", [1, CAP], F32, kind="Internal").ap()

    guards = []

    def sbuf_r(name, shape, dtp):
        return sbuf(name, shape, dtp, side="right")

    def sbuf(name, shape, dtp, side="left"):
        g = nc.sbuf_tensor(name, shape, dtp, side=side)
        t = g.__enter__()
        guards.append((name, g))
        return t

    def psum(name, shape, dtp=F32):
        g = nc.psum_tensor(name, shape, dtp)
        t = g.__enter__()
        guards.append((name, g))
        return t

    def free(*names):
        for nm in names:
            for i, (n, g) in enumerate(guards):
                if n == nm:
                    g.__exit__(None, None, None)
                    guards.pop(i)
                    break
            else:
                raise KeyError(nm)

    identt = sbuf_r("identt", [128, 128], BF16)
    identf = sbuf_r("identf", [128, 128], F32)
    onest = sbuf_r("onest", [128, 1], F32)
    ones1t = sbuf_r("ones1t", [1, 128], F32)
    iota8t = sbuf_r("iota8t", [128, 8], F32)
    tokp1t = sbuf_r("tokp1t", [128, 256], F32)
    myexpt = sbuf_r("myexpt", [128, 1], F32)
    carryselt = sbuf_r("carryselt", [128, 8], F32)
    triut = sbuf_r("triut", [128, 128], F32)
    biast = sbuf_r("biast", [128, 3 * DC], F32)
    rmswt = sbuf_r("rmswt", [128, 2 * DC], F32)
    zerocol = sbuf_r("zerocol", [128, 1], F32)
    EPSC = sbuf_r("EPSC", [128, 2], F32)
    rtb = sbuf_r("rtb", [128, DC * E], BF16)
    HFIN = sbuf_r("HFIN", [128, DC], F32)
    HALL = sbuf_r("HALL", [128, 8, DC], F32)
    H0 = sbuf_r("H0", [128, DC], F32)
    IG = sbuf_r("IG", [128, 320], I16)
    ISC = sbuf_r("ISC", [128, 320], I16)
    WTSL = sbuf_r("WTSL", [128, 40], F32)
    XN = sbuf_r("XN", [128, DC, TL], BF16)

    class Ctr:
        def __init__(self, h):
            self.h = h
            self.n = 0

        def inc(self, instr, k=1):
            instr.then_inc(self.h, k)
            self.n += k
            return instr

    def sem(name):
        g = nc.semaphore(name)
        h = g.__enter__()
        guards.append((name, g))
        return Ctr(h)

    def dma(ctr, eng, fn, *a, **k):
        eng.wait_ge(ctr.h, ctr.n)
        return ctr.inc(fn(*a, **k), 16)

    class SerEng:

        def __init__(self, eng, ctr):
            self._e = eng
            self._c = ctr

        def wait_ge(self, h, v_):
            self._e.wait_ge(h, v_)

        def __getattr__(self, name):
            f = getattr(self._e, name)

            def g(*a, **k):
                self._e.wait_ge(self._c.h, self._c.n)
                return self._c.inc(f(*a, **k))
            return g

    s_w = sem("s_w")
    s_gw = sem("s_gw")
    s_pe = sem("s_pe")
    s_ev = sem("s_ev")
    s_sc = sem("s_sc")
    s_gp = sem("s_gp")
    s_gd = sem("s_gd")
    s_cc = sem("s_cc")
    s_st = sem("s_st")

    blk_g = nc.Block()
    blk = blk_g.__enter__()

    def finish_partial():
        @blk.sync
        def _(q):
            q.wait_ge(s_pe.h, s_pe.n)
            q.wait_ge(s_ev.h, s_ev.n)
            q.wait_ge(s_sc.h, s_sc.n)
            q.wait_ge(s_gp.h, s_gp.n)
            q.wait_ge(s_gd.h, s_gd.n)
            q.wait_ge(s_gw.h, s_gw.n)
            q.wait_ge(s_cc.h, s_cc.n)
            q.wait_ge(s_st.h, s_st.n)
            for tb in range(16):
                s_st.inc(q.dma_start(out=out_tm[128 * tb:128 * (tb + 1), :],
                                     in_=x_tm[128 * tb:128 * (tb + 1), :]), 16)
            q.wait_ge(s_st.h, s_st.n)
        blk_g.__exit__(None, None, None)
        for _n, _g in reversed(guards):
            _g.__exit__(None, None, None)
        return nc


    XT = sbuf("XT", [128, DC, TL], F32)
    XNT = sbuf("XNT", [128, D], BF16)
    LGV = sbuf("LGV", [128, 16 * E], F32)
    zeroz = sbuf("zeroz", [128, 4096], F32)
    SR = sbuf("SR", [1, 2 * TL], F32)
    SRX = sbuf("SRX", [1, TL], F32)
    X2T = sbuf("X2T", [128, DC, TL], F32)
    rA = SR[:, 0:TL]
    rB = SR[:, TL:2 * TL]
    rX = SRX[:, :]

    psA = psum("psA", [1, 512])
    psB = psum("psB", [1, 512])
    psBC = psum("psBC", [128, 512])
    psT = psum("psT", [128, 128], BF16)
    psL = psum("psL", [128, E])

    @blk.sync
    def _(q):
        for d, s in [(identt, c_ident), (identf, c_identf), (onest, c_ones), (ones1t, c_ones1),
                     (iota8t, c_iota8), (tokp1t, c_tokp1), (myexpt, c_myexp),
                     (carryselt, c_carrysel), (triut, c_triu),
                     (biast, biases), (rmswt, rmsw)]:
            s_w.inc(q.dma_start(out=d[:], in_=s[:]), 16)
        s_w.inc(q.dma_start(out=XT[:],
                            in_=x_fm.rearrange("(k p) t -> p k t", p=128)), 16)
        s_w.inc(q.dma_start(
            out=X2T[:, 0, 0:DC * E].rearrange("p (k e) -> p k e", e=E),
            in_=router_t.rearrange("(k p) e -> p k e", p=128)), 16)
    w_all = s_w.n

    @blk.vector
    def _(v):
        v = SerEng(v, s_ev)
        v.memset(zerocol[:], 0.0)
        v.memset(EPSC[:, 0:1], 1e-5)
        v.memset(EPSC[:, 1:2], 1e-6)
        v.memset(zeroz[:], 0.0)
    ev_zero = s_ev.n

    @blk.scalar
    def _(q):
        q.wait_ge(s_ev.h, ev_zero)
        for r0 in range(0, T, 512):
            dma(s_st, q, q.dma_start, 
                out=moeacc[r0:r0 + 512, :].rearrange("(a b) d -> a (b d)", b=4),
                in_=zeroz[:, :])
        dma(s_st, q, q.dma_start, out=moeacc[T:T + 1, :], in_=zeroz[0:1, 0:D])
        q.wait_ge(s_w.h, w_all)
        s_sc.inc(q.copy(out=rtb[:], in_=X2T[:, 0, 0:DC * E]))
    st_zero = s_st.n
    sc_rtb = s_sc.n

    @blk.scalar
    def _(sc):
        sc = SerEng(sc, s_sc)
        sc.wait_ge(s_sc.h, sc_rtb)
        for k in range(DC):
            sc.activation(out=X2T[:, k, :], in_=XT[:, k, :],
                                   func=AF.Square)
    sc_sq = s_sc.n

    pe_stats0 = s_pe.n

    @blk.tensor
    def _(t):
        t.wait_ge(s_sc.h, sc_sq)
        for tt in range(4):
            sl = slice(512 * tt, 512 * (tt + 1))
            if tt > 0:
                t.wait_ge(s_ev.h, ev_zero + 2 * tt)
            for k in range(DC):
                t.matmul(psA[:, :], onest[:], XT[:, k, sl],
                         start=(k == 0), stop=(k == DC - 1))
            for k in range(DC):
                mm = t.matmul(psB[:, :], onest[:], X2T[:, k, sl],
                              start=(k == 0), stop=(k == DC - 1))
            s_pe.inc(mm)

    @blk.vector
    def _(v):
        v = SerEng(v, s_ev)
        for tt in range(4):
            v.wait_ge(s_pe.h, pe_stats0 + tt + 1)
            sl = slice(512 * tt, 512 * (tt + 1))
            v.tensor_scalar(out=rA[:, sl], in0=psA[:, :],
                                     scalar1=1.0 / D, scalar2=None, op0=ALU.mult)
            v.tensor_scalar(out=rB[:, sl], in0=psB[:, :],
                                     scalar1=1.0 / D, scalar2=None, op0=ALU.mult)
        v.tensor_tensor(out=rX, in0=rA, in1=rA, op=ALU.mult)
        v.tensor_sub(rB, rB, rX)
    ev_var = s_ev.n

    @blk.scalar
    def _(sc):
        sc = SerEng(sc, s_sc)
        sc.wait_ge(s_ev.h, ev_var)
        sc.activation(out=rX, in_=rB, func=AF.Sqrt, bias=EPSC[0:1, 0:1])
    sc_sd = s_sc.n

    @blk.vector
    def _(v):
        v = SerEng(v, s_ev)
        v.wait_ge(s_sc.h, sc_sd)
        v.reciprocal(out=rX, in_=rX)
        v.tensor_tensor(out=rB, in0=rB, in1=rX, op=ALU.mult)
        v.tensor_tensor(out=rB, in0=rB, in1=rX, op=ALU.mult)
    ev_mx2 = s_ev.n

    @blk.scalar
    def _(sc):
        sc = SerEng(sc, s_sc)
        sc.wait_ge(s_ev.h, ev_mx2)
        sc.activation(out=rB, in_=rB, func=AF.Sqrt, bias=EPSC[0:1, 1:2])
    sc_s2 = s_sc.n

    @blk.vector
    def _(v):
        v = SerEng(v, s_ev)
        v.wait_ge(s_sc.h, sc_s2)
        v.reciprocal(out=rB, in_=rB)
    ev_s2 = s_ev.n

    free("X2T")
    RBC = sbuf_r("RBC", [128, TL], F32)
    SBC = sbuf_r("SBC", [128, TL], F32)
    MBC = sbuf_r("MBC", [128, TL], F32)

    pe_bc0 = s_pe.n

    @blk.tensor
    def _(t):
        t.wait_ge(s_ev.h, ev_s2)
        for i in range(12):
            row = [rA, rX, rB][i // 4]
            tt = i % 4
            if i >= 1:
                t.wait_ge(s_ev.h, ev_s2 + i)
            s_pe.inc(t.matmul(psBC[:, :], ones1t[:],
                              row[:, 512 * tt:512 * (tt + 1)],
                              start=True, stop=True))

    @blk.vector
    def _(v):
        v = SerEng(v, s_ev)
        for i in range(12):
            dst = [MBC, RBC, SBC][i // 4]
            tt = i % 4
            v.wait_ge(s_pe.h, pe_bc0 + i + 1)
            if i < 4:
                v.tensor_scalar(out=dst[:, 512 * tt:512 * (tt + 1)],
                                         in0=psBC[:, :], scalar1=-1.0,
                                         scalar2=None, op0=ALU.mult)
            else:
                v.tensor_copy(out=dst[:, 512 * tt:512 * (tt + 1)],
                                       in_=psBC[:, :])
    ev_bc = s_ev.n

    @blk.vector
    def _(v):
        v = SerEng(v, s_ev)
        for k in range(DC):
            v.tensor_add(XN[:, k, :], XT[:, k, :], MBC[:, :])
            v.tensor_tensor(out=XN[:, k, :], in0=XN[:, k, :],
                                     in1=RBC[:, :], op=ALU.mult)
    ev_xn = s_ev.n

    pe_tb0 = s_pe.n
    ev_tb0 = s_ev.n

    @blk.tensor
    def _(t):
        t.wait_ge(s_ev.h, ev_xn)
        for tb in range(16):
            sl = slice(128 * tb, 128 * (tb + 1))
            if tb > 0:
                t.wait_ge(s_st.h, st_zero + 16 * tb)
            for k in range(DC):
                if tb > 0 or k >= 1:
                    t.wait_ge(s_ev.h, ev_tb0 + tb * (DC + 1) + k)
                s_pe.inc(t.transpose(psT[:, :], XN[:, k, sl], identt[:]))
            for k in range(DC):
                mm = t.matmul(psL[:, :], XN[:, k, sl], rtb[:, E * k:E * (k + 1)],
                              start=(k == 0), stop=(k == DC - 1))
            s_pe.inc(mm)

    @blk.vector
    def _(v):
        v = SerEng(v, s_ev)
        for tb in range(16):
            base = pe_tb0 + tb * (DC + 1)
            for k in range(DC):
                v.wait_ge(s_pe.h, base + k + 1)
                v.tensor_copy(out=XNT[:, 128 * k:128 * (k + 1)],
                                       in_=psT[:, :])
            v.wait_ge(s_pe.h, base + DC + 1)
            v.tensor_copy(out=LGV[:, E * tb:E * (tb + 1)], in_=psL[:, :])
    ev_tb_end = s_ev.n

    @blk.sync
    def _(q):
        for tb in range(16):
            q.wait_ge(s_ev.h, ev_tb0 + tb * (DC + 1) + DC)
            dma(s_st, q, q.dma_start, out=xnloc[128 * tb:128 * (tb + 1), :],
                                 in_=XNT[:, :])
        q.wait_ge(s_ev.h, ev_tb_end)
        dma(s_st, q, q.dma_start, out=lgloc.rearrange("(c p) e -> p c e", p=128),
                             in_=LGV[:, :].rearrange("p (c e) -> p c e", e=E))
        if "d_xn" in dbg:
            dma(s_st, q, q.dma_start, out=dbg["d_xn"],
                                 in_=XN.rearrange("p k t -> p (k t)"))
    st_xn = s_st.n

    @blk.gpsimd
    def _(g):
        g.wait_ge(s_st.h, st_xn)
        s_cc.inc(g.collective_compute("AllGather", ALU.bypass, replica_groups=RG,
                                      ins=[lgloc[:]], outs=[lgall[:]]))
        s_cc.inc(g.collective_compute("AllGather", ALU.bypass, replica_groups=RG,
                                      ins=[xnloc[:]], outs=[xnall[:]]))
    cc_lg, cc_xn = 1, 2

    if stage <= 1:
        return finish_partial()

    free("SRX", "SR", "zeroz", "LGV", "XNT", "XT")
    free("psL", "psT")
    free("MBC")
    XSA = sbuf("XSA", [128, DC, TL], BF16)
    TMPV = sbuf("TMPV", [128, 2, 512], F32)
    ATT = sbuf("ATT", [128, DC, TL], BF16)
    XN2 = sbuf("XN2", [128, DC, TL], BF16)
    WMX0 = sbuf("WMX0", [128, DC, D], BF16)
    WMX1 = sbuf("WMX1", [128, DC, D], BF16)
    psW0 = psum("psW0", [128, 512])
    psW1 = psum("psW1", [128, 512])

    @blk.vector
    def _(v):
        v = SerEng(v, s_ev)
        v.wait_ge(s_st.h, st_xn)
        v.wait_ge(s_pe.h, pe_tb0 + 16 * (DC + 1))
        v.wait_ge(s_sc.h, sc_sq)
        for k in range(DC):
            v.tensor_scalar(out=ATT[:, k, :], in0=XN[:, k, :],
                            scalar1=rmswt[:, k:k + 1], scalar2=None, op0=ALU.mult)
            v.tensor_tensor(out=XN2[:, k, :], in0=ATT[:, k, :],
                                     in1=SBC[:, :], op=ALU.mult)
    ev_xn2 = s_ev.n

    gw_mix0 = s_gw.n

    @blk.gpsimd
    def _(g):
        g.wait_ge(s_st.h, st_xn)
        g.wait_ge(s_pe.h, pe_tb0 + 16 * (DC + 1))
        g.wait_ge(s_sc.h, sc_sq)
        g.wait_ge(s_ev.h, ev_xn)
        dma(s_gw, g, g.dma_start, out=WMX0[:],
                             in_=wg_t.rearrange("(k p) o -> p k o", p=128))
        dma(s_gw, g, g.dma_start, out=WMX1[:],
                             in_=wv_t.rearrange("(k p) o -> p k o", p=128))

    pe_mix0 = s_pe.n
    sc_mix0 = s_sc.n
    ev_mix0 = s_ev.n

    @blk.tensor
    def _(t):
        t.wait_ge(s_ev.h, ev_xn2)
        for gi in range(96):
            wi, oc, tt = gi // 32, (gi // 4) % 8, gi % 4
            buf = [WMX0, WMX1, WMX0][wi]
            if gi % 32 == 0:
                t.wait_ge(s_gw.h, gw_mix0 + 16 * (wi + 1))
            if gi >= 2:
                t.wait_ge(s_sc.h, sc_mix0 + gi - 1)
            pw = psW0 if gi % 2 == 0 else psW1
            sl = slice(512 * tt, 512 * (tt + 1))
            for k in range(DC):
                mm = t.matmul(pw[:, :], buf[:, k, 128 * oc:128 * (oc + 1)],
                              XN2[:, k, sl], start=(k == 0), stop=(k == DC - 1))
            s_pe.inc(mm)
    pe_mix_end = s_pe.n

    @blk.gpsimd
    def _(g):
        g.wait_ge(s_pe.h, pe_mix0 + 32)
        dma(s_gw, g, g.dma_start, out=WMX0[:],
                             in_=wd_t.rearrange("(k p) o -> p k o", p=128))
    gw_mix_end = s_gw.n

    @blk.scalar
    def _(sc):
        sc = SerEng(sc, s_sc)
        for gi in range(96):
            wi, oc, tt = gi // 32, (gi // 4) % 8, gi % 4
            sc.wait_ge(s_pe.h, pe_mix0 + gi + 1)
            pw = psW0 if gi % 2 == 0 else psW1
            sl = slice(512 * tt, 512 * (tt + 1))
            bias = biast[:, wi * DC + oc:wi * DC + oc + 1]
            if wi == 0:
                sc.activation(out=XSA[:, oc, sl], in_=pw[:, :],
                                       func=AF.Sigmoid, bias=bias)
            else:
                if gi >= 34:
                    sc.wait_ge(s_ev.h, ev_mix0 + gi - 33)
                fn = AF.Tanh if wi == 1 else AF.Sigmoid
                sc.activation(out=TMPV[:, gi % 2, :], in_=pw[:, :],
                                       func=fn, bias=bias)
    sc_mix_end = s_sc.n

    @blk.vector
    def _(v):
        v = SerEng(v, s_ev)
        for gi in range(32, 96):
            wi, oc, tt = gi // 32, (gi // 4) % 8, gi % 4
            sl = slice(512 * tt, 512 * (tt + 1))
            v.wait_ge(s_sc.h, sc_mix0 + gi + 1)
            if wi == 1:
                v.tensor_tensor(out=XSA[:, oc, sl], in0=XSA[:, oc, sl],
                                         in1=TMPV[:, gi % 2, :], op=ALU.mult)
            else:
                v.tensor_scalar(out=ATT[:, oc, sl],
                                         in0=TMPV[:, gi % 2, :], scalar1=0.998,
                                         scalar2=0.001, op0=ALU.mult, op1=ALU.add)

    @blk.vector
    def _(v):
        v = SerEng(v, s_ev)
        for k in range(DC):
            v.tensor_tensor_scan(out=XSA[:, k, :], data0=ATT[:, k, :],
                                 data1=XSA[:, k, :], initial=0.0,
                                 op0=ALU.mult, op1=ALU.add)
            v.tensor_tensor_scan(out=ATT[:, k, :], data0=ATT[:, k, :],
                                 data1=zerocol[:].broadcast_to([128, TL]),
                                 initial=1.0, op0=ALU.mult, op1=ALU.add)
            v.tensor_copy(out=HFIN[:, k:k + 1], in_=XSA[:, k, TL - 1:TL])
        v.engine_nop()
    ev_scan = s_ev.n

    @blk.sync
    def _(q):
        q.wait_ge(s_ev.h, ev_scan)
        dma(s_st, q, q.dma_start, out=carrloc.rearrange("o (k p) -> p o k", p=128),
                             in_=HFIN[:, :].unsqueeze(1),
                             allow_slow_non_contiguous=True)
    st_carr = s_st.n

    @blk.gpsimd
    def _(g):
        g.wait_ge(s_st.h, st_carr)
        s_cc.inc(g.collective_compute("AllGather", ALU.bypass, replica_groups=RG,
                                      ins=[carrloc[:]], outs=[carrall[:]]))
    cc_carr = s_cc.n

    @blk.sync
    def _(q):
        q.wait_ge(s_cc.h, cc_carr)
        dma(s_st, q, q.dma_start, out=HALL[:],
                             in_=carrall.rearrange("r (k p) -> p r k", p=128),
                             allow_slow_non_contiguous=True)
    st_hall = s_st.n

    @blk.vector
    def _(v):
        v = SerEng(v, s_ev)
        v.wait_ge(s_st.h, st_hall)
        v.tensor_scalar(out=H0[:, :], in0=HALL[:, 0, :],
                        scalar1=carryselt[:, 0:1], scalar2=None, op0=ALU.mult)
        for r in range(1, 8):
            v.scalar_tensor_tensor(out=H0[:, :], in0=HALL[:, r, :],
                                   scalar=carryselt[:, r:r + 1], in1=H0[:, :],
                                   op0=ALU.mult, op1=ALU.add)
        for k in range(DC):
            v.scalar_tensor_tensor(out=XSA[:, k, :], in0=ATT[:, k, :],
                                   scalar=H0[:, k:k + 1], in1=XSA[:, k, :],
                                   op0=ALU.mult, op1=ALU.add)
            v.tensor_add(XN[:, k, :], XN[:, k, :], XSA[:, k, :])
    ev_x1 = s_ev.n

    def ev_x1k(k):
        return ev_x1 - 2 * DC + 2 * (k + 1)

    @blk.sync
    def _(q):
        if "d_x1" in dbg:
            q.wait_ge(s_ev.h, ev_x1)
            dma(s_st, q, q.dma_start, out=dbg["d_x1"],
                                 in_=XN.rearrange("p k t -> p (k t)"))

    if stage <= 2:
        return finish_partial()

    free("WMX1", "WMX0", "XN2", "ATT")
    free("SBC")
    SQ2 = sbuf("SQ2", [128, TL], F32)
    SR3 = sbuf("SR3", [1, TL], F32)
    r3 = SR3[:, :]
    MT = sbuf("MT", [128, FC, 1024], BF16)
    psX = psum("psX", [128, 512])
    psY5 = psum("psY5", [128, 512])
    WGF = sbuf("WGF", [128, DC, 6 * 128], BF16)
    WUF = sbuf("WUF", [128, DC, 6 * 128], BF16)
    WOF0 = sbuf("WOF0", [128, FC, 128], BF16)
    WOF1 = sbuf("WOF1", [128, FC, 128], BF16)
    ASTG = sbuf("ASTG", [128, 2, 512], BF16)

    pe_st2_0 = s_pe.n
    sc_st2_0 = s_sc.n

    @blk.scalar
    def _(sc):
        sc = SerEng(sc, s_sc)
        sc.wait_ge(s_ev.h, ev_x1)
        for k in range(DC):
            if k >= 1:
                sc.wait_ge(s_pe.h, pe_st2_0 + k)
            sc.activation(out=SQ2[:, :], in_=XN[:, k, :], func=AF.Square)

    @blk.tensor
    def _(t):
        for k in range(DC):
            t.wait_ge(s_sc.h, sc_st2_0 + k + 1)
            for tt in range(4):
                pw = [psA, psB, psW0, psW1][tt]
                mm = t.matmul(pw[0:1, 0:512], onest[:],
                              SQ2[:, 512 * tt:512 * (tt + 1)],
                              start=(k == 0), stop=(k == DC - 1))
            s_pe.inc(mm)
    pe_st2_end = s_pe.n

    @blk.vector
    def _(v):
        v = SerEng(v, s_ev)
        v.wait_ge(s_pe.h, pe_st2_end)
        for tt in range(4):
            pw = [psA, psB, psW0, psW1][tt]
            v.tensor_scalar(out=r3[:, 512 * tt:512 * (tt + 1)], in0=pw[0:1, 0:512],
                            scalar1=1.0 / D, scalar2=None, op0=ALU.mult)
        v.engine_nop()
    ev_st2 = s_ev.n

    @blk.scalar
    def _(sc):
        sc = SerEng(sc, s_sc)
        sc.wait_ge(s_ev.h, ev_st2)
        sc.activation(out=r3, in_=r3, func=AF.Sqrt, bias=EPSC[0:1, 1:2])
    sc_r3 = s_sc.n

    @blk.vector
    def _(v):
        v = SerEng(v, s_ev)
        v.wait_ge(s_sc.h, sc_r3)
        v.reciprocal(out=r3, in_=r3)
    ev_r3 = s_ev.n

    pe_r3b0 = s_pe.n

    @blk.tensor
    def _(t):
        t.wait_ge(s_ev.h, ev_r3)
        for tt in range(4):
            if tt >= 1:
                t.wait_ge(s_ev.h, ev_r3 + tt)
            s_pe.inc(t.matmul(psBC[:, :], ones1t[:],
                              r3[:, 512 * tt:512 * (tt + 1)],
                              start=True, stop=True))

    @blk.vector
    def _(v):
        v = SerEng(v, s_ev)
        for tt in range(4):
            v.wait_ge(s_pe.h, pe_r3b0 + tt + 1)
            v.tensor_copy(out=RBC[:, 512 * tt:512 * (tt + 1)],
                                   in_=psBC[:, :])
        for k in range(DC):
            v.tensor_scalar(out=SQ2[:, :], in0=XN[:, k, :],
                            scalar1=rmswt[:, DC + k:DC + k + 1], scalar2=None,
                            op0=ALU.mult)
            v.tensor_tensor(out=XSA[:, k, :], in0=SQ2[:, :],
                                     in1=RBC[:, :], op=ALU.mult)
    ev_xn3 = s_ev.n

    gwF = s_gw.n
    peF = s_pe.n
    scF = s_sc.n
    evF = s_ev.n
    stF = s_st.n

    def gw_ab(h, grp):
        return gwF + (16 * h + 2 * (grp + 1)) * 16

    def gw_wo(h, dsub):
        return gwF + (16 * h + 8 + dsub + 1) * 16

    def pe_a(h, gl):
        return peF + 64 * h + gl + 1

    def pe_b(h, bl):
        return peF + 64 * h + 48 + bl + 1

    def sc_a(h, gl):
        return scF + 48 * h + gl + 1

    def ev_a(h, gl):
        return evF + 64 * h + gl + 1

    def ev_b(h, bl):
        return evF + 64 * h + 48 + bl + 1

    @blk.gpsimd
    def _(g):
        g.wait_ge(s_pe.h, pe_mix_end)
        g.wait_ge(s_ev.h, ev_x1)
        for h in range(2):
            for grp in range(4):
                if h > 0 or grp >= 1:
                    g.wait_ge(s_pe.h, peF + 64 * h + 12 * grp if grp >= 1
                              else peF + 64 * (h - 1) + 48)
                csl = slice(6 * 128 * grp, 6 * 128 * (grp + 1))
                dma(s_gw, g, g.dma_start, 
                    out=WGF[:],
                    in_=wgate_t.rearrange("(k p) o -> p k o", p=128)[:, :, csl])
                dma(s_gw, g, g.dma_start, 
                    out=WUF[:],
                    in_=wup_t.rearrange("(k p) o -> p k o", p=128)[:, :, csl])
            for dsub in range(DC):
                wo = WOF0 if dsub % 2 == 0 else WOF1
                if h > 0 or dsub >= 2:
                    g.wait_ge(s_pe.h, pe_b(h, 2 * (dsub - 2) + 2) if dsub >= 2
                              else peF + 64 * h)
                dsl = slice(128 * dsub, 128 * (dsub + 1))
                dma(s_gw, g, g.dma_start, 
                    out=wo[:],
                    in_=wout_t.rearrange("(c p) d -> p c d", p=128)[:, :, dsl])
    gw_ffn_end = s_gw.n

    @blk.tensor
    def _(t):
        t.wait_ge(s_ev.h, ev_xn3)
        for h in range(2):
            gl = 0
            for grp in range(4):
                t.wait_ge(s_gw.h, gw_ab(h, grp))
                for ttl in range(2):
                    tt = 2 * h + ttl
                    sl = slice(512 * tt, 512 * (tt + 1))
                    for j in range(6):
                        if h > 0 or gl >= 2:
                            t.wait_ge(s_sc.h, sc_a(h, gl - 2) if gl >= 2
                                      else sc_a(h - 1, 46 + gl))
                            t.wait_ge(s_ev.h, ev_a(h, gl - 2) if gl >= 2
                                      else ev_b(h - 1, 15))
                        gbank = psW0 if gl % 2 == 0 else psX
                        ubank = psW1 if gl % 2 == 0 else psY5
                        for k in range(DC):
                            t.matmul(gbank[:, :], WGF[:, k, 128 * j:128 * (j + 1)],
                                     XSA[:, k, sl], start=(k == 0),
                                     stop=(k == DC - 1))
                        for k in range(DC):
                            mm = t.matmul(ubank[:, :],
                                          WUF[:, k, 128 * j:128 * (j + 1)],
                                          XSA[:, k, sl], start=(k == 0),
                                          stop=(k == DC - 1))
                        s_pe.inc(mm)
                        gl += 1
            bl = 0
            for dsub in range(DC):
                wo = WOF0 if dsub % 2 == 0 else WOF1
                t.wait_ge(s_gw.h, gw_wo(h, dsub))
                for ttl in range(2):
                    if bl == 0:
                        t.wait_ge(s_sc.h, sc_a(h, 47))
                        t.wait_ge(s_ev.h, ev_a(h, 47))
                    elif bl >= 2:
                        t.wait_ge(s_ev.h, ev_b(h, bl - 2))
                    pw = psW0 if bl % 2 == 0 else psW1
                    for oc in range(FC):
                        mm = t.matmul(pw[:, :], wo[:, oc, :],
                                      MT[:, oc, 512 * ttl:512 * (ttl + 1)],
                                      start=(oc == 0), stop=(oc == FC - 1))
                    s_pe.inc(mm)
                    bl += 1
    pe_ffn_end = s_pe.n

    @blk.scalar
    def _(sc):
        sc = SerEng(sc, s_sc)
        for h in range(2):
            for gl in range(48):
                sc.wait_ge(s_pe.h, pe_a(h, gl))
                if h > 0 or gl >= 2:
                    sc.wait_ge(s_ev.h, ev_a(h, gl - 2) if gl >= 2
                               else ev_a(h - 1, 46 + gl))
                gbank = psW0 if gl % 2 == 0 else psX
                sc.activation(out=TMPV[:, gl % 2, :], in_=gbank[:, :],
                              func=AF.Silu)
    sc_ffn_end = s_sc.n

    @blk.vector
    def _(v):
        v = SerEng(v, s_ev)
        for h in range(2):
            for gl in range(48):
                grp, ttl, j = gl // 12, (gl // 6) % 2, gl % 6
                oc = grp * 6 + j
                v.wait_ge(s_sc.h, sc_a(h, gl))
                ubank = psW1 if gl % 2 == 0 else psY5
                v.tensor_tensor(out=MT[:, oc, 512 * ttl:512 * (ttl + 1)],
                                in0=TMPV[:, gl % 2, :], in1=ubank[:, :],
                                op=ALU.mult)
            for bl in range(16):
                dsub, ttl = bl // 2, bl % 2
                tt = 2 * h + ttl
                v.wait_ge(s_pe.h, pe_b(h, bl))
                nbi = 16 * h + bl
                if nbi >= 2:
                    v.wait_ge(s_st.h, stF + 16 * (nbi - 1))
                pw = psW0 if bl % 2 == 0 else psW1
                v.tensor_add(ASTG[:, bl % 2, :], pw[:, :],
                                      XN[:, dsub, 512 * tt:512 * (tt + 1)])
    ev_ffn_end = s_ev.n

    @blk.sync
    def _(q):
        for h in range(2):
            for bl in range(16):
                dsub, ttl = bl // 2, bl % 2
                tt = 2 * h + ttl
                q.wait_ge(s_ev.h, ev_b(h, bl))
                dma(s_st, q, q.dma_start, 
                    out=attnfm[128 * dsub:128 * (dsub + 1),
                               512 * tt:512 * (tt + 1)],
                    in_=ASTG[:, bl % 2, :])
    st_attn = s_st.n

    if stage <= 3:
        return finish_partial()

    free("ASTG", "WOF1", "WOF0", "WUF", "WGF", "MT", "SR3", "SQ2", "TMPV",
         "XSA")
    free("RBC")
    LGA = sbuf("LGA", [128, 128 * E], F32)
    TK = sbuf("TK", [128, 6 * 128], F32)
    SC1 = sbuf("SC1", [128, 32 * E], F32)
    SC2 = sbuf("SC2", [128, 32 * E], F32)
    WPLN = sbuf("WPLN", [128, 256], F32)
    EPL = sbuf("EPL", [128, 256], F32)
    WPL = sbuf("WPL", [128, E, 256], F32)
    EQF = sbuf("EQF", [128, E, 256], F32)
    KEEPA = sbuf("KEEPA", [128, E, 256], F32)
    JK = sbuf("JK", [128, 256], F32)
    JK2 = sbuf("JK2", [128, 256], F32)
    CNTS = sbuf("CNTS", [128, 32], F32)
    CT32 = sbuf("CT32", [32, 128], F32)
    M32 = sbuf("M32", [32, 128], F32)
    BCT = sbuf("BCT", [128, 32], F32)
    RST = sbuf("RST", [32, 16], F32)
    TOTS = sbuf("TOTS", [128, E], F32)
    CARR = sbuf("CARR", [128, E], F32)
    KANY = sbuf("KANY", [128, 256], F32)
    PWp = sbuf("PWp", [128, 256], F32)
    PTp = sbuf("PTp", [128, 256], F32)
    WSUM = sbuf("WSUM", [128, 128], F32)
    SRCA = sbuf("SRCA", [16, 2048], F32)
    CMP1 = sbuf("CMP1", [16, 320], F32)
    CMP2 = sbuf("CMP2", [16, 320], F32)
    L1A = sbuf("L1A", [16, 1280], F32)
    L2A = sbuf("L2A", [16, 640], F32)
    L1B = sbuf("L1B", [16, 1280], F32)
    L2B = sbuf("L2B", [16, 640], F32)
    NF1 = sbuf("NF1", [1, 16], U32)
    NFF = sbuf("NFF", [1, 16], F32)
    NFB16 = sbuf("NFB16", [16, 16], F32)
    MK = sbuf("MK", [16, 320], F32)
    iotaslt = sbuf("iotaslt", [16, 320], F32)

    lo, hi, mid, wst = (RST[:, i:i + 1] for i in range(4))
    done, tot, cgt, ge8 = (RST[:, i:i + 1] for i in range(4, 8))
    m8, c28, ngt, quo = (RST[:, i:i + 1] for i in range(8, 12))
    sc8a, sc8b, sc8c, sc8d = (RST[:, i:i + 1] for i in range(12, 16))

    def transp(v, dst, src):
        for b in range(4):
            v.transpose(dst[32 * b:32 * (b + 1), 0:32],
                        src[0:32, 32 * b:32 * (b + 1)])

    def transp_c(v, dst, src):
        for b in range(4):
            v.transpose(dst[0:32, 32 * b:32 * (b + 1)],
                        src[32 * b:32 * (b + 1), 0:32])

    def colsum(v, dstcol, op=ALU.add):
        transp_c(v, CT32, CNTS)
        v.tensor_reduce(out=dstcol, in_=CT32[:, :], axis=AX, op=op)

    def bcast8(v, dstT, srccol):
        v.tensor_copy(out=M32[:, :], in_=srccol.broadcast_to([32, 128]))
        transp(v, dstT, M32)

    @blk.sync
    def _(q):
        q.wait_ge(s_cc.h, cc_lg)
        q.wait_ge(s_pe.h, pe_ffn_end)
        q.wait_ge(s_ev.h, ev_ffn_end)
        dma(s_st, q, q.dma_start, out=LGA[:],
                             in_=lgall.rearrange("(p j) e -> p (j e)", p=128))
        dma(s_st, q, q.dma_start, out=iotaslt[:], in_=c_iotasl[:, 0:320])
    st_lga = s_st.n

    m1 = TK[:, 0:128]
    m2 = TK[:, 128:256]
    e1 = TK[:, 256:384]
    e2 = TK[:, 384:512]
    w1 = TK[:, 512:640]
    w2 = TK[:, 640:768]

    @blk.vector
    def _(v):
        v = SerEng(v, s_ev)
        v.wait_ge(s_st.h, st_lga)
        for qd in range(4):
            jsl = slice(32 * qd, 32 * (qd + 1))
            LV = LGA.rearrange("p (j e) -> p j e", e=E)[:, jsl, :]
            A3 = SC1.rearrange("p (j e) -> p j e", e=E)
            B3 = SC2.rearrange("p (j e) -> p j e", e=E)
            io8 = iota8t[:].unsqueeze(1).broadcast_to([128, 32, E])
            v.tensor_reduce(out=m1[:, jsl], in_=LV, axis=AX, op=ALU.max)
            v.tensor_tensor(out=A3, in0=LV,
                            in1=m1[:, jsl].unsqueeze(2).broadcast_to([128, 32, E]),
                            op=ALU.is_equal)
            v.tensor_tensor(out=B3, in0=A3, in1=io8, op=ALU.mult)
            v.scalar_tensor_tensor(out=B3, in0=A3, scalar=-9.0, in1=B3,
                                   op0=ALU.mult, op1=ALU.add)
            v.tensor_scalar(out=B3, in0=B3, scalar1=9.0, scalar2=None, op0=ALU.add)
            v.tensor_reduce(out=e1[:, jsl], in_=B3, axis=AX, op=ALU.min)
            v.tensor_tensor(out=A3, in0=io8,
                            in1=e1[:, jsl].unsqueeze(2).broadcast_to([128, 32, E]),
                            op=ALU.is_equal)
            v.scalar_tensor_tensor(out=A3, in0=A3, scalar=-1e30, in1=LV,
                                   op0=ALU.mult, op1=ALU.add)
            v.tensor_reduce(out=m2[:, jsl], in_=A3, axis=AX, op=ALU.max)
            v.tensor_tensor(out=B3, in0=A3,
                            in1=m2[:, jsl].unsqueeze(2).broadcast_to([128, 32, E]),
                            op=ALU.is_equal)
            v.tensor_tensor(out=A3, in0=B3, in1=io8, op=ALU.mult)
            v.scalar_tensor_tensor(out=A3, in0=B3, scalar=-9.0, in1=A3,
                                   op0=ALU.mult, op1=ALU.add)
            v.tensor_scalar(out=A3, in0=A3, scalar1=9.0, scalar2=None, op0=ALU.add)
            v.tensor_reduce(out=e2[:, jsl], in_=A3, axis=AX, op=ALU.min)
        v.tensor_sub(w1[:, :], m1[:, :], m2[:, :])
    ev_top2a = s_ev.n

    @blk.scalar
    def _(sc):
        sc = SerEng(sc, s_sc)
        sc.wait_ge(s_ev.h, ev_top2a)
        sc.activation(out=w1[:, :], in_=w1[:, :], func=AF.Sigmoid)
    sc_w1 = s_sc.n

    if stage <= 3.1:
        return finish_partial()

    @blk.vector
    def _(v):
        v = SerEng(v, s_ev)
        v.wait_ge(s_sc.h, sc_w1)
        v.memset(CNTS[:, :], 0.0)
        v.memset(RST[:, :], 0.0)
        v.tensor_scalar(out=w2[:, :], in0=w1[:, :], scalar1=-1.0, scalar2=1.0,
                        op0=ALU.mult, op1=ALU.add)
        v.tensor_copy(out=WPLN[:, 0:256:2], in_=w1[:, :])
        v.tensor_copy(out=WPLN[:, 1:256:2], in_=w2[:, :])
        v.tensor_copy(out=EPL[:, 0:256:2], in_=e1[:, :])
        v.tensor_copy(out=EPL[:, 1:256:2], in_=e2[:, :])
        v.tensor_scalar(out=JK[:, :], in0=WPLN[:, :], scalar1=2.0, scalar2=None,
                        op0=ALU.add)
        for e in range(E):
            v.tensor_scalar(out=JK2[:, :], in0=EPL[:, :], scalar1=float(e),
                            scalar2=None, op0=ALU.is_equal)
            v.tensor_tensor(out=WPL[:, e, :], in0=JK[:, :], in1=JK2[:, :],
                            op=ALU.mult)
            v.tensor_scalar(out=WPL[:, e, :], in0=WPL[:, e, :], scalar1=-2.0,
                            scalar2=None, op0=ALU.add)
            v.tensor_scalar(out=JK2[:, :], in0=WPL[:, e, :], scalar1=-0.5,
                            scalar2=None, op0=ALU.is_gt, op1=ALU.add,
                            accum_out=CNTS[:, e:e + 1])
        colsum(v, tot)
        v.memset(lo, -1.0)
        v.memset(hi, 1.0)
        v.memset(wst, -1.0)
        v.tensor_scalar(out=done, in0=tot, scalar1=float(CAP) + 0.5,
                        scalar2=None, op0=ALU.is_lt)

        for it in range(VAL_ITERS):
            v.tensor_add(sc8a, lo, hi)
            v.tensor_scalar(out=mid, in0=sc8a, scalar1=0.5, scalar2=None,
                            op0=ALU.mult)
            bcast8(v, BCT, mid)
            for e in range(E):
                v.tensor_scalar(out=JK[:, :], in0=WPL[:, e, :],
                                scalar1=BCT[:, e:e + 1], scalar2=None,
                                op0=ALU.is_gt, op1=ALU.add,
                                accum_out=CNTS[:, e:e + 1])
            colsum(v, cgt)
            v.tensor_scalar(out=ge8, in0=cgt, scalar1=float(CAP) - 0.5,
                            scalar2=None, op0=ALU.is_gt)
            v.tensor_sub(sc8a, mid, lo)
            v.tensor_tensor(out=sc8a, in0=sc8a, in1=ge8, op=ALU.mult)
            v.tensor_add(lo, lo, sc8a)
            v.tensor_scalar(out=sc8b, in0=ge8, scalar1=-1.0, scalar2=1.0,
                            op0=ALU.mult, op1=ALU.add)
            v.tensor_sub(sc8a, mid, hi)
            v.tensor_tensor(out=sc8a, in0=sc8a, in1=sc8b, op=ALU.mult)
            v.tensor_add(hi, hi, sc8a)

        for it in range(SNAP_ITERS):
            bcast8(v, BCT, lo)
            for e in range(E):
                v.tensor_scalar(out=JK[:, :], in0=WPL[:, e, :],
                                scalar1=BCT[:, e:e + 1], scalar2=None,
                                op0=ALU.is_gt)
                v.tensor_scalar(out=JK2[:, :], in0=WPL[:, e, :], scalar1=-2.0,
                                scalar2=None, op0=ALU.add)
                v.tensor_tensor(out=JK2[:, :], in0=JK2[:, :], in1=JK[:, :],
                                op=ALU.mult)
                v.tensor_scalar(out=JK2[:, :], in0=JK2[:, :], scalar1=2.0,
                                scalar2=None, op0=ALU.add, op1=ALU.min,
                                accum_out=CNTS[:, e:e + 1])
            colsum(v, m8, op=ALU.min)
            bcast8(v, BCT, m8)
            for e in range(E):
                v.tensor_scalar(out=JK[:, :], in0=WPL[:, e, :],
                                scalar1=BCT[:, e:e + 1], scalar2=None,
                                op0=ALU.is_gt, op1=ALU.add,
                                accum_out=CNTS[:, e:e + 1])
            colsum(v, c28)
            v.tensor_scalar(out=ge8, in0=c28, scalar1=float(CAP) - 0.5,
                            scalar2=None, op0=ALU.is_gt)
            v.tensor_scalar(out=sc8b, in0=done, scalar1=-1.0, scalar2=1.0,
                            op0=ALU.mult, op1=ALU.add)
            v.tensor_scalar(out=sc8c, in0=ge8, scalar1=-1.0, scalar2=1.0,
                            op0=ALU.mult, op1=ALU.add)
            v.tensor_tensor(out=sc8c, in0=sc8c, in1=sc8b, op=ALU.mult)
            v.tensor_sub(sc8a, m8, wst)
            v.tensor_tensor(out=sc8a, in0=sc8a, in1=sc8c, op=ALU.mult)
            v.tensor_add(wst, wst, sc8a)
            v.tensor_add(done, done, sc8c)
            v.tensor_scalar(out=sc8b, in0=done, scalar1=-1.0, scalar2=1.0,
                            op0=ALU.mult, op1=ALU.add)
            v.tensor_tensor(out=sc8c, in0=ge8, in1=sc8b, op=ALU.mult)
            v.tensor_sub(sc8a, m8, lo)
            v.tensor_tensor(out=sc8a, in0=sc8a, in1=sc8c, op=ALU.mult)
            v.tensor_add(lo, lo, sc8a)

        v.tensor_scalar(out=sc8b, in0=done, scalar1=-1.0, scalar2=1.0,
                        op0=ALU.mult, op1=ALU.add)
        v.tensor_sub(sc8a, lo, wst)
        v.tensor_tensor(out=sc8a, in0=sc8a, in1=sc8b, op=ALU.mult)
        v.tensor_add(wst, wst, sc8a)

        bcast8(v, BCT, wst)
        for e in range(E):
            v.tensor_scalar(out=JK[:, :], in0=WPL[:, e, :],
                            scalar1=BCT[:, e:e + 1], scalar2=None,
                            op0=ALU.is_gt, op1=ALU.add,
                            accum_out=CNTS[:, e:e + 1])
            v.tensor_scalar(out=EQF[:, e, :], in0=WPL[:, e, :],
                            scalar1=BCT[:, e:e + 1], scalar2=None,
                            op0=ALU.is_equal)
        colsum(v, ngt)
        v.tensor_scalar(out=quo, in0=ngt, scalar1=-1.0, scalar2=float(CAP),
                        op0=ALU.mult, op1=ALU.add)
        for e in range(E):
            v.tensor_tensor_scan(out=KEEPA[:, e, :], data0=EQF[:, e, :],
                                 data1=zerocol[:].broadcast_to([128, 256]),
                                 initial=0.0, op0=ALU.add, op1=ALU.add)
            v.tensor_copy(out=TOTS[:, e:e + 1], in_=KEEPA[:, e, 255:256])
        v.engine_nop()
    ev_rt_pre = s_ev.n

    if stage <= 3.2:
        return finish_partial()

    @blk.tensor
    def _(t):
        t.wait_ge(s_ev.h, ev_rt_pre)
        s_pe.inc(t.matmul(psBC[:, 0:E], triut[:], TOTS[:, :],
                          start=True, stop=True))
    pe_carr = s_pe.n

    @blk.vector
    def _(v):
        v = SerEng(v, s_ev)
        v.wait_ge(s_pe.h, pe_carr)
        v.tensor_copy(out=CARR[:, :], in_=psBC[:, 0:E])
        bcast8(v, BCT, quo)
        for e in range(E):
            v.tensor_sub(JK[:, :], KEEPA[:, e, :], EQF[:, e, :])
            v.tensor_scalar(out=JK[:, :], in0=JK[:, :],
                            scalar1=CARR[:, e:e + 1], scalar2=None, op0=ALU.add)
            v.tensor_scalar(out=JK[:, :], in0=JK[:, :],
                            scalar1=BCT[:, e:e + 1], scalar2=None, op0=ALU.is_lt)
            v.tensor_tensor(out=KEEPA[:, e, :], in0=JK[:, :], in1=EQF[:, e, :],
                            op=ALU.mult)
        bcast8(v, BCT, wst)
        for e in range(E):
            v.tensor_scalar(out=JK[:, :], in0=WPL[:, e, :],
                            scalar1=BCT[:, e:e + 1], scalar2=None, op0=ALU.is_gt)
            v.tensor_add(KEEPA[:, e, :], KEEPA[:, e, :], JK[:, :])
        v.tensor_reduce(out=KANY[:, :],
                        in_=KEEPA.rearrange("p e j -> p j e"), axis=AX, op=ALU.add)
        v.tensor_tensor(out=PWp[:, :], in0=WPLN[:, :], in1=KANY[:, :], op=ALU.mult)
        v.tensor_add(WSUM[:, :], PWp[:, 0:256:2], PWp[:, 1:256:2])
        v.tensor_scalar(out=WSUM[:, :], in0=WSUM[:, :], scalar1=1.1920929e-07,
                        scalar2=None, op0=ALU.max)
        v.reciprocal(out=WSUM[:, :], in_=WSUM[:, :])
        v.tensor_tensor(out=PWp[:, 0:256:2], in0=PWp[:, 0:256:2], in1=WSUM[:, :],
                        op=ALU.mult)
        v.tensor_tensor(out=PWp[:, 1:256:2], in0=PWp[:, 1:256:2], in1=WSUM[:, :],
                        op=ALU.mult)
        v.tensor_scalar(out=JK[:, :], in0=EPL[:, :], scalar1=myexpt[:, 0:1],
                        scalar2=None, op0=ALU.is_equal)
        v.tensor_tensor(out=JK[:, :], in0=JK[:, :], in1=KANY[:, :], op=ALU.mult)
        v.tensor_tensor_scan(out=JK2[:, :], data0=JK[:, :],
                             data1=zerocol[:].broadcast_to([128, 256]),
                             initial=0.0, op0=ALU.add, op1=ALU.add)
        v.tensor_copy(out=TOTS[:, 0:1], in_=JK2[:, 255:256])
        v.engine_nop()
    ev_rt2 = s_ev.n

    @blk.tensor
    def _(t):
        t.wait_ge(s_ev.h, ev_rt2)
        s_pe.inc(t.matmul(psBC[:, 0:1], triut[:], TOTS[:, 0:1],
                          start=True, stop=True))
    pe_carr2 = s_pe.n

    @blk.vector
    def _(v):
        v = SerEng(v, s_ev)
        v.wait_ge(s_pe.h, pe_carr2)
        v.tensor_copy(out=CARR[:, 0:1], in_=psBC[:, 0:1])
        v.tensor_sub(JK2[:, :], JK2[:, :], JK[:, :])
        v.tensor_scalar(out=JK2[:, :], in0=JK2[:, :], scalar1=CARR[:, 0:1],
                        scalar2=None, op0=ALU.add)
        v.tensor_scalar(out=JK2[:, :], in0=JK2[:, :], scalar1=float(CAP) - 0.5,
                        scalar2=None, op0=ALU.is_lt)
        v.tensor_tensor(out=JK[:, :], in0=JK[:, :], in1=JK2[:, :], op=ALU.mult)
        v.tensor_tensor(out=PTp[:, :], in0=tokp1t[:, :], in1=JK[:, :], op=ALU.mult)
        v.tensor_scalar(out=PTp[:, :], in0=PTp[:, :], scalar1=-1.0, scalar2=None,
                        op0=ALU.add)
        v.tensor_scalar(out=PWp[:, :], in0=PWp[:, :], scalar1=1.0, scalar2=None,
                        op0=ALU.add)
        v.tensor_tensor(out=PWp[:, :], in0=PWp[:, :], in1=JK[:, :], op=ALU.mult)
        v.tensor_scalar(out=PWp[:, :], in0=PWp[:, :], scalar1=-1.0, scalar2=None,
                        op0=ALU.add)
        v.engine_nop()
    ev_planes = s_ev.n

    @blk.sync
    def _(q):
        q.wait_ge(s_ev.h, ev_planes)
        if "d_kany" in dbg:
            dma(s_st, q, q.dma_start, out=dbg["d_kany"], in_=KANY[:, :])
        if "d_thr" in dbg:
            dma(s_st, q, q.dma_start, out=dbg["d_thr"], in_=RST[:, :])
    st_pln = s_st.n

    if stage <= 3.3:
        return finish_partial()

    @blk.vector
    def _(v):
        v = SerEng(v, s_ev)
        v.memset(L1A[:, :], -1.0)
        v.memset(L2A[:, :], -1.0)
        v.memset(CMP1[:, :], -1.0)
        v.memset(L1B[:, :], -1.0)
        v.memset(L2B[:, :], -1.0)
        v.memset(CMP2[:, :], -1.0)
    ev_cmpset = s_ev.n

    gp0 = s_gp.n
    pe_tr0 = s_pe.n
    ev_tr0 = s_ev.n

    @blk.tensor
    def _(t):
        t.wait_ge(s_ev.h, ev_planes)
        for hi in range(16):
            if hi >= 1:
                t.wait_ge(s_ev.h, ev_tr0 + hi)
            ps_sl = psBC[0:16, 128 * (hi % 2):128 * (hi % 2) + 128]
            s_pe.inc(t.matmul(ps_sl, PTp[:, 16 * hi:16 * hi + 16],
                              identf[:], start=True, stop=True))

    @blk.vector
    def _(v):
        v = SerEng(v, s_ev)
        for hi in range(16):
            v.wait_ge(s_pe.h, pe_tr0 + hi + 1)
            ps_sl = psBC[0:16, 128 * (hi % 2):128 * (hi % 2) + 128]
            v.tensor_copy(
                out=SRCA[0:16, :].rearrange("r (p hi2) -> r p hi2", hi2=16)[:, :, hi],
                in_=ps_sl)
    assert s_ev.n == ev_tr0 + 16

    if stage <= 3.31:
        return finish_partial()

    @blk.gpsimd
    def _(g):
        g.wait_ge(s_ev.h, ev_tr0 + 16)
        g.wait_ge(s_ev.h, ev_cmpset)
        for c in range(4):
            g.load_library(libcfg.sparse_gather)
            s_gp.inc(g.sparse_gather(out=L1A[:, 320 * c:320 * (c + 1)],
                                     in_=SRCA[:, 512 * c:512 * (c + 1)],
                                     num_found=NF1[:, c:c + 1]))

    if stage <= 3.32:
        return finish_partial()

    def nf_group(gpw, k0, k1, tiles):
        @blk.vector
        def _(v):
            v = SerEng(v, s_ev)
            v.wait_ge(s_gp.h, gpw)
            v.tensor_copy(out=NFF[:, k0:k1], in_=NF1[:, k0:k1])
        ev_c = s_ev.n

        @blk.tensor
        def _(t):
            t.wait_ge(s_ev.h, ev_c)
            s_pe.inc(t.matmul(psBC[0:16, 496:496 + (k1 - k0)],
                              ones1t[0:1, 0:16], NFF[:, k0:k1],
                              start=True, stop=True))
        pe_c = s_pe.n

        @blk.vector
        def _(v):
            v = SerEng(v, s_ev)
            v.wait_ge(s_pe.h, pe_c)
            v.tensor_copy(out=NFB16[:, k0:k1], in_=psBC[0:16, 496:496 + (k1 - k0)])
            for j, tile in enumerate(tiles):
                v.tensor_scalar(out=MK[:, :], in0=iotaslt[:, :],
                                scalar1=NFB16[:, k0 + j:k0 + j + 1],
                                scalar2=None, op0=ALU.is_lt)
                v.scalar_tensor_tensor(out=tile, in0=tile, scalar=1.0,
                                       in1=MK[:, :], op0=ALU.add, op1=ALU.mult)
                v.tensor_scalar(out=tile, in0=tile, scalar1=-1.0, scalar2=None,
                                op0=ALU.add)
        return s_ev.n

    ev_sanA1 = nf_group(gp0 + 4, 0, 4,
                        [L1A[:, 320 * c:320 * (c + 1)] for c in range(4)])

    if "d_l1a" in dbg:
        @blk.sync
        def _(q):
            q.wait_ge(s_ev.h, ev_sanA1)
            dma(s_st, q, q.dma_start, out=dbg["d_l1a"][0:16, :], in_=L1A[:, :])
            dma(s_st, q, q.dma_start, out=dbg["d_l1a"][16:17, 0:16], in_=NFF[:, :])

    if stage <= 3.35:
        return finish_partial()

    pe_tr1 = s_pe.n
    ev_tr1 = s_ev.n

    @blk.tensor
    def _(t):
        t.wait_ge(s_gp.h, gp0 + 4)
        t.wait_ge(s_ev.h, ev_sanA1)
        for hi in range(16):
            if hi >= 1:
                t.wait_ge(s_ev.h, ev_tr1 + hi)
            ps_sl = psBC[0:16, 128 * (hi % 2):128 * (hi % 2) + 128]
            s_pe.inc(t.matmul(ps_sl, PWp[:, 16 * hi:16 * hi + 16],
                              identf[:], start=True, stop=True))

    @blk.vector
    def _(v):
        v = SerEng(v, s_ev)
        v.wait_ge(s_gp.h, gp0 + 4)
        for hi in range(16):
            v.wait_ge(s_pe.h, pe_tr1 + hi + 1)
            ps_sl = psBC[0:16, 128 * (hi % 2):128 * (hi % 2) + 128]
            v.tensor_copy(
                out=SRCA[0:16, :].rearrange("r (p hi2) -> r p hi2", hi2=16)[:, :, hi],
                in_=ps_sl)
    ev_tr1_end = s_ev.n

    @blk.gpsimd
    def _(g):
        g.wait_ge(s_ev.h, ev_sanA1)
        g.load_library(libcfg.sparse_gather)
        s_gp.inc(g.sparse_gather(out=L2A[:, 0:320], in_=L1A[:, 0:640],
                                 num_found=NF1[:, 4:5]))
        g.load_library(libcfg.sparse_gather)
        s_gp.inc(g.sparse_gather(out=L2A[:, 320:640], in_=L1A[:, 640:1280],
                                 num_found=NF1[:, 5:6]))

    ev_sanA2 = nf_group(gp0 + 6, 4, 6, [L2A[:, 0:320], L2A[:, 320:640]])

    @blk.gpsimd
    def _(g):
        g.wait_ge(s_ev.h, ev_sanA2)
        g.load_library(libcfg.sparse_gather)
        s_gp.inc(g.sparse_gather(out=CMP1[:, :], in_=L2A[:, :],
                                 num_found=NF1[:, 6:7]))

    ev_sanA3 = nf_group(gp0 + 7, 6, 7, [CMP1[:, :]])

    if stage <= 3.4:
        return finish_partial()

    @blk.gpsimd
    def _(g):
        g.wait_ge(s_ev.h, ev_tr1_end)
        for c in range(4):
            g.load_library(libcfg.sparse_gather)
            s_gp.inc(g.sparse_gather(out=L1B[:, 320 * c:320 * (c + 1)],
                                     in_=SRCA[:, 512 * c:512 * (c + 1)],
                                     num_found=NF1[:, 8 + c:9 + c]))

    ev_sanB1 = nf_group(gp0 + 11, 8, 12,
                        [L1B[:, 320 * c:320 * (c + 1)] for c in range(4)])

    @blk.gpsimd
    def _(g):
        g.wait_ge(s_ev.h, ev_sanB1)
        g.load_library(libcfg.sparse_gather)
        s_gp.inc(g.sparse_gather(out=L2B[:, 0:320], in_=L1B[:, 0:640],
                                 num_found=NF1[:, 12:13]))
        g.load_library(libcfg.sparse_gather)
        s_gp.inc(g.sparse_gather(out=L2B[:, 320:640], in_=L1B[:, 640:1280],
                                 num_found=NF1[:, 13:14]))

    ev_sanB2 = nf_group(gp0 + 13, 12, 14, [L2B[:, 0:320], L2B[:, 320:640]])

    @blk.gpsimd
    def _(g):
        g.wait_ge(s_ev.h, ev_sanB2)
        g.load_library(libcfg.sparse_gather)
        s_gp.inc(g.sparse_gather(out=CMP2[:, :], in_=L2B[:, :],
                                 num_found=NF1[:, 14:15]))
        g.load_library(libcfg.mlp)

    ev_sanB3 = nf_group(gp0 + 14, 14, 15, [CMP2[:, :]])

    @blk.vector
    def _(v):
        v = SerEng(v, s_ev)
        v.wait_ge(s_gp.h, gp0 + 14)
        v.memset(IG[:, :], 0)
        v.memset(ISC[:, :], 0)
        v.tensor_scalar(out=CMP1[:, :], in0=CMP1[:, :], scalar1=1.0,
                        scalar2=None, op0=ALU.add)
        v.tensor_scalar(out=CMP2[:, :], in0=CMP2[:, :], scalar1=0.0,
                        scalar2=None, op0=ALU.max)
        v.tensor_scalar(out=SRCA[:, 0:320], in0=CMP1[:, :], scalar1=-1.0,
                        scalar2=0.0, op0=ALU.add, op1=ALU.max)
        v.tensor_copy(out=IG[0:16, :], in_=SRCA[:, 0:320])
        v.tensor_copy(out=ISC[0:16, :], in_=CMP1[:, 0:320])
    ev_lists = s_ev.n

    if stage <= 3.5:
        return finish_partial()

    @blk.sync
    def _(q):
        q.wait_ge(s_ev.h, ev_lists)
        dma(s_st, q, q.dma_start, out=IG[16:32, :], in_=IG[0:16, :])
        dma(s_st, q, q.dma_start, out=ISC[16:32, :], in_=ISC[0:16, :])
        dma(s_st, q, q.dma_start, out=bncw.rearrange("o (r jj) -> r o jj", r=16),
            in_=CMP2[:, 0:320].unsqueeze(1))
        if "d_lists" in dbg:
            dma(s_st, q, q.dma_start, out=dbg["d_lists"][:, 0:320], in_=CMP1[:, :])
            dma(s_st, q, q.dma_start, out=dbg["d_lists"][:, 512:832], in_=CMP2[:, :])
    st_idx0 = s_st.n

    if stage <= 3.6:
        return finish_partial()

    @blk.sync
    def _(q):
        q.wait_ge(s_st.h, st_idx0)
        for hi in range(8):
            dma(s_st, q, q.dma_start, out=WTSL[16 * hi:16 * (hi + 1), :],
                in_=bncw.rearrange("o (r c2 hi2) -> r (o c2 hi2)", r=16, hi2=8)
                [:, hi::8][:, 0:40],
                allow_slow_non_contiguous=True)
        if "d_wtsl" in dbg:
            dma(s_st, q, q.dma_start, out=dbg["d_wtsl"], in_=WTSL[:, :])
    st_wtsl = s_st.n

    if stage <= 4:
        return finish_partial()

    free("iotaslt", "MK", "NFB16", "NFF", "NF1", "L2B", "L1B", "L2A", "L1A",
         "CMP2", "CMP1", "SRCA", "WSUM",
         "PTp", "PWp", "KANY", "CARR", "TOTS", "RST", "BCT", "M32", "CT32",
         "CNTS", "JK2", "JK", "KEEPA", "EQF", "WPL", "EPL", "WPLN", "SC2",
         "SC1", "TK", "LGA")
    free("XN")
    M2 = sbuf("M2", [128, FFC, 1024], BF16)
    GIN = sbuf("GIN", [128, 2, DC, 512], BF16)
    WPR = sbuf("WPR", [128, 2, DC, 512], BF16)
    WGA = sbuf("WGA", [128, 2, DC, 512], BF16)
    WPO = sbuf("WPO", [128, FFC, D], BF16)
    EOU = sbuf("EOU", [128, 4, D], F32)
    TMPX = sbuf("TMPX", [128, 2, 512], F32)

    CH5 = 5
    if stage <= 4.3:
        CH5 = 1
    elif stage <= 4.6:
        CH5 = 3

    gp5 = s_gp.n
    gd5 = s_gd.n
    gw5 = s_gw.n
    pe5 = s_pe.n
    sc5 = s_sc.n
    ev5 = s_ev.n

    def pe_g(c, gl):
        return pe5 + 80 * c + gl + 1

    def pe_p(c, bl):
        return pe5 + 80 * c + 64 + bl + 1

    def sc_g(c, gl):
        return sc5 + 64 * c + gl + 1

    def ev_g(c, gl):
        return ev5 + 80 * c + gl + 1

    def ev_p(c, bl):
        return ev5 + 80 * c + 64 + bl + 1

    def gw_grp(c, grp):
        return gw5 + (17 * c + 2 * (grp + 1)) * 16

    def gw_wpo(c):
        return gw5 + (17 * c + 17) * 16

    def gp_gath(c):
        return gd5 + 64 * c + 32

    def gp_scat(c, half):
        if c < 0:
            return gd5
        return gd5 + 64 * c + 32 + 16 * (half + 1)

    if stage <= 4.25:
        @blk.gpsimd
        def _(g):
            g.wait_ge(s_cc.h, cc_xn)
            g.wait_ge(s_st.h, st_wtsl)
            g.load_library(libcfg.mlp)
            if stage > 4.15:
                dma(s_gd, g, g.dma_gather,
                    out_ap=GIN[:, 0, :, :], in_ap=xnall[:],
                    idxs_ap=IG[:, 0:32], num_idxs=512,
                    num_idxs_reg=512, elem_size=D, transpose=True)
        return finish_partial()

    @blk.gpsimd
    def _(g):
        g.wait_ge(s_cc.h, cc_xn)
        g.wait_ge(s_st.h, st_wtsl)
        for c in range(CH5):
            if c >= 1:
                g.wait_ge(s_pe.h, pe5 + 80 * (c - 1) + 64)
            for gh in range(2):
                g.load_library(libcfg.mlp)
                dma(s_gd, g, g.dma_gather,
                    out_ap=GIN[:, gh, :, :], in_ap=xnall[:],
                    idxs_ap=IG[:, 64 * c + 32 * gh:64 * c + 32 * (gh + 1)],
                    num_idxs=512, num_idxs_reg=512, elem_size=D, transpose=True)
            for grp in range(8):
                if grp >= 2:
                    g.wait_ge(s_pe.h, pe5 + 80 * c + 8 * (grp - 1))
                elif c > 0:
                    g.wait_ge(s_pe.h, pe5 + 80 * (c - 1) + 8 * (6 + grp) + 8)
                fsl = slice(512 * grp, 512 * (grp + 1))
                dma(s_gw, g, g.dma_start, 
                    out=WPR[:, grp % 2, :, :],
                    in_=pre_t.rearrange("(k p) f -> p k f", p=128)[:, :, fsl])
                dma(s_gw, g, g.dma_start, 
                    out=WGA[:, grp % 2, :, :],
                    in_=gatew_t.rearrange("(k p) f -> p k f", p=128)[:, :, fsl])
            if c >= 1:
                g.wait_ge(s_pe.h, pe5 + 80 * (c - 1) + 80)
            dma(s_gw, g, g.dma_start, 
                out=WPO[:],
                in_=post_t.rearrange("(f p) d -> p f d", p=128))
            for half in range(2):
                g.wait_ge(s_ev.h, ev_p(c, 8 * (half + 1) - 1))
                g.load_library(libcfg.mlp)
                dma(s_gd, g, g.dma_scatter_add, 
                    out_ap=moeacc[:], in_ap=EOU[:],
                    idxs_ap=ISC[:, 64 * c + 32 * half:64 * c + 32 * (half + 1)],
                    num_idxs=512, num_idxs_reg=512, elem_size=D)

    @blk.tensor
    def _(t):
        for c in range(CH5):
            t.wait_ge(s_gd.h, gp_gath(c))
            for gl in range(64):
                ffc, tt = gl // 2, gl % 2
                grp, j = ffc // 4, ffc % 4
                t.wait_ge(s_gw.h, gw_grp(c, grp))
                gg = 64 * c + gl
                if gg >= 2:
                    t.wait_ge(s_sc.h, sc_g(c, gl - 2) if gl >= 2
                              else sc_g(c - 1, 62 + gl))
                    t.wait_ge(s_ev.h, ev_g(c, gl - 2) if gl >= 2
                              else ev_p(c - 1, 15))
                pre_b = psW0 if gl % 2 == 0 else psX
                gat_b = psW1 if gl % 2 == 0 else psY5
                sl = slice(512 * tt, 512 * (tt + 1))
                for k in range(DC):
                    t.matmul(pre_b[:, :],
                             WPR[:, grp % 2, k, 128 * j:128 * (j + 1)],
                             GIN[:, tt, k, :], start=(k == 0),
                             stop=(k == DC - 1))
                for k in range(DC):
                    mm = t.matmul(gat_b[:, :],
                                  WGA[:, grp % 2, k, 128 * j:128 * (j + 1)],
                                  GIN[:, tt, k, :], start=(k == 0),
                                  stop=(k == DC - 1))
                s_pe.inc(mm)
            t.wait_ge(s_gw.h, gw_wpo(c))
            for bl in range(16):
                half, scp, dtile = bl // 8, (bl // 2) % 4, bl % 2
                scg = half * 4 + scp
                bb = 16 * c + bl
                if bl >= 2:
                    t.wait_ge(s_ev.h, ev_p(c, bl - 2))
                else:
                    t.wait_ge(s_ev.h, ev_g(c, 63))
                pw = psW0 if bl % 2 == 0 else psW1
                for ffc in range(FFC):
                    mm = t.matmul(pw[:, :], M2[:, ffc, 128 * scg:128 * (scg + 1)],
                                  WPO[:, ffc, 512 * dtile:512 * (dtile + 1)],
                                  start=(ffc == 0), stop=(ffc == FFC - 1))
                s_pe.inc(mm)
    pe5_end = s_pe.n

    @blk.scalar
    def _(sc):
        sc = SerEng(sc, s_sc)
        for c in range(CH5):
            for gl in range(64):
                sc.wait_ge(s_pe.h, pe_g(c, gl))
                gg = 64 * c + gl
                if gg >= 2:
                    sc.wait_ge(s_ev.h, ev_g(c, gl - 2) if gl >= 2
                               else ev_g(c - 1, 62 + gl))
                gat_b = psW1 if gl % 2 == 0 else psY5
                sc.activation(out=TMPX[:, gl % 2, :], in_=gat_b[:, :],
                                       func=AF.Silu)

    @blk.vector
    def _(v):
        v = SerEng(v, s_ev)
        for c in range(CH5):
            for gl in range(64):
                ffc, tt = gl // 2, gl % 2
                v.wait_ge(s_sc.h, sc_g(c, gl))
                pre_b = psW0 if gl % 2 == 0 else psX
                sl = slice(512 * tt, 512 * (tt + 1))
                v.tensor_tensor(out=M2[:, ffc, sl], in0=pre_b[:, :],
                                         in1=TMPX[:, gl % 2, :], op=ALU.mult)
            for bl in range(16):
                half, scp, dtile = bl // 8, (bl // 2) % 4, bl % 2
                v.wait_ge(s_pe.h, pe_p(c, bl))
                if bl < 8:
                    v.wait_ge(s_gd.h, gp_scat(c - 1, 1))
                else:
                    v.wait_ge(s_gd.h, gp_scat(c, 0))
                pw = psW0 if bl % 2 == 0 else psW1
                col = 8 * c + 4 * half + scp
                v.tensor_scalar(
                    out=EOU[:, scp, 512 * dtile:512 * (dtile + 1)],
                    in0=pw[:, :], scalar1=WTSL[:, col:col + 1], scalar2=None,
                    op0=ALU.mult)
    ev5_end = s_ev.n

    if stage <= 4.9:
        return finish_partial()

    @blk.gpsimd
    def _(g):
        g.wait_ge(s_gd.h, gd5 + 64 * CH5)
        s_cc.inc(g.collective_compute(
            "ReduceScatter", ALU.add, replica_groups=RG,
            ins=[moeacc[1:T + 1, :]], outs=[moers[:]]))
    cc_rs = s_cc.n

    if stage <= 5:
        return finish_partial()

    free("TMPX", "EOU", "WPO", "WGA", "WPR", "GIN", "M2")
    XF = sbuf("XF", [128, 2, D], F32)
    MF = sbuf("MF", [128, 2, D], F32)
    AFB = sbuf("AFB", [128, 2, D], BF16)
    OF = sbuf("OF", [128, 16, D], F32)

    st6 = s_st.n
    ev6 = s_ev.n
    n_per_tb = 3 + (1 if "d_moe" in dbg else 0)

    @blk.sync
    def _(q):
        q.wait_ge(s_cc.h, cc_rs)
        for tb in range(16):
            r = tb % 2
            if tb >= 2:
                q.wait_ge(s_ev.h, ev6 + 2 * (tb - 1))
            sl = slice(128 * tb, 128 * (tb + 1))
            dma(s_st, q, q.dma_start, out=XF[:, r, :], in_=x_tm[sl, :])
            dma(s_st, q, q.dma_start, out=MF[:, r, :], in_=moers[sl, :])
            dma(s_st, q, q.dma_start, out=AFB[:, r, :], in_=attnfm[:, sl],
                                 transpose=True)
            if "d_moe" in dbg:
                dma(s_st, q, q.dma_start, out=dbg["d_moe"][sl, :], in_=moers[sl, :])

    @blk.vector
    def _(v):
        v = SerEng(v, s_ev)
        for tb in range(16):
            r = tb % 2
            v.wait_ge(s_st.h, st6 + 16 * n_per_tb * (tb + 1))
            v.tensor_add(OF[:, tb, :], XF[:, r, :], AFB[:, r, :])
            v.tensor_add(OF[:, tb, :], OF[:, tb, :], MF[:, r, :])
    ev6_end = s_ev.n

    @blk.sync
    def _(q):
        for tb in range(16):
            r = tb % 2
            q.wait_ge(s_ev.h, ev6 + 2 * (tb + 1))
            dma(s_st, q, q.dma_start, out=out_tm[128 * tb:128 * (tb + 1), :],
                                 in_=OF[:, tb, :])
        q.wait_ge(s_st.h, s_st.n)

    blk_g.__exit__(None, None, None)
    for n, g in reversed(guards):
        g.__exit__(None, None, None)
    return nc


def make_consts():
    ident = np.eye(128, dtype=ml_dtypes.bfloat16)
    ones = np.ones((128, 1), np.float32)
    ones1 = np.ones((1, 128), np.float32)
    iota8 = np.broadcast_to(np.arange(8, dtype=np.float32), (128, 8)).copy()
    f = np.arange(128)[:, None] * 256 + np.arange(256)[None, :]
    tokp1 = (f // 2 + 1).astype(np.float32)
    iotasl = (np.arange(2048)[None, :] * 16
              + np.arange(16)[:, None]).astype(np.float32)
    triu = (np.arange(128)[:, None] < np.arange(128)[None, :]).astype(np.float32)
    return ident, ones, ones1, iota8, tokp1, iotasl, triu


_BUILD_CACHE = {}


def kernel(L0_attn_x, L0_attn_rms_mix_w, L0_attn_rms_ffn_w, L0_attn_Wg,
           L0_attn_bg, L0_attn_Wv, L0_attn_bv, L0_attn_Wd, L0_attn_bd,
           L0_attn_W_gate, L0_attn_W_up, L0_attn_W_out, L0_ffn_router_weight,
           L0_ffn_ff_pre_act_weight, L0_ffn_gate_weight,
           L0_ffn_ff_post_act_weight, _debug=(), _return_debug=False):
    key = tuple(sorted(_debug))
    if key not in _BUILD_CACHE:
        nc = build(debug=_debug)
        lower_extended_insts(nc)
        _BUILD_CACHE[key] = nc
    nc = _BUILD_CACHE[key]

    x = np.ascontiguousarray(np.asarray(L0_attn_x, dtype=np.float32)
                             .reshape(T, D))
    ident, ones, ones1, iota8, tokp1, iotasl, triu = make_consts()

    def tr(w):
        return np.ascontiguousarray(np.asarray(w, np.float32).T)

    wgT, wvT, wdT = tr(L0_attn_Wg), tr(L0_attn_Wv), tr(L0_attn_Wd)
    wgateT, wupT = tr(L0_attn_W_gate), tr(L0_attn_W_up)
    woutT = tr(L0_attn_W_out)
    routerT = tr(L0_ffn_router_weight)
    b3 = np.stack([np.asarray(L0_attn_bg, np.float32),
                   np.asarray(L0_attn_bv, np.float32),
                   np.asarray(L0_attn_bd, np.float32)], 0)
    biases = np.ascontiguousarray(
        b3.reshape(3, DC, 128).transpose(2, 0, 1).reshape(128, 3 * DC))
    rw = np.stack([np.asarray(L0_attn_rms_mix_w, np.float32),
                   np.asarray(L0_attn_rms_ffn_w, np.float32)], 0)
    rmsw = np.ascontiguousarray(
        rw.reshape(2, DC, 128).transpose(2, 0, 1).reshape(128, 2 * DC))

    in_maps = []
    for c in range(8):
        xs = x[TL * c:TL * (c + 1)]
        carrysel = np.zeros((128, 8), np.float32)
        if c % 2 == 1:
            carrysel[:, c - 1] = 1.0
        in_maps.append({
            "x_fm": np.ascontiguousarray(xs.T),
            "x_tm": np.ascontiguousarray(xs),
            "wg_t": wgT, "wv_t": wvT, "wd_t": wdT,
            "biases": biases, "rmsw": rmsw,
            "wgate_t": wgateT, "wup_t": wupT, "wout_t": woutT,
            "router_t": routerT,
            "pre_t": tr(np.asarray(L0_ffn_ff_pre_act_weight)[c]),
            "gatew_t": tr(np.asarray(L0_ffn_gate_weight)[c]),
            "post_t": tr(np.asarray(L0_ffn_ff_post_act_weight)[c]),
            "c_ident": ident, "c_identf": np.eye(128, dtype=np.float32), "c_ones": ones, "c_ones1": ones1,
            "c_iota8": iota8, "c_tokp1": tokp1, "c_iotasl": iotasl,
            "c_myexp": np.full((128, 1), float(c), np.float32),
            "c_carrysel": carrysel, "c_triu": triu,
        })

    res = run_bass_kernel_spmd(nc, in_maps, core_ids=list(range(8)))
    out = np.concatenate([res.results[c]["out_tm"] for c in range(8)], axis=0)
    out = out.reshape(4, 4096, 1024).astype(np.float32)
    if _return_debug:
        return out, res
    return out


# version 03 (final)
# speedup vs baseline: 1.1266x; 1.0140x over previous
import numpy as np
import ml_dtypes
import concourse.bass as bass
import concourse.mybir as mybir
import concourse.library_config as libcfg
from concourse.library_overlay import lower_extended_insts
from concourse.bass_utils import run_bass_kernel_spmd

F32 = mybir.dt.float32
BF16 = mybir.dt.bfloat16
I16 = mybir.dt.int16
U32 = mybir.dt.uint32
AF = mybir.ActivationFunctionType
ALU = mybir.AluOpType
AX = mybir.AxisListType.X

D, FF, E, DFF, CAP = 1024, 3072, 8, 4096, 5120
T, TL = 16384, 2048
DC, FC, FFC = 8, 24, 32
A = 2 * T
RG = [list(range(8))]
VAL_ITERS = 30
SNAP_ITERS = 8


def build(debug=(), stage=6):
    nc = bass.Bass("TRN2", target_bir_lowering=False)
    dram = nc.dram_tensor

    x_fm = dram("x_fm", [D, TL], F32, kind="ExternalInput").ap()
    x_tm = dram("x_tm", [TL, D], F32, kind="ExternalInput").ap()
    wg_t = dram("wg_t", [D, D], F32, kind="ExternalInput").ap()
    wv_t = dram("wv_t", [D, D], F32, kind="ExternalInput").ap()
    wd_t = dram("wd_t", [D, D], F32, kind="ExternalInput").ap()
    biases = dram("biases", [128, 3 * DC], F32, kind="ExternalInput").ap()
    rmsw = dram("rmsw", [128, 2 * DC], F32, kind="ExternalInput").ap()
    wgate_t = dram("wgate_t", [D, FF], F32, kind="ExternalInput").ap()
    wup_t = dram("wup_t", [D, FF], F32, kind="ExternalInput").ap()
    wout_t = dram("wout_t", [FF, D], F32, kind="ExternalInput").ap()
    router_t = dram("router_t", [D, E], F32, kind="ExternalInput").ap()
    pre_t = dram("pre_t", [D, DFF], F32, kind="ExternalInput").ap()
    gatew_t = dram("gatew_t", [D, DFF], F32, kind="ExternalInput").ap()
    post_t = dram("post_t", [DFF, D], F32, kind="ExternalInput").ap()
    c_ident = dram("c_ident", [128, 128], BF16, kind="ExternalInput").ap()
    c_identf = dram("c_identf", [128, 128], F32, kind="ExternalInput").ap()
    c_ones = dram("c_ones", [128, 1], F32, kind="ExternalInput").ap()
    c_ones1 = dram("c_ones1", [1, 128], F32, kind="ExternalInput").ap()
    c_iota8 = dram("c_iota8", [128, 8], F32, kind="ExternalInput").ap()
    c_tokp1 = dram("c_tokp1", [128, 256], F32, kind="ExternalInput").ap()
    c_iotasl = dram("c_iotasl", [16, 2048], F32, kind="ExternalInput").ap()
    c_myexp = dram("c_myexp", [128, 1], F32, kind="ExternalInput").ap()
    c_carrysel = dram("c_carrysel", [128, 8], F32, kind="ExternalInput").ap()
    c_triu = dram("c_triu", [128, 128], F32, kind="ExternalInput").ap()

    out_tm = dram("out_tm", [TL, D], F32, kind="ExternalOutput").ap()
    dbg_specs = {
        "d_xn": ([128, DC * TL], BF16), "d_x1": ([128, DC * TL], BF16),
        "d_thr": ([32, 16], F32), "d_lists": ([16, 2 * 512], F32),
        "d_wtsl": ([128, 40], F32), "d_kany": ([128, 256], F32),
        "d_moe": ([TL, D], F32), "d_l1a": ([17, 1280], F32),
    }
    dbg = {}
    for name in debug:
        shp, dtp = dbg_specs[name]
        dbg[name] = dram(name, shp, dtp, kind="ExternalOutput").ap()

    xnloc = dram("xnloc", [TL, D], BF16, kind="Internal").ap()
    xnall = dram("xnall", [T, D], BF16, kind="Internal").ap()
    lgloc = dram("lgloc", [TL, E], F32, kind="Internal").ap()
    lgall = dram("lgall", [T, E], F32, kind="Internal").ap()
    carrloc = dram("carrloc", [1, D], F32, kind="Internal").ap()
    carrall = dram("carrall", [8, D], F32, kind="Internal").ap()
    moeacc = dram("moeacc", [T + 1, D], F32, kind="Internal").ap()
    moers = dram("moers", [TL, D], F32, kind="Internal").ap()
    attnfm = dram("attnfm", [D, TL], BF16, kind="Internal").ap()
    bnc1 = dram("bnc1", [1, A], F32, kind="Internal").ap()
    bnc2 = dram("bnc2", [1, A], F32, kind="Internal").ap()
    bncw = dram("bncw", [1, CAP], F32, kind="Internal").ap()

    guards = []

    def sbuf_r(name, shape, dtp):
        return sbuf(name, shape, dtp, side="right")

    def sbuf(name, shape, dtp, side="left"):
        g = nc.sbuf_tensor(name, shape, dtp, side=side)
        t = g.__enter__()
        guards.append((name, g))
        return t

    def psum(name, shape, dtp=F32):
        g = nc.psum_tensor(name, shape, dtp)
        t = g.__enter__()
        guards.append((name, g))
        return t

    def free(*names):
        for nm in names:
            for i, (n, g) in enumerate(guards):
                if n == nm:
                    g.__exit__(None, None, None)
                    guards.pop(i)
                    break
            else:
                raise KeyError(nm)

    identt = sbuf_r("identt", [128, 128], BF16)
    identf = sbuf_r("identf", [128, 128], F32)
    onest = sbuf_r("onest", [128, 1], F32)
    ones1t = sbuf_r("ones1t", [1, 128], F32)
    iota8t = sbuf_r("iota8t", [128, 8], F32)
    tokp1t = sbuf_r("tokp1t", [128, 256], F32)
    myexpt = sbuf_r("myexpt", [128, 1], F32)
    carryselt = sbuf_r("carryselt", [128, 8], F32)
    triut = sbuf_r("triut", [128, 128], F32)
    biast = sbuf_r("biast", [128, 3 * DC], F32)
    rmswt = sbuf_r("rmswt", [128, 2 * DC], F32)
    zerocol = sbuf_r("zerocol", [128, 1], F32)
    EPSC = sbuf_r("EPSC", [128, 2], F32)
    rtb = sbuf_r("rtb", [128, DC * E], BF16)
    HFIN = sbuf_r("HFIN", [128, DC], F32)
    HALL = sbuf_r("HALL", [128, 8, DC], F32)
    H0 = sbuf_r("H0", [128, DC], F32)
    IG = sbuf_r("IG", [128, 320], I16)
    ISC = sbuf_r("ISC", [128, 320], I16)
    WTSL = sbuf_r("WTSL", [128, 40], F32)
    XN = sbuf_r("XN", [128, DC, TL], BF16)

    class Ctr:
        def __init__(self, h):
            self.h = h
            self.n = 0

        def inc(self, instr, k=1):
            instr.then_inc(self.h, k)
            self.n += k
            return instr

    def sem(name):
        g = nc.semaphore(name)
        h = g.__enter__()
        guards.append((name, g))
        return Ctr(h)

    def dma(ctr, eng, fn, *a, **k):
        eng.wait_ge(ctr.h, ctr.n)
        return ctr.inc(fn(*a, **k), 16)

    class SerEng:

        def __init__(self, eng, ctr):
            self._e = eng
            self._c = ctr

        def wait_ge(self, h, v_):
            self._e.wait_ge(h, v_)

        def __getattr__(self, name):
            f = getattr(self._e, name)

            def g(*a, **k):
                self._e.wait_ge(self._c.h, self._c.n)
                return self._c.inc(f(*a, **k))
            return g

    s_w = sem("s_w")
    s_gw = sem("s_gw")
    s_pe = sem("s_pe")
    s_ev = sem("s_ev")
    s_sc = sem("s_sc")
    s_gp = sem("s_gp")
    s_gd = sem("s_gd")
    s_cc = sem("s_cc")
    s_st = sem("s_st")

    blk_g = nc.Block()
    blk = blk_g.__enter__()

    def finish_partial():
        @blk.sync
        def _(q):
            q.wait_ge(s_pe.h, s_pe.n)
            q.wait_ge(s_ev.h, s_ev.n)
            q.wait_ge(s_sc.h, s_sc.n)
            q.wait_ge(s_gp.h, s_gp.n)
            q.wait_ge(s_gd.h, s_gd.n)
            q.wait_ge(s_gw.h, s_gw.n)
            q.wait_ge(s_cc.h, s_cc.n)
            q.wait_ge(s_st.h, s_st.n)
            for tb in range(16):
                s_st.inc(q.dma_start(out=out_tm[128 * tb:128 * (tb + 1), :],
                                     in_=x_tm[128 * tb:128 * (tb + 1), :]), 16)
            q.wait_ge(s_st.h, s_st.n)
        blk_g.__exit__(None, None, None)
        for _n, _g in reversed(guards):
            _g.__exit__(None, None, None)
        return nc


    XT = sbuf("XT", [128, DC, TL], F32)
    XNT = sbuf("XNT", [128, D], BF16)
    LGV = sbuf("LGV", [128, 16 * E], F32)
    zeroz = sbuf("zeroz", [128, 4096], F32)
    SR = sbuf("SR", [1, 2 * TL], F32)
    SRX = sbuf("SRX", [1, TL], F32)
    X2T = sbuf("X2T", [128, DC, TL], F32)
    rA = SR[:, 0:TL]
    rB = SR[:, TL:2 * TL]
    rX = SRX[:, :]

    psA = psum("psA", [1, 512])
    psB = psum("psB", [1, 512])
    psBC = psum("psBC", [128, 512])
    psT = psum("psT", [128, 128], BF16)
    psL = psum("psL", [128, E])

    @blk.sync
    def _(q):
        for d, s in [(identt, c_ident), (identf, c_identf), (onest, c_ones), (ones1t, c_ones1),
                     (iota8t, c_iota8), (tokp1t, c_tokp1), (myexpt, c_myexp),
                     (carryselt, c_carrysel), (triut, c_triu),
                     (biast, biases), (rmswt, rmsw)]:
            s_w.inc(q.dma_start(out=d[:], in_=s[:]), 16)
        s_w.inc(q.dma_start(out=XT[:],
                            in_=x_fm.rearrange("(k p) t -> p k t", p=128)), 16)
        s_w.inc(q.dma_start(
            out=X2T[:, 0, 0:DC * E].rearrange("p (k e) -> p k e", e=E),
            in_=router_t.rearrange("(k p) e -> p k e", p=128)), 16)
    w_all = s_w.n

    @blk.vector
    def _(v):
        v = SerEng(v, s_ev)
        v.memset(zerocol[:], 0.0)
        v.memset(EPSC[:, 0:1], 1e-5)
        v.memset(EPSC[:, 1:2], 1e-6)
        v.memset(zeroz[:], 0.0)
    ev_zero = s_ev.n

    @blk.scalar
    def _(q):
        q.wait_ge(s_ev.h, ev_zero)
        for r0 in range(0, T, 512):
            dma(s_st, q, q.dma_start, 
                out=moeacc[r0:r0 + 512, :].rearrange("(a b) d -> a (b d)", b=4),
                in_=zeroz[:, :])
        dma(s_st, q, q.dma_start, out=moeacc[T:T + 1, :], in_=zeroz[0:1, 0:D])
        q.wait_ge(s_w.h, w_all)
        s_sc.inc(q.copy(out=rtb[:], in_=X2T[:, 0, 0:DC * E]))
    st_zero = s_st.n
    sc_rtb = s_sc.n

    @blk.scalar
    def _(sc):
        sc = SerEng(sc, s_sc)
        sc.wait_ge(s_sc.h, sc_rtb)
        for k in range(DC):
            sc.activation(out=X2T[:, k, :], in_=XT[:, k, :],
                                   func=AF.Square)
    sc_sq = s_sc.n

    pe_stats0 = s_pe.n

    @blk.tensor
    def _(t):
        t.wait_ge(s_sc.h, sc_sq)
        for tt in range(4):
            sl = slice(512 * tt, 512 * (tt + 1))
            if tt > 0:
                t.wait_ge(s_ev.h, ev_zero + 2 * tt)
            for k in range(DC):
                t.matmul(psA[:, :], onest[:], XT[:, k, sl],
                         start=(k == 0), stop=(k == DC - 1))
            for k in range(DC):
                mm = t.matmul(psB[:, :], onest[:], X2T[:, k, sl],
                              start=(k == 0), stop=(k == DC - 1))
            s_pe.inc(mm)

    @blk.vector
    def _(v):
        v = SerEng(v, s_ev)
        for tt in range(4):
            v.wait_ge(s_pe.h, pe_stats0 + tt + 1)
            sl = slice(512 * tt, 512 * (tt + 1))
            v.tensor_scalar(out=rA[:, sl], in0=psA[:, :],
                                     scalar1=1.0 / D, scalar2=None, op0=ALU.mult)
            v.tensor_scalar(out=rB[:, sl], in0=psB[:, :],
                                     scalar1=1.0 / D, scalar2=None, op0=ALU.mult)
        v.tensor_tensor(out=rX, in0=rA, in1=rA, op=ALU.mult)
        v.tensor_sub(rB, rB, rX)
    ev_var = s_ev.n

    @blk.scalar
    def _(sc):
        sc = SerEng(sc, s_sc)
        sc.wait_ge(s_ev.h, ev_var)
        sc.activation(out=rX, in_=rB, func=AF.Sqrt, bias=EPSC[0:1, 0:1])
    sc_sd = s_sc.n

    @blk.vector
    def _(v):
        v = SerEng(v, s_ev)
        v.wait_ge(s_sc.h, sc_sd)
        v.reciprocal(out=rX, in_=rX)
        v.tensor_tensor(out=rB, in0=rB, in1=rX, op=ALU.mult)
        v.tensor_tensor(out=rB, in0=rB, in1=rX, op=ALU.mult)
    ev_mx2 = s_ev.n

    @blk.scalar
    def _(sc):
        sc = SerEng(sc, s_sc)
        sc.wait_ge(s_ev.h, ev_mx2)
        sc.activation(out=rB, in_=rB, func=AF.Sqrt, bias=EPSC[0:1, 1:2])
    sc_s2 = s_sc.n

    @blk.vector
    def _(v):
        v = SerEng(v, s_ev)
        v.wait_ge(s_sc.h, sc_s2)
        v.reciprocal(out=rB, in_=rB)
    ev_s2 = s_ev.n

    free("X2T")
    RBC = sbuf_r("RBC", [128, TL], F32)
    SBC = sbuf_r("SBC", [128, TL], F32)
    MBC = sbuf_r("MBC", [128, TL], F32)

    pe_bc0 = s_pe.n

    @blk.tensor
    def _(t):
        t.wait_ge(s_ev.h, ev_s2)
        for i in range(12):
            row = [rA, rX, rB][i // 4]
            tt = i % 4
            if i >= 1:
                t.wait_ge(s_ev.h, ev_s2 + i)
            s_pe.inc(t.matmul(psBC[:, :], ones1t[:],
                              row[:, 512 * tt:512 * (tt + 1)],
                              start=True, stop=True))

    @blk.vector
    def _(v):
        v = SerEng(v, s_ev)
        for i in range(12):
            dst = [MBC, RBC, SBC][i // 4]
            tt = i % 4
            v.wait_ge(s_pe.h, pe_bc0 + i + 1)
            if i < 4:
                v.tensor_scalar(out=dst[:, 512 * tt:512 * (tt + 1)],
                                         in0=psBC[:, :], scalar1=-1.0,
                                         scalar2=None, op0=ALU.mult)
            else:
                v.tensor_copy(out=dst[:, 512 * tt:512 * (tt + 1)],
                                       in_=psBC[:, :])
    ev_bc = s_ev.n

    @blk.vector
    def _(v):
        v = SerEng(v, s_ev)
        for k in range(DC):
            v.tensor_add(XN[:, k, :], XT[:, k, :], MBC[:, :])
            v.tensor_tensor(out=XN[:, k, :], in0=XN[:, k, :],
                                     in1=RBC[:, :], op=ALU.mult)
    ev_xn = s_ev.n

    pe_tb0 = s_pe.n
    ev_tb0 = s_ev.n

    @blk.tensor
    def _(t):
        t.wait_ge(s_ev.h, ev_xn)
        for tb in range(16):
            sl = slice(128 * tb, 128 * (tb + 1))
            if tb > 0:
                t.wait_ge(s_st.h, st_zero + 16 * tb)
            for k in range(DC):
                if tb > 0 or k >= 1:
                    t.wait_ge(s_ev.h, ev_tb0 + tb * (DC + 1) + k)
                s_pe.inc(t.transpose(psT[:, :], XN[:, k, sl], identt[:]))
            for k in range(DC):
                mm = t.matmul(psL[:, :], XN[:, k, sl], rtb[:, E * k:E * (k + 1)],
                              start=(k == 0), stop=(k == DC - 1))
            s_pe.inc(mm)

    @blk.vector
    def _(v):
        v = SerEng(v, s_ev)
        for tb in range(16):
            base = pe_tb0 + tb * (DC + 1)
            for k in range(DC):
                v.wait_ge(s_pe.h, base + k + 1)
                v.tensor_copy(out=XNT[:, 128 * k:128 * (k + 1)],
                                       in_=psT[:, :])
            v.wait_ge(s_pe.h, base + DC + 1)
            v.tensor_copy(out=LGV[:, E * tb:E * (tb + 1)], in_=psL[:, :])
    ev_tb_end = s_ev.n

    @blk.sync
    def _(q):
        for tb in range(16):
            q.wait_ge(s_ev.h, ev_tb0 + tb * (DC + 1) + DC)
            dma(s_st, q, q.dma_start, out=xnloc[128 * tb:128 * (tb + 1), :],
                                 in_=XNT[:, :])
        q.wait_ge(s_ev.h, ev_tb_end)
        dma(s_st, q, q.dma_start, out=lgloc.rearrange("(c p) e -> p c e", p=128),
                             in_=LGV[:, :].rearrange("p (c e) -> p c e", e=E))
        if "d_xn" in dbg:
            dma(s_st, q, q.dma_start, out=dbg["d_xn"],
                                 in_=XN.rearrange("p k t -> p (k t)"))
    st_xn = s_st.n

    @blk.gpsimd
    def _(g):
        g.wait_ge(s_st.h, st_xn)
        s_cc.inc(g.collective_compute("AllGather", ALU.bypass, replica_groups=RG,
                                      ins=[lgloc[:]], outs=[lgall[:]]))
        s_cc.inc(g.collective_compute("AllGather", ALU.bypass, replica_groups=RG,
                                      ins=[xnloc[:]], outs=[xnall[:]]))
    cc_lg, cc_xn = 1, 2

    if stage <= 1:
        return finish_partial()

    free("SRX", "SR", "zeroz", "LGV", "XNT", "XT")
    free("psL", "psT")
    free("MBC")
    XSA = sbuf("XSA", [128, DC, TL], BF16)
    TMPV = sbuf("TMPV", [128, 2, 512], F32)
    ATT = sbuf("ATT", [128, DC, TL], BF16)
    XN2 = sbuf("XN2", [128, DC, TL], BF16)
    WMX0 = sbuf("WMX0", [128, DC, D], BF16)
    WMX1 = sbuf("WMX1", [128, DC, D], BF16)
    psW0 = psum("psW0", [128, 512])
    psW1 = psum("psW1", [128, 512])

    @blk.vector
    def _(v):
        v = SerEng(v, s_ev)
        v.wait_ge(s_st.h, st_xn)
        v.wait_ge(s_pe.h, pe_tb0 + 16 * (DC + 1))
        v.wait_ge(s_sc.h, sc_sq)
        for k in range(DC):
            v.tensor_scalar(out=ATT[:, k, :], in0=XN[:, k, :],
                            scalar1=rmswt[:, k:k + 1], scalar2=None, op0=ALU.mult)
            v.tensor_tensor(out=XN2[:, k, :], in0=ATT[:, k, :],
                                     in1=SBC[:, :], op=ALU.mult)
    ev_xn2 = s_ev.n

    gw_mix0 = s_gw.n

    @blk.gpsimd
    def _(g):
        g.wait_ge(s_st.h, st_xn)
        g.wait_ge(s_pe.h, pe_tb0 + 16 * (DC + 1))
        g.wait_ge(s_sc.h, sc_sq)
        g.wait_ge(s_ev.h, ev_xn)
        dma(s_gw, g, g.dma_start, out=WMX0[:],
                             in_=wg_t.rearrange("(k p) o -> p k o", p=128))
        dma(s_gw, g, g.dma_start, out=WMX1[:],
                             in_=wv_t.rearrange("(k p) o -> p k o", p=128))

    pe_mix0 = s_pe.n
    sc_mix0 = s_sc.n
    ev_mix0 = s_ev.n

    @blk.tensor
    def _(t):
        t.wait_ge(s_ev.h, ev_xn2)
        for gi in range(96):
            wi, oc, tt = gi // 32, (gi // 4) % 8, gi % 4
            buf = [WMX0, WMX1, WMX0][wi]
            if gi % 32 == 0:
                t.wait_ge(s_gw.h, gw_mix0 + 16 * (wi + 1))
            if gi >= 2:
                t.wait_ge(s_sc.h, sc_mix0 + gi - 1)
            pw = psW0 if gi % 2 == 0 else psW1
            sl = slice(512 * tt, 512 * (tt + 1))
            for k in range(DC):
                mm = t.matmul(pw[:, :], buf[:, k, 128 * oc:128 * (oc + 1)],
                              XN2[:, k, sl], start=(k == 0), stop=(k == DC - 1))
            s_pe.inc(mm)
    pe_mix_end = s_pe.n

    @blk.gpsimd
    def _(g):
        g.wait_ge(s_pe.h, pe_mix0 + 32)
        dma(s_gw, g, g.dma_start, out=WMX0[:],
                             in_=wd_t.rearrange("(k p) o -> p k o", p=128))
    gw_mix_end = s_gw.n

    @blk.scalar
    def _(sc):
        sc = SerEng(sc, s_sc)
        for gi in range(96):
            wi, oc, tt = gi // 32, (gi // 4) % 8, gi % 4
            sc.wait_ge(s_pe.h, pe_mix0 + gi + 1)
            pw = psW0 if gi % 2 == 0 else psW1
            sl = slice(512 * tt, 512 * (tt + 1))
            bias = biast[:, wi * DC + oc:wi * DC + oc + 1]
            if wi == 0:
                sc.activation(out=XSA[:, oc, sl], in_=pw[:, :],
                                       func=AF.Sigmoid, bias=bias)
            else:
                if gi >= 34:
                    sc.wait_ge(s_ev.h, ev_mix0 + gi - 33)
                fn = AF.Tanh if wi == 1 else AF.Sigmoid
                sc.activation(out=TMPV[:, gi % 2, :], in_=pw[:, :],
                                       func=fn, bias=bias)
    sc_mix_end = s_sc.n

    @blk.vector
    def _(v):
        v = SerEng(v, s_ev)
        for gi in range(32, 96):
            wi, oc, tt = gi // 32, (gi // 4) % 8, gi % 4
            sl = slice(512 * tt, 512 * (tt + 1))
            v.wait_ge(s_sc.h, sc_mix0 + gi + 1)
            if wi == 1:
                v.tensor_tensor(out=XSA[:, oc, sl], in0=XSA[:, oc, sl],
                                         in1=TMPV[:, gi % 2, :], op=ALU.mult)
            else:
                v.tensor_scalar(out=ATT[:, oc, sl],
                                         in0=TMPV[:, gi % 2, :], scalar1=0.998,
                                         scalar2=0.001, op0=ALU.mult, op1=ALU.add)

    @blk.vector
    def _(v):
        v = SerEng(v, s_ev)
        for k in range(DC):
            v.tensor_tensor_scan(out=XSA[:, k, :], data0=ATT[:, k, :],
                                 data1=XSA[:, k, :], initial=0.0,
                                 op0=ALU.mult, op1=ALU.add)
            v.tensor_tensor_scan(out=ATT[:, k, :], data0=ATT[:, k, :],
                                 data1=zerocol[:].broadcast_to([128, TL]),
                                 initial=1.0, op0=ALU.mult, op1=ALU.add)
            v.tensor_copy(out=HFIN[:, k:k + 1], in_=XSA[:, k, TL - 1:TL])
        v.engine_nop()
    ev_scan = s_ev.n

    @blk.sync
    def _(q):
        q.wait_ge(s_ev.h, ev_scan)
        dma(s_st, q, q.dma_start, out=carrloc.rearrange("o (k p) -> p o k", p=128),
                             in_=HFIN[:, :].unsqueeze(1),
                             allow_slow_non_contiguous=True)
    st_carr = s_st.n

    @blk.gpsimd
    def _(g):
        g.wait_ge(s_st.h, st_carr)
        s_cc.inc(g.collective_compute("AllGather", ALU.bypass, replica_groups=RG,
                                      ins=[carrloc[:]], outs=[carrall[:]]))
    cc_carr = s_cc.n

    @blk.sync
    def _(q):
        q.wait_ge(s_cc.h, cc_carr)
        dma(s_st, q, q.dma_start, out=HALL[:],
                             in_=carrall.rearrange("r (k p) -> p r k", p=128),
                             allow_slow_non_contiguous=True)
    st_hall = s_st.n

    @blk.vector
    def _(v):
        v = SerEng(v, s_ev)
        v.wait_ge(s_st.h, st_hall)
        v.tensor_scalar(out=H0[:, :], in0=HALL[:, 0, :],
                        scalar1=carryselt[:, 0:1], scalar2=None, op0=ALU.mult)
        for r in range(1, 8):
            v.scalar_tensor_tensor(out=H0[:, :], in0=HALL[:, r, :],
                                   scalar=carryselt[:, r:r + 1], in1=H0[:, :],
                                   op0=ALU.mult, op1=ALU.add)
        for k in range(DC):
            v.scalar_tensor_tensor(out=XSA[:, k, :], in0=ATT[:, k, :],
                                   scalar=H0[:, k:k + 1], in1=XSA[:, k, :],
                                   op0=ALU.mult, op1=ALU.add)
            v.tensor_add(XN[:, k, :], XN[:, k, :], XSA[:, k, :])
    ev_x1 = s_ev.n

    def ev_x1k(k):
        return ev_x1 - 2 * DC + 2 * (k + 1)

    @blk.sync
    def _(q):
        if "d_x1" in dbg:
            q.wait_ge(s_ev.h, ev_x1)
            dma(s_st, q, q.dma_start, out=dbg["d_x1"],
                                 in_=XN.rearrange("p k t -> p (k t)"))

    if stage <= 2:
        return finish_partial()

    free("WMX1", "WMX0", "XN2", "ATT")
    free("SBC")
    SQ2 = sbuf("SQ2", [128, TL], F32)
    SR3 = sbuf("SR3", [1, TL], F32)
    r3 = SR3[:, :]
    MT = sbuf("MT", [128, FC, 1024], BF16)
    psX = psum("psX", [128, 512])
    psY5 = psum("psY5", [128, 512])
    WGF = sbuf("WGF", [128, DC, 6 * 128], BF16)
    WUF = sbuf("WUF", [128, DC, 6 * 128], BF16)
    WOF0 = sbuf("WOF0", [128, FC, 128], BF16)
    WOF1 = sbuf("WOF1", [128, FC, 128], BF16)
    ASTG = sbuf("ASTG", [128, 2, 512], BF16)

    pe_st2_0 = s_pe.n
    sc_st2_0 = s_sc.n

    @blk.scalar
    def _(sc):
        sc = SerEng(sc, s_sc)
        sc.wait_ge(s_ev.h, ev_x1)
        for k in range(DC):
            if k >= 1:
                sc.wait_ge(s_pe.h, pe_st2_0 + k)
            sc.activation(out=SQ2[:, :], in_=XN[:, k, :], func=AF.Square)

    @blk.tensor
    def _(t):
        for k in range(DC):
            t.wait_ge(s_sc.h, sc_st2_0 + k + 1)
            for tt in range(4):
                pw = [psA, psB, psW0, psW1][tt]
                mm = t.matmul(pw[0:1, 0:512], onest[:],
                              SQ2[:, 512 * tt:512 * (tt + 1)],
                              start=(k == 0), stop=(k == DC - 1))
            s_pe.inc(mm)
    pe_st2_end = s_pe.n

    @blk.vector
    def _(v):
        v = SerEng(v, s_ev)
        v.wait_ge(s_pe.h, pe_st2_end)
        for tt in range(4):
            pw = [psA, psB, psW0, psW1][tt]
            v.tensor_scalar(out=r3[:, 512 * tt:512 * (tt + 1)], in0=pw[0:1, 0:512],
                            scalar1=1.0 / D, scalar2=None, op0=ALU.mult)
        v.engine_nop()
    ev_st2 = s_ev.n

    @blk.scalar
    def _(sc):
        sc = SerEng(sc, s_sc)
        sc.wait_ge(s_ev.h, ev_st2)
        sc.activation(out=r3, in_=r3, func=AF.Sqrt, bias=EPSC[0:1, 1:2])
    sc_r3 = s_sc.n

    @blk.vector
    def _(v):
        v = SerEng(v, s_ev)
        v.wait_ge(s_sc.h, sc_r3)
        v.reciprocal(out=r3, in_=r3)
    ev_r3 = s_ev.n

    pe_r3b0 = s_pe.n

    @blk.tensor
    def _(t):
        t.wait_ge(s_ev.h, ev_r3)
        for tt in range(4):
            if tt >= 1:
                t.wait_ge(s_ev.h, ev_r3 + tt)
            s_pe.inc(t.matmul(psBC[:, :], ones1t[:],
                              r3[:, 512 * tt:512 * (tt + 1)],
                              start=True, stop=True))

    @blk.vector
    def _(v):
        v = SerEng(v, s_ev)
        for tt in range(4):
            v.wait_ge(s_pe.h, pe_r3b0 + tt + 1)
            v.tensor_copy(out=RBC[:, 512 * tt:512 * (tt + 1)],
                                   in_=psBC[:, :])
        for k in range(DC):
            v.tensor_scalar(out=SQ2[:, :], in0=XN[:, k, :],
                            scalar1=rmswt[:, DC + k:DC + k + 1], scalar2=None,
                            op0=ALU.mult)
            v.tensor_tensor(out=XSA[:, k, :], in0=SQ2[:, :],
                                     in1=RBC[:, :], op=ALU.mult)
    ev_xn3 = s_ev.n

    gwF = s_gw.n
    peF = s_pe.n
    scF = s_sc.n
    evF = s_ev.n
    stF = s_st.n

    def gw_ab(h, grp):
        return gwF + (16 * h + 2 * (grp + 1)) * 16

    def gw_wo(h, dsub):
        return gwF + (16 * h + 8 + dsub + 1) * 16

    def pe_a(h, gl):
        return peF + 64 * h + gl + 1

    def pe_b(h, bl):
        return peF + 64 * h + 48 + bl + 1

    def sc_a(h, gl):
        return scF + 48 * h + gl + 1

    def ev_a(h, gl):
        return evF + 64 * h + gl + 1

    def ev_b(h, bl):
        return evF + 64 * h + 48 + bl + 1

    @blk.gpsimd
    def _(g):
        g.wait_ge(s_pe.h, pe_mix_end)
        g.wait_ge(s_ev.h, ev_x1)
        for h in range(2):
            for grp in range(4):
                if h > 0 or grp >= 1:
                    g.wait_ge(s_pe.h, peF + 64 * h + 12 * grp if grp >= 1
                              else peF + 64 * (h - 1) + 48)
                csl = slice(6 * 128 * grp, 6 * 128 * (grp + 1))
                dma(s_gw, g, g.dma_start, 
                    out=WGF[:],
                    in_=wgate_t.rearrange("(k p) o -> p k o", p=128)[:, :, csl])
                dma(s_gw, g, g.dma_start, 
                    out=WUF[:],
                    in_=wup_t.rearrange("(k p) o -> p k o", p=128)[:, :, csl])
            for dsub in range(DC):
                wo = WOF0 if dsub % 2 == 0 else WOF1
                if h > 0 or dsub >= 2:
                    g.wait_ge(s_pe.h, pe_b(h, 2 * (dsub - 2) + 2) if dsub >= 2
                              else peF + 64 * h)
                dsl = slice(128 * dsub, 128 * (dsub + 1))
                dma(s_gw, g, g.dma_start, 
                    out=wo[:],
                    in_=wout_t.rearrange("(c p) d -> p c d", p=128)[:, :, dsl])
    gw_ffn_end = s_gw.n

    @blk.tensor
    def _(t):
        t.wait_ge(s_ev.h, ev_xn3)
        for h in range(2):
            gl = 0
            for grp in range(4):
                t.wait_ge(s_gw.h, gw_ab(h, grp))
                for ttl in range(2):
                    tt = 2 * h + ttl
                    sl = slice(512 * tt, 512 * (tt + 1))
                    for j in range(6):
                        if h > 0 or gl >= 2:
                            t.wait_ge(s_sc.h, sc_a(h, gl - 2) if gl >= 2
                                      else sc_a(h - 1, 46 + gl))
                            t.wait_ge(s_ev.h, ev_a(h, gl - 2) if gl >= 2
                                      else ev_b(h - 1, 15))
                        gbank = psW0 if gl % 2 == 0 else psX
                        ubank = psW1 if gl % 2 == 0 else psY5
                        for k in range(DC):
                            t.matmul(gbank[:, :], WGF[:, k, 128 * j:128 * (j + 1)],
                                     XSA[:, k, sl], start=(k == 0),
                                     stop=(k == DC - 1))
                        for k in range(DC):
                            mm = t.matmul(ubank[:, :],
                                          WUF[:, k, 128 * j:128 * (j + 1)],
                                          XSA[:, k, sl], start=(k == 0),
                                          stop=(k == DC - 1))
                        s_pe.inc(mm)
                        gl += 1
            bl = 0
            for dsub in range(DC):
                wo = WOF0 if dsub % 2 == 0 else WOF1
                t.wait_ge(s_gw.h, gw_wo(h, dsub))
                for ttl in range(2):
                    if bl == 0:
                        t.wait_ge(s_sc.h, sc_a(h, 47))
                        t.wait_ge(s_ev.h, ev_a(h, 47))
                    elif bl >= 2:
                        t.wait_ge(s_ev.h, ev_b(h, bl - 2))
                    pw = psW0 if bl % 2 == 0 else psW1
                    for oc in range(FC):
                        mm = t.matmul(pw[:, :], wo[:, oc, :],
                                      MT[:, oc, 512 * ttl:512 * (ttl + 1)],
                                      start=(oc == 0), stop=(oc == FC - 1))
                    s_pe.inc(mm)
                    bl += 1
    pe_ffn_end = s_pe.n

    @blk.scalar
    def _(sc):
        sc = SerEng(sc, s_sc)
        for h in range(2):
            for gl in range(48):
                sc.wait_ge(s_pe.h, pe_a(h, gl))
                if h > 0 or gl >= 2:
                    sc.wait_ge(s_ev.h, ev_a(h, gl - 2) if gl >= 2
                               else ev_a(h - 1, 46 + gl))
                gbank = psW0 if gl % 2 == 0 else psX
                sc.activation(out=TMPV[:, gl % 2, :], in_=gbank[:, :],
                              func=AF.Silu)
    sc_ffn_end = s_sc.n

    @blk.vector
    def _(v):
        v = SerEng(v, s_ev)
        for h in range(2):
            for gl in range(48):
                grp, ttl, j = gl // 12, (gl // 6) % 2, gl % 6
                oc = grp * 6 + j
                v.wait_ge(s_sc.h, sc_a(h, gl))
                ubank = psW1 if gl % 2 == 0 else psY5
                v.tensor_tensor(out=MT[:, oc, 512 * ttl:512 * (ttl + 1)],
                                in0=TMPV[:, gl % 2, :], in1=ubank[:, :],
                                op=ALU.mult)
            for bl in range(16):
                dsub, ttl = bl // 2, bl % 2
                tt = 2 * h + ttl
                v.wait_ge(s_pe.h, pe_b(h, bl))
                nbi = 16 * h + bl
                if nbi >= 2:
                    v.wait_ge(s_st.h, stF + 16 * (nbi - 1))
                pw = psW0 if bl % 2 == 0 else psW1
                v.tensor_add(ASTG[:, bl % 2, :], pw[:, :],
                                      XN[:, dsub, 512 * tt:512 * (tt + 1)])
    ev_ffn_end = s_ev.n

    @blk.sync
    def _(q):
        for h in range(2):
            for bl in range(16):
                dsub, ttl = bl // 2, bl % 2
                tt = 2 * h + ttl
                q.wait_ge(s_ev.h, ev_b(h, bl))
                dma(s_st, q, q.dma_start, 
                    out=attnfm[128 * dsub:128 * (dsub + 1),
                               512 * tt:512 * (tt + 1)],
                    in_=ASTG[:, bl % 2, :])
    st_attn = s_st.n

    if stage <= 3:
        return finish_partial()

    free("ASTG", "WOF1", "WOF0", "WUF", "WGF", "MT", "SR3", "SQ2", "TMPV",
         "XSA")
    free("RBC")
    LGA = sbuf("LGA", [128, 128 * E], F32)
    TK = sbuf("TK", [128, 6 * 128], F32)
    SC1 = sbuf("SC1", [128, 32 * E], F32)
    SC2 = sbuf("SC2", [128, 32 * E], F32)
    WPLN = sbuf("WPLN", [128, 256], F32)
    EPL = sbuf("EPL", [128, 256], F32)
    WPL = sbuf("WPL", [128, E, 256], F32)
    EQF = sbuf("EQF", [128, E, 256], F32)
    KEEPA = sbuf("KEEPA", [128, E, 256], F32)
    JK = sbuf("JK", [128, 256], F32)
    JK2 = sbuf("JK2", [128, 256], F32)
    CNTS = sbuf("CNTS", [128, 32], F32)
    CT32 = sbuf("CT32", [32, 128], F32)
    M32 = sbuf("M32", [32, 128], F32)
    BCT = sbuf("BCT", [128, 32], F32)
    RST = sbuf("RST", [32, 16], F32)
    TOTS = sbuf("TOTS", [128, E], F32)
    CARR = sbuf("CARR", [128, E], F32)
    KANY = sbuf("KANY", [128, 256], F32)
    PWp = sbuf("PWp", [128, 256], F32)
    PTp = sbuf("PTp", [128, 256], F32)
    WSUM = sbuf("WSUM", [128, 128], F32)
    SRCA = sbuf("SRCA", [16, 2048], F32)
    CMP1 = sbuf("CMP1", [16, 320], F32)
    CMP2 = sbuf("CMP2", [16, 320], F32)
    L1A = sbuf("L1A", [16, 1280], F32)
    L2A = sbuf("L2A", [16, 640], F32)
    L1B = sbuf("L1B", [16, 1280], F32)
    L2B = sbuf("L2B", [16, 640], F32)
    NF1 = sbuf("NF1", [1, 16], U32)
    NFF = sbuf("NFF", [1, 16], F32)
    NFB16 = sbuf("NFB16", [16, 16], F32)
    MK = sbuf("MK", [16, 320], F32)
    iotaslt = sbuf("iotaslt", [16, 320], F32)

    lo, hi, mid, wst = (RST[:, i:i + 1] for i in range(4))
    done, tot, cgt, ge8 = (RST[:, i:i + 1] for i in range(4, 8))
    m8, c28, ngt, quo = (RST[:, i:i + 1] for i in range(8, 12))
    sc8a, sc8b, sc8c, sc8d = (RST[:, i:i + 1] for i in range(12, 16))

    def transp(v, dst, src):
        for b in range(4):
            v.transpose(dst[32 * b:32 * (b + 1), 0:32],
                        src[0:32, 32 * b:32 * (b + 1)])

    def transp_c(v, dst, src):
        for b in range(4):
            v.transpose(dst[0:32, 32 * b:32 * (b + 1)],
                        src[32 * b:32 * (b + 1), 0:32])

    def colsum(v, dstcol, op=ALU.add):
        transp_c(v, CT32, CNTS)
        v.tensor_reduce(out=dstcol, in_=CT32[:, :], axis=AX, op=op)

    def bcast8(v, dstT, srccol):
        v.tensor_copy(out=M32[:, :], in_=srccol.broadcast_to([32, 128]))
        transp(v, dstT, M32)

    @blk.sync
    def _(q):
        q.wait_ge(s_cc.h, cc_lg)
        q.wait_ge(s_pe.h, pe_ffn_end)
        q.wait_ge(s_ev.h, ev_ffn_end)
        dma(s_st, q, q.dma_start, out=LGA[:],
                             in_=lgall.rearrange("(p j) e -> p (j e)", p=128))
        dma(s_st, q, q.dma_start, out=iotaslt[:], in_=c_iotasl[:, 0:320])
    st_lga = s_st.n

    m1 = TK[:, 0:128]
    m2 = TK[:, 128:256]
    e1 = TK[:, 256:384]
    e2 = TK[:, 384:512]
    w1 = TK[:, 512:640]
    w2 = TK[:, 640:768]

    @blk.vector
    def _(v):
        v = SerEng(v, s_ev)
        v.wait_ge(s_st.h, st_lga)
        for qd in range(4):
            jsl = slice(32 * qd, 32 * (qd + 1))
            LV = LGA.rearrange("p (j e) -> p j e", e=E)[:, jsl, :]
            A3 = SC1.rearrange("p (j e) -> p j e", e=E)
            B3 = SC2.rearrange("p (j e) -> p j e", e=E)
            io8 = iota8t[:].unsqueeze(1).broadcast_to([128, 32, E])
            v.tensor_reduce(out=m1[:, jsl], in_=LV, axis=AX, op=ALU.max)
            v.tensor_tensor(out=A3, in0=LV,
                            in1=m1[:, jsl].unsqueeze(2).broadcast_to([128, 32, E]),
                            op=ALU.is_equal)
            v.tensor_tensor(out=B3, in0=A3, in1=io8, op=ALU.mult)
            v.scalar_tensor_tensor(out=B3, in0=A3, scalar=-9.0, in1=B3,
                                   op0=ALU.mult, op1=ALU.add)
            v.tensor_scalar(out=B3, in0=B3, scalar1=9.0, scalar2=None, op0=ALU.add)
            v.tensor_reduce(out=e1[:, jsl], in_=B3, axis=AX, op=ALU.min)
            v.tensor_tensor(out=A3, in0=io8,
                            in1=e1[:, jsl].unsqueeze(2).broadcast_to([128, 32, E]),
                            op=ALU.is_equal)
            v.scalar_tensor_tensor(out=A3, in0=A3, scalar=-1e30, in1=LV,
                                   op0=ALU.mult, op1=ALU.add)
            v.tensor_reduce(out=m2[:, jsl], in_=A3, axis=AX, op=ALU.max)
            v.tensor_tensor(out=B3, in0=A3,
                            in1=m2[:, jsl].unsqueeze(2).broadcast_to([128, 32, E]),
                            op=ALU.is_equal)
            v.tensor_tensor(out=A3, in0=B3, in1=io8, op=ALU.mult)
            v.scalar_tensor_tensor(out=A3, in0=B3, scalar=-9.0, in1=A3,
                                   op0=ALU.mult, op1=ALU.add)
            v.tensor_scalar(out=A3, in0=A3, scalar1=9.0, scalar2=None, op0=ALU.add)
            v.tensor_reduce(out=e2[:, jsl], in_=A3, axis=AX, op=ALU.min)
        v.tensor_sub(w1[:, :], m1[:, :], m2[:, :])
    ev_top2a = s_ev.n

    @blk.scalar
    def _(sc):
        sc = SerEng(sc, s_sc)
        sc.wait_ge(s_ev.h, ev_top2a)
        sc.activation(out=w1[:, :], in_=w1[:, :], func=AF.Sigmoid)
    sc_w1 = s_sc.n

    if stage <= 3.1:
        return finish_partial()

    @blk.vector
    def _(v):
        v = SerEng(v, s_ev)
        v.wait_ge(s_sc.h, sc_w1)
        v.memset(CNTS[:, :], 0.0)
        v.memset(RST[:, :], 0.0)
        v.tensor_scalar(out=w2[:, :], in0=w1[:, :], scalar1=-1.0, scalar2=1.0,
                        op0=ALU.mult, op1=ALU.add)
        v.tensor_copy(out=WPLN[:, 0:256:2], in_=w1[:, :])
        v.tensor_copy(out=WPLN[:, 1:256:2], in_=w2[:, :])
        v.tensor_copy(out=EPL[:, 0:256:2], in_=e1[:, :])
        v.tensor_copy(out=EPL[:, 1:256:2], in_=e2[:, :])
        v.tensor_scalar(out=JK[:, :], in0=WPLN[:, :], scalar1=2.0, scalar2=None,
                        op0=ALU.add)
        for e in range(E):
            v.tensor_scalar(out=JK2[:, :], in0=EPL[:, :], scalar1=float(e),
                            scalar2=None, op0=ALU.is_equal)
            v.tensor_tensor(out=WPL[:, e, :], in0=JK[:, :], in1=JK2[:, :],
                            op=ALU.mult)
            v.tensor_scalar(out=WPL[:, e, :], in0=WPL[:, e, :], scalar1=-2.0,
                            scalar2=None, op0=ALU.add)
            v.tensor_scalar(out=JK2[:, :], in0=WPL[:, e, :], scalar1=-0.5,
                            scalar2=None, op0=ALU.is_gt, op1=ALU.add,
                            accum_out=CNTS[:, e:e + 1])
        colsum(v, tot)
        v.memset(lo, -1.0)
        v.memset(hi, 1.0)
        v.memset(wst, -1.0)
        v.tensor_scalar(out=done, in0=tot, scalar1=float(CAP) + 0.5,
                        scalar2=None, op0=ALU.is_lt)

        for it in range(VAL_ITERS):
            v.tensor_add(sc8a, lo, hi)
            v.tensor_scalar(out=mid, in0=sc8a, scalar1=0.5, scalar2=None,
                            op0=ALU.mult)
            bcast8(v, BCT, mid)
            for e in range(E):
                v.tensor_scalar(out=JK[:, :], in0=WPL[:, e, :],
                                scalar1=BCT[:, e:e + 1], scalar2=None,
                                op0=ALU.is_gt, op1=ALU.add,
                                accum_out=CNTS[:, e:e + 1])
            colsum(v, cgt)
            v.tensor_scalar(out=ge8, in0=cgt, scalar1=float(CAP) - 0.5,
                            scalar2=None, op0=ALU.is_gt)
            v.tensor_sub(sc8a, mid, lo)
            v.tensor_tensor(out=sc8a, in0=sc8a, in1=ge8, op=ALU.mult)
            v.tensor_add(lo, lo, sc8a)
            v.tensor_scalar(out=sc8b, in0=ge8, scalar1=-1.0, scalar2=1.0,
                            op0=ALU.mult, op1=ALU.add)
            v.tensor_sub(sc8a, mid, hi)
            v.tensor_tensor(out=sc8a, in0=sc8a, in1=sc8b, op=ALU.mult)
            v.tensor_add(hi, hi, sc8a)

        for it in range(SNAP_ITERS):
            bcast8(v, BCT, lo)
            for e in range(E):
                v.tensor_scalar(out=JK[:, :], in0=WPL[:, e, :],
                                scalar1=BCT[:, e:e + 1], scalar2=None,
                                op0=ALU.is_gt)
                v.tensor_scalar(out=JK2[:, :], in0=WPL[:, e, :], scalar1=-2.0,
                                scalar2=None, op0=ALU.add)
                v.tensor_tensor(out=JK2[:, :], in0=JK2[:, :], in1=JK[:, :],
                                op=ALU.mult)
                v.tensor_scalar(out=JK2[:, :], in0=JK2[:, :], scalar1=2.0,
                                scalar2=None, op0=ALU.add, op1=ALU.min,
                                accum_out=CNTS[:, e:e + 1])
            colsum(v, m8, op=ALU.min)
            bcast8(v, BCT, m8)
            for e in range(E):
                v.tensor_scalar(out=JK[:, :], in0=WPL[:, e, :],
                                scalar1=BCT[:, e:e + 1], scalar2=None,
                                op0=ALU.is_gt, op1=ALU.add,
                                accum_out=CNTS[:, e:e + 1])
            colsum(v, c28)
            v.tensor_scalar(out=ge8, in0=c28, scalar1=float(CAP) - 0.5,
                            scalar2=None, op0=ALU.is_gt)
            v.tensor_scalar(out=sc8b, in0=done, scalar1=-1.0, scalar2=1.0,
                            op0=ALU.mult, op1=ALU.add)
            v.tensor_scalar(out=sc8c, in0=ge8, scalar1=-1.0, scalar2=1.0,
                            op0=ALU.mult, op1=ALU.add)
            v.tensor_tensor(out=sc8c, in0=sc8c, in1=sc8b, op=ALU.mult)
            v.tensor_sub(sc8a, m8, wst)
            v.tensor_tensor(out=sc8a, in0=sc8a, in1=sc8c, op=ALU.mult)
            v.tensor_add(wst, wst, sc8a)
            v.tensor_add(done, done, sc8c)
            v.tensor_scalar(out=sc8b, in0=done, scalar1=-1.0, scalar2=1.0,
                            op0=ALU.mult, op1=ALU.add)
            v.tensor_tensor(out=sc8c, in0=ge8, in1=sc8b, op=ALU.mult)
            v.tensor_sub(sc8a, m8, lo)
            v.tensor_tensor(out=sc8a, in0=sc8a, in1=sc8c, op=ALU.mult)
            v.tensor_add(lo, lo, sc8a)

        v.tensor_scalar(out=sc8b, in0=done, scalar1=-1.0, scalar2=1.0,
                        op0=ALU.mult, op1=ALU.add)
        v.tensor_sub(sc8a, lo, wst)
        v.tensor_tensor(out=sc8a, in0=sc8a, in1=sc8b, op=ALU.mult)
        v.tensor_add(wst, wst, sc8a)

        bcast8(v, BCT, wst)
        for e in range(E):
            v.tensor_scalar(out=JK[:, :], in0=WPL[:, e, :],
                            scalar1=BCT[:, e:e + 1], scalar2=None,
                            op0=ALU.is_gt, op1=ALU.add,
                            accum_out=CNTS[:, e:e + 1])
            v.tensor_scalar(out=EQF[:, e, :], in0=WPL[:, e, :],
                            scalar1=BCT[:, e:e + 1], scalar2=None,
                            op0=ALU.is_equal)
        colsum(v, ngt)
        v.tensor_scalar(out=quo, in0=ngt, scalar1=-1.0, scalar2=float(CAP),
                        op0=ALU.mult, op1=ALU.add)
        for e in range(E):
            v.tensor_tensor_scan(out=KEEPA[:, e, :], data0=EQF[:, e, :],
                                 data1=zerocol[:].broadcast_to([128, 256]),
                                 initial=0.0, op0=ALU.add, op1=ALU.add)
            v.tensor_copy(out=TOTS[:, e:e + 1], in_=KEEPA[:, e, 255:256])
        v.engine_nop()
    ev_rt_pre = s_ev.n

    if stage <= 3.2:
        return finish_partial()

    @blk.tensor
    def _(t):
        t.wait_ge(s_ev.h, ev_rt_pre)
        s_pe.inc(t.matmul(psBC[:, 0:E], triut[:], TOTS[:, :],
                          start=True, stop=True))
    pe_carr = s_pe.n

    @blk.vector
    def _(v):
        v = SerEng(v, s_ev)
        v.wait_ge(s_pe.h, pe_carr)
        v.tensor_copy(out=CARR[:, :], in_=psBC[:, 0:E])
        bcast8(v, BCT, quo)
        for e in range(E):
            v.tensor_sub(JK[:, :], KEEPA[:, e, :], EQF[:, e, :])
            v.tensor_scalar(out=JK[:, :], in0=JK[:, :],
                            scalar1=CARR[:, e:e + 1], scalar2=None, op0=ALU.add)
            v.tensor_scalar(out=JK[:, :], in0=JK[:, :],
                            scalar1=BCT[:, e:e + 1], scalar2=None, op0=ALU.is_lt)
            v.tensor_tensor(out=KEEPA[:, e, :], in0=JK[:, :], in1=EQF[:, e, :],
                            op=ALU.mult)
        bcast8(v, BCT, wst)
        for e in range(E):
            v.tensor_scalar(out=JK[:, :], in0=WPL[:, e, :],
                            scalar1=BCT[:, e:e + 1], scalar2=None, op0=ALU.is_gt)
            v.tensor_add(KEEPA[:, e, :], KEEPA[:, e, :], JK[:, :])
        v.tensor_reduce(out=KANY[:, :],
                        in_=KEEPA.rearrange("p e j -> p j e"), axis=AX, op=ALU.add)
        v.tensor_tensor(out=PWp[:, :], in0=WPLN[:, :], in1=KANY[:, :], op=ALU.mult)
        v.tensor_add(WSUM[:, :], PWp[:, 0:256:2], PWp[:, 1:256:2])
        v.tensor_scalar(out=WSUM[:, :], in0=WSUM[:, :], scalar1=1.1920929e-07,
                        scalar2=None, op0=ALU.max)
        v.reciprocal(out=WSUM[:, :], in_=WSUM[:, :])
        v.tensor_tensor(out=PWp[:, 0:256:2], in0=PWp[:, 0:256:2], in1=WSUM[:, :],
                        op=ALU.mult)
        v.tensor_tensor(out=PWp[:, 1:256:2], in0=PWp[:, 1:256:2], in1=WSUM[:, :],
                        op=ALU.mult)
        v.tensor_scalar(out=JK[:, :], in0=EPL[:, :], scalar1=myexpt[:, 0:1],
                        scalar2=None, op0=ALU.is_equal)
        v.tensor_tensor(out=JK[:, :], in0=JK[:, :], in1=KANY[:, :], op=ALU.mult)
        v.tensor_tensor_scan(out=JK2[:, :], data0=JK[:, :],
                             data1=zerocol[:].broadcast_to([128, 256]),
                             initial=0.0, op0=ALU.add, op1=ALU.add)
        v.tensor_copy(out=TOTS[:, 0:1], in_=JK2[:, 255:256])
        v.engine_nop()
    ev_rt2 = s_ev.n

    @blk.tensor
    def _(t):
        t.wait_ge(s_ev.h, ev_rt2)
        s_pe.inc(t.matmul(psBC[:, 0:1], triut[:], TOTS[:, 0:1],
                          start=True, stop=True))
    pe_carr2 = s_pe.n

    @blk.vector
    def _(v):
        v = SerEng(v, s_ev)
        v.wait_ge(s_pe.h, pe_carr2)
        v.tensor_copy(out=CARR[:, 0:1], in_=psBC[:, 0:1])
        v.tensor_sub(JK2[:, :], JK2[:, :], JK[:, :])
        v.tensor_scalar(out=JK2[:, :], in0=JK2[:, :], scalar1=CARR[:, 0:1],
                        scalar2=None, op0=ALU.add)
        v.tensor_scalar(out=JK2[:, :], in0=JK2[:, :], scalar1=float(CAP) - 0.5,
                        scalar2=None, op0=ALU.is_lt)
        v.tensor_tensor(out=JK[:, :], in0=JK[:, :], in1=JK2[:, :], op=ALU.mult)
        v.tensor_tensor(out=PTp[:, :], in0=tokp1t[:, :], in1=JK[:, :], op=ALU.mult)
        v.tensor_scalar(out=PTp[:, :], in0=PTp[:, :], scalar1=-1.0, scalar2=None,
                        op0=ALU.add)
        v.tensor_scalar(out=PWp[:, :], in0=PWp[:, :], scalar1=1.0, scalar2=None,
                        op0=ALU.add)
        v.tensor_tensor(out=PWp[:, :], in0=PWp[:, :], in1=JK[:, :], op=ALU.mult)
        v.tensor_scalar(out=PWp[:, :], in0=PWp[:, :], scalar1=-1.0, scalar2=None,
                        op0=ALU.add)
        v.engine_nop()
    ev_planes = s_ev.n

    @blk.sync
    def _(q):
        q.wait_ge(s_ev.h, ev_planes)
        if "d_kany" in dbg:
            dma(s_st, q, q.dma_start, out=dbg["d_kany"], in_=KANY[:, :])
        if "d_thr" in dbg:
            dma(s_st, q, q.dma_start, out=dbg["d_thr"], in_=RST[:, :])
    st_pln = s_st.n

    if stage <= 3.3:
        return finish_partial()

    @blk.vector
    def _(v):
        v = SerEng(v, s_ev)
        v.memset(L1A[:, :], -1.0)
        v.memset(L2A[:, :], -1.0)
        v.memset(CMP1[:, :], -1.0)
        v.memset(L1B[:, :], -1.0)
        v.memset(L2B[:, :], -1.0)
        v.memset(CMP2[:, :], -1.0)
    ev_cmpset = s_ev.n

    gp0 = s_gp.n
    pe_tr0 = s_pe.n
    ev_tr0 = s_ev.n

    @blk.tensor
    def _(t):
        t.wait_ge(s_ev.h, ev_planes)
        for hi in range(16):
            if hi >= 1:
                t.wait_ge(s_ev.h, ev_tr0 + hi)
            ps_sl = psBC[0:16, 128 * (hi % 2):128 * (hi % 2) + 128]
            s_pe.inc(t.matmul(ps_sl, PTp[:, 16 * hi:16 * hi + 16],
                              identf[:], start=True, stop=True))

    @blk.vector
    def _(v):
        v = SerEng(v, s_ev)
        for hi in range(16):
            v.wait_ge(s_pe.h, pe_tr0 + hi + 1)
            ps_sl = psBC[0:16, 128 * (hi % 2):128 * (hi % 2) + 128]
            v.tensor_copy(
                out=SRCA[0:16, :].rearrange("r (p hi2) -> r p hi2", hi2=16)[:, :, hi],
                in_=ps_sl)
    assert s_ev.n == ev_tr0 + 16

    if stage <= 3.31:
        return finish_partial()

    @blk.gpsimd
    def _(g):
        g.wait_ge(s_ev.h, ev_tr0 + 16)
        g.wait_ge(s_ev.h, ev_cmpset)
        for c in range(4):
            g.load_library(libcfg.sparse_gather)
            s_gp.inc(g.sparse_gather(out=L1A[:, 320 * c:320 * (c + 1)],
                                     in_=SRCA[:, 512 * c:512 * (c + 1)],
                                     num_found=NF1[:, c:c + 1]))

    if stage <= 3.32:
        return finish_partial()

    def nf_group(gpw, k0, k1, tiles):
        @blk.vector
        def _(v):
            v = SerEng(v, s_ev)
            v.wait_ge(s_gp.h, gpw)
            v.tensor_copy(out=NFF[:, k0:k1], in_=NF1[:, k0:k1])
        ev_c = s_ev.n

        @blk.tensor
        def _(t):
            t.wait_ge(s_ev.h, ev_c)
            s_pe.inc(t.matmul(psBC[0:16, 496:496 + (k1 - k0)],
                              ones1t[0:1, 0:16], NFF[:, k0:k1],
                              start=True, stop=True))
        pe_c = s_pe.n

        @blk.vector
        def _(v):
            v = SerEng(v, s_ev)
            v.wait_ge(s_pe.h, pe_c)
            v.tensor_copy(out=NFB16[:, k0:k1], in_=psBC[0:16, 496:496 + (k1 - k0)])
            for j, tile in enumerate(tiles):
                v.tensor_scalar(out=MK[:, :], in0=iotaslt[:, :],
                                scalar1=NFB16[:, k0 + j:k0 + j + 1],
                                scalar2=None, op0=ALU.is_lt)
                v.scalar_tensor_tensor(out=tile, in0=tile, scalar=1.0,
                                       in1=MK[:, :], op0=ALU.add, op1=ALU.mult)
                v.tensor_scalar(out=tile, in0=tile, scalar1=-1.0, scalar2=None,
                                op0=ALU.add)
        return s_ev.n

    ev_sanA1 = nf_group(gp0 + 4, 0, 4,
                        [L1A[:, 320 * c:320 * (c + 1)] for c in range(4)])

    if "d_l1a" in dbg:
        @blk.sync
        def _(q):
            q.wait_ge(s_ev.h, ev_sanA1)
            dma(s_st, q, q.dma_start, out=dbg["d_l1a"][0:16, :], in_=L1A[:, :])
            dma(s_st, q, q.dma_start, out=dbg["d_l1a"][16:17, 0:16], in_=NFF[:, :])

    if stage <= 3.35:
        return finish_partial()

    pe_tr1 = s_pe.n
    ev_tr1 = s_ev.n

    @blk.tensor
    def _(t):
        t.wait_ge(s_gp.h, gp0 + 4)
        t.wait_ge(s_ev.h, ev_sanA1)
        for hi in range(16):
            if hi >= 1:
                t.wait_ge(s_ev.h, ev_tr1 + hi)
            ps_sl = psBC[0:16, 128 * (hi % 2):128 * (hi % 2) + 128]
            s_pe.inc(t.matmul(ps_sl, PWp[:, 16 * hi:16 * hi + 16],
                              identf[:], start=True, stop=True))

    @blk.vector
    def _(v):
        v = SerEng(v, s_ev)
        v.wait_ge(s_gp.h, gp0 + 4)
        for hi in range(16):
            v.wait_ge(s_pe.h, pe_tr1 + hi + 1)
            ps_sl = psBC[0:16, 128 * (hi % 2):128 * (hi % 2) + 128]
            v.tensor_copy(
                out=SRCA[0:16, :].rearrange("r (p hi2) -> r p hi2", hi2=16)[:, :, hi],
                in_=ps_sl)
    ev_tr1_end = s_ev.n

    @blk.gpsimd
    def _(g):
        g.wait_ge(s_ev.h, ev_sanA1)
        g.load_library(libcfg.sparse_gather)
        s_gp.inc(g.sparse_gather(out=L2A[:, 0:320], in_=L1A[:, 0:640],
                                 num_found=NF1[:, 4:5]))
        g.load_library(libcfg.sparse_gather)
        s_gp.inc(g.sparse_gather(out=L2A[:, 320:640], in_=L1A[:, 640:1280],
                                 num_found=NF1[:, 5:6]))

    ev_sanA2 = nf_group(gp0 + 6, 4, 6, [L2A[:, 0:320], L2A[:, 320:640]])

    @blk.gpsimd
    def _(g):
        g.wait_ge(s_ev.h, ev_sanA2)
        g.load_library(libcfg.sparse_gather)
        s_gp.inc(g.sparse_gather(out=CMP1[:, :], in_=L2A[:, :],
                                 num_found=NF1[:, 6:7]))

    ev_sanA3 = nf_group(gp0 + 7, 6, 7, [CMP1[:, :]])

    if stage <= 3.4:
        return finish_partial()

    @blk.gpsimd
    def _(g):
        g.wait_ge(s_ev.h, ev_tr1_end)
        for c in range(4):
            g.load_library(libcfg.sparse_gather)
            s_gp.inc(g.sparse_gather(out=L1B[:, 320 * c:320 * (c + 1)],
                                     in_=SRCA[:, 512 * c:512 * (c + 1)],
                                     num_found=NF1[:, 8 + c:9 + c]))

    ev_sanB1 = nf_group(gp0 + 11, 8, 12,
                        [L1B[:, 320 * c:320 * (c + 1)] for c in range(4)])

    @blk.gpsimd
    def _(g):
        g.wait_ge(s_ev.h, ev_sanB1)
        g.load_library(libcfg.sparse_gather)
        s_gp.inc(g.sparse_gather(out=L2B[:, 0:320], in_=L1B[:, 0:640],
                                 num_found=NF1[:, 12:13]))
        g.load_library(libcfg.sparse_gather)
        s_gp.inc(g.sparse_gather(out=L2B[:, 320:640], in_=L1B[:, 640:1280],
                                 num_found=NF1[:, 13:14]))

    ev_sanB2 = nf_group(gp0 + 13, 12, 14, [L2B[:, 0:320], L2B[:, 320:640]])

    @blk.gpsimd
    def _(g):
        g.wait_ge(s_ev.h, ev_sanB2)
        g.load_library(libcfg.sparse_gather)
        s_gp.inc(g.sparse_gather(out=CMP2[:, :], in_=L2B[:, :],
                                 num_found=NF1[:, 14:15]))
        g.load_library(libcfg.mlp)

    ev_sanB3 = nf_group(gp0 + 14, 14, 15, [CMP2[:, :]])

    @blk.vector
    def _(v):
        v = SerEng(v, s_ev)
        v.wait_ge(s_gp.h, gp0 + 14)
        v.memset(IG[:, :], 0)
        v.memset(ISC[:, :], 0)
        v.tensor_scalar(out=CMP1[:, :], in0=CMP1[:, :], scalar1=1.0,
                        scalar2=None, op0=ALU.add)
        v.tensor_scalar(out=CMP2[:, :], in0=CMP2[:, :], scalar1=0.0,
                        scalar2=None, op0=ALU.max)
        v.tensor_scalar(out=SRCA[:, 0:320], in0=CMP1[:, :], scalar1=-1.0,
                        scalar2=0.0, op0=ALU.add, op1=ALU.max)
        v.tensor_copy(out=IG[0:16, :], in_=SRCA[:, 0:320])
        v.tensor_copy(out=ISC[0:16, :], in_=CMP1[:, 0:320])
    ev_lists = s_ev.n

    if stage <= 3.5:
        return finish_partial()

    @blk.sync
    def _(q):
        q.wait_ge(s_ev.h, ev_lists)
        dma(s_st, q, q.dma_start, out=IG[16:32, :], in_=IG[0:16, :])
        dma(s_st, q, q.dma_start, out=ISC[16:32, :], in_=ISC[0:16, :])
        dma(s_st, q, q.dma_start, out=bncw.rearrange("o (r jj) -> r o jj", r=16),
            in_=CMP2[:, 0:320].unsqueeze(1))
        if "d_lists" in dbg:
            dma(s_st, q, q.dma_start, out=dbg["d_lists"][:, 0:320], in_=CMP1[:, :])
            dma(s_st, q, q.dma_start, out=dbg["d_lists"][:, 512:832], in_=CMP2[:, :])
    st_idx0 = s_st.n

    if stage <= 3.6:
        return finish_partial()

    @blk.sync
    def _(q):
        q.wait_ge(s_st.h, st_idx0)
        for hi in range(8):
            dma(s_st, q, q.dma_start, out=WTSL[16 * hi:16 * (hi + 1), :],
                in_=bncw.rearrange("o (r c2 hi2) -> r (o c2 hi2)", r=16, hi2=8)
                [:, hi::8][:, 0:40],
                allow_slow_non_contiguous=True)
        if "d_wtsl" in dbg:
            dma(s_st, q, q.dma_start, out=dbg["d_wtsl"], in_=WTSL[:, :])
    st_wtsl = s_st.n

    if stage <= 4:
        return finish_partial()

    free("iotaslt", "MK", "NFB16", "NFF", "NF1", "L2B", "L1B", "L2A", "L1A",
         "CMP2", "CMP1", "SRCA", "WSUM",
         "PTp", "PWp", "KANY", "CARR", "TOTS", "RST", "BCT", "M32", "CT32",
         "CNTS", "JK2", "JK", "KEEPA", "EQF", "WPL", "EPL", "WPLN", "SC2",
         "SC1", "TK", "LGA")
    free("XN")
    M2 = sbuf("M2", [128, FFC, 1024], BF16)
    GIN = sbuf("GIN", [128, 2, DC, 512], BF16)
    WPR = sbuf("WPR", [128, 2, DC, 512], BF16)
    WGA = sbuf("WGA", [128, 2, DC, 512], BF16)
    WPO = sbuf("WPO", [128, FFC, D], BF16)
    EOU = sbuf("EOU", [128, 4, D], F32)
    TMPX = sbuf("TMPX", [128, 2, 512], F32)

    CH5 = 5
    if stage <= 4.3:
        CH5 = 1
    elif stage <= 4.6:
        CH5 = 3

    gp5 = s_gp.n
    gd5 = s_gd.n
    gw5 = s_gw.n
    pe5 = s_pe.n
    sc5 = s_sc.n
    ev5 = s_ev.n

    def pe_g(c, gl):
        return pe5 + 80 * c + gl + 1

    def pe_p(c, bl):
        return pe5 + 80 * c + 64 + bl + 1

    def sc_g(c, gl):
        return sc5 + 64 * c + gl + 1

    def ev_g(c, gl):
        return ev5 + 80 * c + gl + 1

    def ev_p(c, bl):
        return ev5 + 80 * c + 64 + bl + 1

    def gw_grp(c, grp):
        return gw5 + (17 * c + 2 * (grp + 1)) * 16

    def gw_wpo(c):
        return gw5 + (17 * c + 17) * 16

    def gp_gath(c):
        return gd5 + 64 * c + 32

    def gp_scat(c, half):
        if c < 0:
            return gd5
        return gd5 + 64 * c + 32 + 16 * (half + 1)

    if stage <= 4.25:
        @blk.gpsimd
        def _(g):
            g.wait_ge(s_cc.h, cc_xn)
            g.wait_ge(s_st.h, st_wtsl)
            g.load_library(libcfg.mlp)
            if stage > 4.15:
                dma(s_gd, g, g.dma_gather,
                    out_ap=GIN[:, 0, :, :], in_ap=xnall[:],
                    idxs_ap=IG[:, 0:32], num_idxs=512,
                    num_idxs_reg=512, elem_size=D, transpose=True)
        return finish_partial()

    @blk.gpsimd
    def _(g):
        g.wait_ge(s_cc.h, cc_xn)
        g.wait_ge(s_st.h, st_idx0)
        for c in range(CH5):
            if c >= 1:
                g.wait_ge(s_pe.h, pe5 + 80 * (c - 1) + 64)
            for gh in range(2):
                g.load_library(libcfg.mlp)
                dma(s_gd, g, g.dma_gather,
                    out_ap=GIN[:, gh, :, :], in_ap=xnall[:],
                    idxs_ap=IG[:, 64 * c + 32 * gh:64 * c + 32 * (gh + 1)],
                    num_idxs=512, num_idxs_reg=512, elem_size=D, transpose=True)
            for grp in range(8):
                if grp >= 2:
                    g.wait_ge(s_pe.h, pe5 + 80 * c + 8 * (grp - 1))
                elif c > 0:
                    g.wait_ge(s_pe.h, pe5 + 80 * (c - 1) + 8 * (6 + grp) + 8)
                fsl = slice(512 * grp, 512 * (grp + 1))
                dma(s_gw, g, g.dma_start, 
                    out=WPR[:, grp % 2, :, :],
                    in_=pre_t.rearrange("(k p) f -> p k f", p=128)[:, :, fsl])
                dma(s_gw, g, g.dma_start, 
                    out=WGA[:, grp % 2, :, :],
                    in_=gatew_t.rearrange("(k p) f -> p k f", p=128)[:, :, fsl])
            if c >= 1:
                g.wait_ge(s_pe.h, pe5 + 80 * (c - 1) + 80)
            dma(s_gw, g, g.dma_start, 
                out=WPO[:],
                in_=post_t.rearrange("(f p) d -> p f d", p=128))
            for half in range(2):
                g.wait_ge(s_ev.h, ev_p(c, 8 * (half + 1) - 1))
                g.load_library(libcfg.mlp)
                dma(s_gd, g, g.dma_scatter_add, 
                    out_ap=moeacc[:], in_ap=EOU[:],
                    idxs_ap=ISC[:, 64 * c + 32 * half:64 * c + 32 * (half + 1)],
                    num_idxs=512, num_idxs_reg=512, elem_size=D)

    @blk.tensor
    def _(t):
        for c in range(CH5):
            t.wait_ge(s_gd.h, gp_gath(c))
            for gl in range(64):
                ffc, tt = gl // 2, gl % 2
                grp, j = ffc // 4, ffc % 4
                t.wait_ge(s_gw.h, gw_grp(c, grp))
                gg = 64 * c + gl
                if gg >= 2:
                    t.wait_ge(s_sc.h, sc_g(c, gl - 2) if gl >= 2
                              else sc_g(c - 1, 62 + gl))
                    t.wait_ge(s_ev.h, ev_g(c, gl - 2) if gl >= 2
                              else ev_p(c - 1, 15))
                pre_b = psW0 if gl % 2 == 0 else psX
                gat_b = psW1 if gl % 2 == 0 else psY5
                sl = slice(512 * tt, 512 * (tt + 1))
                for k in range(DC):
                    t.matmul(pre_b[:, :],
                             WPR[:, grp % 2, k, 128 * j:128 * (j + 1)],
                             GIN[:, tt, k, :], start=(k == 0),
                             stop=(k == DC - 1))
                for k in range(DC):
                    mm = t.matmul(gat_b[:, :],
                                  WGA[:, grp % 2, k, 128 * j:128 * (j + 1)],
                                  GIN[:, tt, k, :], start=(k == 0),
                                  stop=(k == DC - 1))
                s_pe.inc(mm)
            t.wait_ge(s_gw.h, gw_wpo(c))
            for bl in range(16):
                half, scp, dtile = bl // 8, (bl // 2) % 4, bl % 2
                scg = half * 4 + scp
                bb = 16 * c + bl
                if bl >= 2:
                    t.wait_ge(s_ev.h, ev_p(c, bl - 2))
                else:
                    t.wait_ge(s_ev.h, ev_g(c, 63))
                pw = psW0 if bl % 2 == 0 else psW1
                for ffc in range(FFC):
                    mm = t.matmul(pw[:, :], M2[:, ffc, 128 * scg:128 * (scg + 1)],
                                  WPO[:, ffc, 512 * dtile:512 * (dtile + 1)],
                                  start=(ffc == 0), stop=(ffc == FFC - 1))
                s_pe.inc(mm)
    pe5_end = s_pe.n

    @blk.scalar
    def _(sc):
        sc = SerEng(sc, s_sc)
        for c in range(CH5):
            for gl in range(64):
                sc.wait_ge(s_pe.h, pe_g(c, gl))
                gg = 64 * c + gl
                if gg >= 2:
                    sc.wait_ge(s_ev.h, ev_g(c, gl - 2) if gl >= 2
                               else ev_g(c - 1, 62 + gl))
                gat_b = psW1 if gl % 2 == 0 else psY5
                sc.activation(out=TMPX[:, gl % 2, :], in_=gat_b[:, :],
                                       func=AF.Silu)

    @blk.vector
    def _(v):
        v = SerEng(v, s_ev)
        for c in range(CH5):
            for gl in range(64):
                ffc, tt = gl // 2, gl % 2
                v.wait_ge(s_sc.h, sc_g(c, gl))
                pre_b = psW0 if gl % 2 == 0 else psX
                sl = slice(512 * tt, 512 * (tt + 1))
                v.tensor_tensor(out=M2[:, ffc, sl], in0=pre_b[:, :],
                                         in1=TMPX[:, gl % 2, :], op=ALU.mult)
            for bl in range(16):
                half, scp, dtile = bl // 8, (bl // 2) % 4, bl % 2
                if c == 0 and bl == 0:
                    v.wait_ge(s_st.h, st_wtsl)
                v.wait_ge(s_pe.h, pe_p(c, bl))
                if bl < 8:
                    v.wait_ge(s_gd.h, gp_scat(c - 1, 1))
                else:
                    v.wait_ge(s_gd.h, gp_scat(c, 0))
                pw = psW0 if bl % 2 == 0 else psW1
                col = 8 * c + 4 * half + scp
                v.tensor_scalar(
                    out=EOU[:, scp, 512 * dtile:512 * (dtile + 1)],
                    in0=pw[:, :], scalar1=WTSL[:, col:col + 1], scalar2=None,
                    op0=ALU.mult)
    ev5_end = s_ev.n

    if stage <= 4.9:
        return finish_partial()

    @blk.gpsimd
    def _(g):
        g.wait_ge(s_gd.h, gd5 + 64 * CH5)
        s_cc.inc(g.collective_compute(
            "ReduceScatter", ALU.add, replica_groups=RG,
            ins=[moeacc[1:T + 1, :]], outs=[moers[:]]))
    cc_rs = s_cc.n

    if stage <= 5:
        return finish_partial()

    free("TMPX", "EOU", "WPO", "WGA", "WPR", "GIN", "M2")
    XF = sbuf("XF", [128, 2, D], F32)
    MF = sbuf("MF", [128, 2, D], F32)
    AFB = sbuf("AFB", [128, 2, D], BF16)
    OF = sbuf("OF", [128, 16, D], F32)

    st6 = s_st.n
    ev6 = s_ev.n
    n_per_tb = 3 + (1 if "d_moe" in dbg else 0)

    @blk.sync
    def _(q):
        q.wait_ge(s_cc.h, cc_rs)
        for tb in range(16):
            r = tb % 2
            if tb >= 2:
                q.wait_ge(s_ev.h, ev6 + 2 * (tb - 1))
            sl = slice(128 * tb, 128 * (tb + 1))
            dma(s_st, q, q.dma_start, out=XF[:, r, :], in_=x_tm[sl, :])
            dma(s_st, q, q.dma_start, out=MF[:, r, :], in_=moers[sl, :])
            dma(s_st, q, q.dma_start, out=AFB[:, r, :], in_=attnfm[:, sl],
                                 transpose=True)
            if "d_moe" in dbg:
                dma(s_st, q, q.dma_start, out=dbg["d_moe"][sl, :], in_=moers[sl, :])

    @blk.vector
    def _(v):
        v = SerEng(v, s_ev)
        for tb in range(16):
            r = tb % 2
            v.wait_ge(s_st.h, st6 + 16 * n_per_tb * (tb + 1))
            v.tensor_add(OF[:, tb, :], XF[:, r, :], AFB[:, r, :])
            v.tensor_add(OF[:, tb, :], OF[:, tb, :], MF[:, r, :])
    ev6_end = s_ev.n

    @blk.sync
    def _(q):
        for tb in range(16):
            r = tb % 2
            q.wait_ge(s_ev.h, ev6 + 2 * (tb + 1))
            dma(s_st, q, q.dma_start, out=out_tm[128 * tb:128 * (tb + 1), :],
                                 in_=OF[:, tb, :])
        q.wait_ge(s_st.h, s_st.n)

    blk_g.__exit__(None, None, None)
    for n, g in reversed(guards):
        g.__exit__(None, None, None)
    return nc


def make_consts():
    ident = np.eye(128, dtype=ml_dtypes.bfloat16)
    ones = np.ones((128, 1), np.float32)
    ones1 = np.ones((1, 128), np.float32)
    iota8 = np.broadcast_to(np.arange(8, dtype=np.float32), (128, 8)).copy()
    f = np.arange(128)[:, None] * 256 + np.arange(256)[None, :]
    tokp1 = (f // 2 + 1).astype(np.float32)
    iotasl = (np.arange(2048)[None, :] * 16
              + np.arange(16)[:, None]).astype(np.float32)
    triu = (np.arange(128)[:, None] < np.arange(128)[None, :]).astype(np.float32)
    return ident, ones, ones1, iota8, tokp1, iotasl, triu


_BUILD_CACHE = {}


def kernel(L0_attn_x, L0_attn_rms_mix_w, L0_attn_rms_ffn_w, L0_attn_Wg,
           L0_attn_bg, L0_attn_Wv, L0_attn_bv, L0_attn_Wd, L0_attn_bd,
           L0_attn_W_gate, L0_attn_W_up, L0_attn_W_out, L0_ffn_router_weight,
           L0_ffn_ff_pre_act_weight, L0_ffn_gate_weight,
           L0_ffn_ff_post_act_weight, _debug=(), _return_debug=False):
    key = tuple(sorted(_debug))
    if key not in _BUILD_CACHE:
        nc = build(debug=_debug)
        lower_extended_insts(nc)
        _BUILD_CACHE[key] = nc
    nc = _BUILD_CACHE[key]

    x = np.ascontiguousarray(np.asarray(L0_attn_x, dtype=np.float32)
                             .reshape(T, D))
    ident, ones, ones1, iota8, tokp1, iotasl, triu = make_consts()

    def tr(w):
        return np.ascontiguousarray(np.asarray(w, np.float32).T)

    wgT, wvT, wdT = tr(L0_attn_Wg), tr(L0_attn_Wv), tr(L0_attn_Wd)
    wgateT, wupT = tr(L0_attn_W_gate), tr(L0_attn_W_up)
    woutT = tr(L0_attn_W_out)
    routerT = tr(L0_ffn_router_weight)
    b3 = np.stack([np.asarray(L0_attn_bg, np.float32),
                   np.asarray(L0_attn_bv, np.float32),
                   np.asarray(L0_attn_bd, np.float32)], 0)
    biases = np.ascontiguousarray(
        b3.reshape(3, DC, 128).transpose(2, 0, 1).reshape(128, 3 * DC))
    rw = np.stack([np.asarray(L0_attn_rms_mix_w, np.float32),
                   np.asarray(L0_attn_rms_ffn_w, np.float32)], 0)
    rmsw = np.ascontiguousarray(
        rw.reshape(2, DC, 128).transpose(2, 0, 1).reshape(128, 2 * DC))

    in_maps = []
    for c in range(8):
        xs = x[TL * c:TL * (c + 1)]
        carrysel = np.zeros((128, 8), np.float32)
        if c % 2 == 1:
            carrysel[:, c - 1] = 1.0
        in_maps.append({
            "x_fm": np.ascontiguousarray(xs.T),
            "x_tm": np.ascontiguousarray(xs),
            "wg_t": wgT, "wv_t": wvT, "wd_t": wdT,
            "biases": biases, "rmsw": rmsw,
            "wgate_t": wgateT, "wup_t": wupT, "wout_t": woutT,
            "router_t": routerT,
            "pre_t": tr(np.asarray(L0_ffn_ff_pre_act_weight)[c]),
            "gatew_t": tr(np.asarray(L0_ffn_gate_weight)[c]),
            "post_t": tr(np.asarray(L0_ffn_ff_post_act_weight)[c]),
            "c_ident": ident, "c_identf": np.eye(128, dtype=np.float32), "c_ones": ones, "c_ones1": ones1,
            "c_iota8": iota8, "c_tokp1": tokp1, "c_iotasl": iotasl,
            "c_myexp": np.full((128, 1), float(c), np.float32),
            "c_carrysel": carrysel, "c_triu": triu,
        })

    res = run_bass_kernel_spmd(nc, in_maps, core_ids=list(range(8)))
    out = np.concatenate([res.results[c]["out_tm"] for c in range(8)], axis=0)
    out = out.reshape(4, 4096, 1024).astype(np.float32)
    if _return_debug:
        return out, res
    return out
